# version 22
# baseline (speedup 1.0000x reference)
import math
from contextlib import ExitStack

import numpy as np
import ml_dtypes

import concourse.bass as bass
import concourse.mybir as mybir
from concourse.bass_utils import run_bass_kernel_spmd

F32 = mybir.dt.float32
BF16 = mybir.dt.bfloat16
AF = mybir.ActivationFunctionType
ALU = mybir.AluOpType
AX = mybir.AxisListType
bf16 = ml_dtypes.bfloat16

N_CORES = 8
T = 4096
D = 1024
NT = T // 128
EPS_LN = 1e-5

ENGINES = ("tensor", "vector", "scalar", "gpsimd", "sync")


class Op:
    __slots__ = ("eng", "fn", "gid", "exec_idx", "dma_idx", "dma", "deps", "waits")


class Sched:

    def __init__(self, nc):
        self.nc = nc
        self.ops = []
        self.per_eng = {e: [] for e in ENGINES}
        self.n_exec = {e: 0 for e in ENGINES}
        self.n_dma = {e: 0 for e in ENGINES}
        self.last = {}
        self.last_dma_slot = {e: {} for e in ENGINES}
        self.last_exec_op = {e: None for e in ENGINES}
        self.pending_barrier = None
        self.barrier_armed = set()

    def barrier(self):
        deps = set()
        for e in ENGINES:
            if self.last_exec_op[e] is not None:
                deps.add(self.last_exec_op[e])
            deps.update(self.last_dma_slot[e].values())
        self.pending_barrier = deps
        self.barrier_armed = set(ENGINES)

    def op(self, eng, fn, reads=(), writes=(), dma=False):
        o = Op()
        o.eng = eng
        o.fn = fn
        o.dma = dma
        o.gid = len(self.ops)
        deps = set()
        if self.pending_barrier is not None and eng in self.barrier_armed:
            deps |= self.pending_barrier
            self.barrier_armed.discard(eng)
            if not self.barrier_armed:
                self.pending_barrier = None
        for k in reads:
            e = self.last.get(k)
            if e and e["w"] is not None:
                deps.add(e["w"])
        for k in writes:
            e = self.last.get(k)
            if e:
                if e["w"] is not None:
                    deps.add(e["w"])
                deps.update(e["r"])
        o.deps = deps
        self.ops.append(o)
        self.per_eng[eng].append(o)
        if dma:
            o.dma_idx = self.n_dma[eng]
            self.n_dma[eng] += 1
            o.exec_idx = None
            self.last_dma_slot[eng][o.dma_idx % self.R_SLOTS] = o
        else:
            o.exec_idx = self.n_exec[eng]
            self.n_exec[eng] += 1
            o.dma_idx = None
            self.last_exec_op[eng] = o
        for k in reads:
            self.last.setdefault(k, {"w": None, "r": []})["r"].append(o)
        for k in writes:
            self.last[k] = {"w": o, "r": []}
        return o

    R_SLOTS = 8

    def finish(self, block, stack):
        R = self.R_SLOTS
        exec_sem = {}
        dma_sem = {}
        for e in ENGINES:
            if self.n_exec[e]:
                exec_sem[e] = stack.enter_context(self.nc.semaphore("sx_" + e))
            if self.n_dma[e]:
                dma_sem[e] = [stack.enter_context(self.nc.semaphore(f"sd_{e}{i}"))
                              for i in range(R)]
        observed = {e: {} for e in ENGINES}
        for o in self.ops:
            o.waits = []
            obs = observed[o.eng]
            if o.dma and o.dma_idx >= R:
                j = o.dma_idx
                key = ("d", o.eng, j % R)
                need = 16 * (j // R)
                if obs.get(key, 0) < need:
                    obs[key] = need
                    o.waits.append((dma_sem[o.eng][j % R], need))
            for d in sorted(o.deps, key=lambda d: d.gid):
                if d.dma:
                    j = d.dma_idx
                    key = ("d", d.eng, j % R)
                    need = 16 * (j // R + 1)
                    sem = dma_sem[d.eng][j % R]
                else:
                    if d.eng == o.eng == "tensor":
                        continue
                    key = ("x", d.eng)
                    need = d.exec_idx + 1
                    sem = exec_sem[d.eng]
                if obs.get(key, 0) >= need:
                    continue
                obs[key] = need
                o.waits.append((sem, need))

        stack.enter_context(self.nc.allow_low_precision(
            "bf16 operands with fp32 accumulation per problem tolerance"))

        def make_body(name):
            def body(eng):
                for o in self.per_eng[name]:
                    for sem, val in o.waits:
                        eng.wait_ge(sem, val)
                    inst = o.fn(eng)
                    if o.dma:
                        inst.then_inc(dma_sem[name][o.dma_idx % R], 16)
                    else:
                        inst.then_inc(exec_sem[name], 1)
                if name == "gpsimd" and self.n_dma["gpsimd"]:
                    n = self.n_dma["gpsimd"]
                    for i in range(min(R, n)):
                        eng.wait_ge(dma_sem["gpsimd"][i], 16 * ((n - 1 - i) // R + 1))
            return body

        for name in ENGINES:
            if self.per_eng[name]:
                getattr(block, name)(make_body(name))


class Ctx:

    ARENA_BYTES = 194 * 1024

    def __init__(self, name, fused=False):
        self.nc = bass.Bass("TRN2", target_bir_lowering=False, debug=False)
        self.sch = Sched(self.nc)
        self.stack = ExitStack()
        self.inputs = {}
        self.name = name
        self.fused = fused
        self.inv = 0
        self.ext_cache = {}
        self.scratch_cache = {}
        self.const_cache = {}
        self.arena_base = None
        self.arena_ptr = 0
        self.arena_peak = 0
        self.pool_f = []
        self.pool_b = []
        self.pool_f_used = 0
        self.pool_b_used = 0

    def init_fused(self):
        r = self.nc.bump_sbuf(self.ARENA_BYTES, align_bytes=64)
        assert r is not None, "arena reservation failed"
        self.arena_base = r[0]
        self.pool_f = [self.stack.enter_context(
            self.nc.psum_tensor(f"poolf{i}", [128, 512], F32)) for i in range(6)]
        self.pool_b = [self.stack.enter_context(
            self.nc.psum_tensor(f"poolb{i}", [128, 1024], BF16)) for i in range(2)]

    def new_sublayer(self):
        self.sch.barrier()
        self.arena_peak = max(self.arena_peak, self.arena_ptr)
        self.arena_ptr = 0
        self.pool_f_used = 0
        self.pool_b_used = 0
        self.inv += 1

    def ext_in(self, name, arr):
        if name in self.ext_cache:
            return self.ext_cache[name]
        dt = {np.dtype(np.float32): F32, np.dtype(bf16): BF16}[arr.dtype]
        ap = self.nc.dram_tensor(name, list(arr.shape), dt, kind="ExternalInput").ap()
        self.inputs[name] = arr
        self.ext_cache[name] = ap
        return ap

    def ext_in_percore(self, name, shape, dtyp=F32):
        return self.nc.dram_tensor(name, list(shape), dtyp, kind="ExternalInput").ap()

    def ext_out(self, name, shape, dtyp=F32):
        return self.nc.dram_tensor(name, list(shape), dtyp, kind="ExternalOutput").ap()

    def scratch(self, name, shape, dtyp):
        if name in self.scratch_cache:
            return self.scratch_cache[name]
        ap = self.nc.dram_tensor(name, list(shape), dtyp, kind="Internal").ap()
        self.scratch_cache[name] = ap
        return ap

    def sb(self, name, shape, dtyp, n=1, persist=False):
        if n != 1:
            return [self.sb(f"{name}{i}", shape, dtyp, persist=persist)
                    for i in range(n)]
        if not self.fused or persist:
            return self.stack.enter_context(self.nc.sbuf_tensor(name, shape, dtyp))
        nbytes = shape[1] * mybir.dt.size(dtyp)
        nbytes = (nbytes + 63) & ~63
        assert self.arena_ptr + nbytes <= self.ARENA_BYTES, \
            f"arena overflow at {name}: {self.arena_ptr + nbytes}"
        h = self.nc.alloc_sbuf_tensor_at(f"{name}_v{self.inv}", shape, dtyp,
                                         offset=self.arena_base + self.arena_ptr)
        self.arena_ptr += nbytes
        return h

    def ps(self, name, shape, dtyp=F32, n=1):
        if n != 1:
            return [self.ps(f"{name}{i}", shape, dtyp) for i in range(n)]
        if not self.fused:
            return self.stack.enter_context(self.nc.psum_tensor(name, shape, dtyp))
        if dtyp == F32:
            assert list(shape) == [128, 512], f"psum pool shape {shape}"
            t = self.pool_f[self.pool_f_used]
            self.pool_f_used += 1
        else:
            assert list(shape) == [128, 1024], f"psum pool shape {shape}"
            t = self.pool_b[self.pool_b_used]
            self.pool_b_used += 1
        return t

    def dma(self, queue, out, in_, reads=(), writes=()):
        self.sch.op(queue, lambda e, o=out, i=in_: e.dma_start(out=o, in_=i),
                    reads=reads, writes=writes, dma=True)

    def mm(self, out, lhsT, rhs, start, stop, reads=(), writes=()):
        self.sch.op("tensor",
                    lambda e, o=out, l=lhsT, r=rhs, s0=start, s1=stop:
                        e.matmul(o, l, r, start=s0, stop=s1),
                    reads=reads, writes=writes)

    def trans(self, out, in_, ident, reads=(), writes=()):
        self.sch.op("tensor",
                    lambda e, o=out, i=in_, d=ident: e.transpose(o, i, d),
                    reads=reads, writes=writes)

    def vec(self, fn, reads=(), writes=()):
        self.sch.op("vector", fn, reads=reads, writes=writes)

    def sca(self, fn, reads=(), writes=()):
        self.sch.op("scalar", fn, reads=reads, writes=writes)

    def gps(self, fn, reads=(), writes=()):
        self.sch.op("gpsimd", fn, reads=reads, writes=writes)



def base_consts():
    return {
        "ident_bf": np.eye(128, dtype=bf16),
        "ones_row_bf": np.ones((1, 512), dtype=bf16),
        "ones_sq_bf": np.ones((128, 128), dtype=bf16),
    }


def declare_consts(ctx, consts):
    return {k: ctx.ext_in(k, v) for k, v in consts.items()}


def load_const_sb(ctx, name, ap, shape, dtyp):
    if name in ctx.const_cache:
        return ctx.const_cache[name]
    sbt = ctx.sb(name + "_sb", shape, dtyp, persist=True)
    ctx.dma("sync", sbt[:], ap, writes=(name,))
    ctx.const_cache[name] = sbt
    return sbt


def load_weight_bf(ctx, name, host_arr):
    arr = np.ascontiguousarray(host_arr, dtype=np.float32)
    ap = ctx.ext_in(name, arr)
    R, C = arr.shape
    if R % 128 == 0:
        nch = R // 128
        sbt = ctx.sb(name + "_sb", [128, nch * C], BF16)
        in_ = ap.rearrange("(c p) m -> p c m", p=128)
        out = sbt[:].rearrange("p (c m) -> p c m", c=nch)
        ctx.dma("gpsimd", out, in_, writes=(name,))
    else:
        assert R < 128
        sbt = ctx.sb(name + "_sb", [R, C], BF16)
        ctx.dma("gpsimd", sbt[:], ap, writes=(name,))
    return sbt



def emit_ln(ctx, kp, kx, xt, xc, sq, st, u_bf, uT=None, ps_tr=None, ident=None):
    ctx.sca(lambda e: e.activation(out=sq[:], in_=xt[:], func=AF.Copy,
                                   accum_out=st[:, 0:1]),
            reads=(kx,), writes=(kp + "sq", kp + "st0"))
    ctx.sca(lambda e: e.activation(out=sq[:], in_=xt[:], func=AF.Square,
                                   accum_out=st[:, 1:2]),
            reads=(kx, kp + "st0"), writes=(kp + "sq", kp + "st1"))
    ctx.vec(lambda e: e.tensor_scalar(st[:, 2:3], st[:, 0:1], 1.0 / D, None,
                                      ALU.mult),
            reads=(kp + "st0",), writes=(kp + "st2",))
    ctx.vec(lambda e: e.tensor_tensor(st[:, 4:5], st[:, 2:3], st[:, 2:3],
                                      ALU.mult),
            reads=(kp + "st2",), writes=(kp + "st4",))
    ctx.vec(lambda e: e.scalar_tensor_tensor(st[:, 3:4], st[:, 1:2], 1.0 / D,
                                             st[:, 4:5], ALU.mult, ALU.subtract),
            reads=(kp + "st1", kp + "st4"), writes=(kp + "st3",))
    ctx.vec(lambda e: e.tensor_scalar(st[:, 3:4], st[:, 3:4], 1.0, EPS_LN,
                                      ALU.mult, ALU.add),
            reads=(kp + "st3",), writes=(kp + "st3b",))
    ctx.sca(lambda e: e.activation(out=st[:, 5:6], in_=st[:, 3:4], func=AF.Sqrt),
            reads=(kp + "st3b",), writes=(kp + "st5",))
    ctx.vec(lambda e: e.reciprocal(st[:, 6:7], st[:, 5:6]),
            reads=(kp + "st5",), writes=(kp + "st6",))
    ctx.vec(lambda e: e.tensor_tensor(st[:, 7:8], st[:, 2:3], st[:, 6:7],
                                      ALU.mult),
            reads=(kp + "st2", kp + "st6"), writes=(kp + "st7",))
    ctx.vec(lambda e: e.tensor_scalar(u_bf[:], xt[:], st[:, 6:7], st[:, 7:8],
                                      ALU.mult, ALU.subtract),
            reads=(kx, kp + "st6", kp + "st7"), writes=(kp + "u",))
    if uT is not None:
        for ch in range(8):
            p = ps_tr[ch % 2]
            pk = f"pstr{ch % 2}"
            ctx.trans(p[:, 0:128], u_bf[:, ch * 128:(ch + 1) * 128], ident[:],
                      reads=(kp + "u", "ident_bf"), writes=(pk,))
            ctx.vec(lambda e, p=p, ch=ch: e.tensor_copy(
                uT[:, ch * 128:(ch + 1) * 128], p[:, 0:128]),
                reads=(pk,), writes=(kp + "uT",))



DECAY = 0.99
LR = 0.01


def s5_consts():
    lam = DECAY
    j = np.arange(512)
    c = {}
    c["s5_colmap"] = np.tile((lam ** j).astype(bf16)[None, :], (128, 1))
    c["s5_rowmap"] = np.tile((lam ** (-(j % 128).astype(np.float64))
                              ).astype(bf16)[None, :], (128, 1))
    return c


def emit_S5(ctx, aps, x_ap, y_ap, w):
    wqT = load_weight_bf(ctx, "s5_wqT", w["Wq"].T)
    wkT = load_weight_bf(ctx, "s5_wkT", w["Wk"].T)
    wvT = load_weight_bf(ctx, "s5_wvT", w["Wv"].T)
    woT = load_weight_bf(ctx, "s5_woT", w["Wo"].T)
    bq = load_weight_bf(ctx, "s5_bq", w["bq"][None, :])
    bk = load_weight_bf(ctx, "s5_bk", w["bk"][None, :])
    bv = load_weight_bf(ctx, "s5_bv", w["bv"][None, :])
    bo = load_weight_bf(ctx, "s5_bo", w["bo"][None, :])
    ident = load_const_sb(ctx, "ident_bf", aps["ident_bf"], [128, 128], BF16)
    ones = load_const_sb(ctx, "ones_row_bf", aps["ones_row_bf"], [1, 512], BF16)
    Wm = load_const_sb(ctx, "s3_W", aps["s3_W"], [128, 896], F32)
    colmap = load_const_sb(ctx, "s5_colmap", aps["s5_colmap"], [128, 512], BF16)
    rowmap = load_const_sb(ctx, "s5_rowmap", aps["s5_rowmap"], [128, 512], BF16)

    xt = ctx.sb("s5_xt", [128, 1024], F32, n=4)
    xc = ctx.sb("s5_xc", [128, 1024], F32, n=4)
    sq = ctx.sb("s5_sq", [128, 1024], F32, n=2)
    st = ctx.sb("s5_st", [128, 8], F32, n=4)
    u_bf = ctx.sb("s5_u", [128, 1024], BF16, n=4)
    uT = ctx.sb("s5_uT", [128, 1024], BF16, n=4)
    qf = ctx.sb("s5_qf", [128, T], BF16)
    kf = ctx.sb("s5_kf", [128, T], BF16)
    vtm = ctx.sb("s5_vtm", [128, T], BF16)
    PT = ctx.sb("s5_PT", [128, 32 * 512], BF16)
    otm = ctx.sb("s5_otT", [128, T], BF16)
    ysb = xc
    x2 = xt

    ps_tr = ctx.ps("s5_ptr", [128, 1024], BF16, n=2)
    ps_a = ctx.ps("s5_pa", [128, 512], F32)
    ps_b = ctx.ps("s5_pb", [128, 512], F32)
    ps_O = ctx.ps("s5_pO", [128, 512], F32, n=2)

    rot4 = None
    for i in range(NT):
        b = i % 4
        kxt = f"s5xt{b}"
        ctx.dma("sync", xt[b][:], x_ap[i * 128:(i + 1) * 128, :], writes=(kxt,))
        kp = f"s5l{b}"
        emit_ln(ctx, kp, kxt, xt[b], xc[b], sq[b % 2], st[b], u_bf[b],
                uT=uT[b], ps_tr=ps_tr, ident=ident)
        kuT = kp + "uT"
        rot4 = (ps_a, ps_b, ps_O[0], ps_O[1])

        def proj(ps, wT, wkey, bias, bkey, dest, dkey, fm, i=i, kuT=kuT, b=b):
            pk = ps.name
            for ch in range(8):
                if fm:
                    lhsT = wT[:, ch * 128:(ch + 1) * 128]
                    rhs = uT[b][:, ch * 128:(ch + 1) * 128]
                else:
                    lhsT = uT[b][:, ch * 128:(ch + 1) * 128]
                    rhs = wT[:, ch * 128:(ch + 1) * 128]
                ctx.mm(ps[:, 0:128], lhsT, rhs, start=(ch == 0), stop=False,
                       reads=(kuT, wkey), writes=(pk,))
            if fm:
                ctx.mm(ps[:, 0:128], bias[:], ones[:, 0:128], start=False, stop=True,
                       reads=(bkey, "ones_row_bf"), writes=(pk,))
            else:
                ctx.mm(ps[:, 0:128], ones[:, 0:128], bias[:], start=False, stop=True,
                       reads=(bkey, "ones_row_bf"), writes=(pk,))
            ctx.vec(lambda e, ps=ps, i=i, dest=dest: e.tensor_copy(
                dest[:, i * 128:(i + 1) * 128], ps[:, 0:128]),
                reads=(pk,), writes=(f"{dkey}{i}",))

        proj(rot4[(3 * i) % 4], wqT, "s5_wqT", bq, "s5_bq", qf, "s5qf", fm=True)
        proj(rot4[(3 * i + 1) % 4], wkT, "s5_wkT", bk, "s5_bk", kf, "s5kf",
             fm=True)
        proj(rot4[(3 * i + 2) % 4], wvT, "s5_wvT", bv, "s5_bv", vtm, "s5vtm",
             fm=False)

    for s in range(8):
        sl = slice(s * 512, (s + 1) * 512)
        tk = tuple(f"s5qf{i}" for i in range(4 * s, 4 * s + 4))
        ctx.vec(lambda e, sl=sl: e.tensor_tensor(qf[:, sl], qf[:, sl],
                                                 colmap[:], ALU.mult),
                reads=tk + ("s5_colmap",), writes=(f"s5qs{s}",))
        tk = tuple(f"s5kf{i}" for i in range(4 * s, 4 * s + 4))
        ctx.gps(lambda e, sl=sl: e.tensor_tensor(kf[:, sl], kf[:, sl],
                                                 rowmap[:], ALU.mult),
                reads=tk + ("s5_rowmap",), writes=(f"s5ks{s}",))

    allq = tuple(f"s5qs{s}" for s in range(8))
    allk = tuple(f"s5ks{s}" for s in range(8))
    lam = DECAY
    for qb in range(8):
        nkt = 4 * qb + 4
        psO = ps_O[qb % 2]
        for ktb in range(nkt):
            di = ktb - 4 * qb
            c0 = max(0, di) * 128
            bs = float(LR * lam ** (qb * 512 - ktb * 128 - 1))
            sb2 = ps_a if ktb % 2 == 0 else ps_b
            ctx.mm(sb2[:, c0:512], kf[:, ktb * 128:(ktb + 1) * 128],
                   qf[:, qb * 512 + c0:(qb + 1) * 512],
                   start=True, stop=True, reads=(allq[qb], allk[ktb // 4]),
                   writes=(sb2.name,))
            kpt = f"s5pt{ktb}"
            pts = PT[:, ktb * 512:(ktb + 1) * 512]
            if di >= 0:
                if c0 > 0:
                    ctx.gps(lambda e, pts=pts, c0=c0: e.memset(pts[:, 0:c0], 0.0),
                            writes=(kpt,))
                ctx.vec(lambda e, pts=pts, sb2=sb2, c0=c0, di=di, bs=bs:
                        e.scalar_tensor_tensor(
                            pts[:, c0:512], sb2[:, c0:512], bs,
                            Wm[:, 383:895 - di * 128], ALU.mult, ALU.mult),
                        reads=(sb2.name, "s3_W"), writes=(kpt,))
            elif ktb % 2 == 0:
                ctx.sca(lambda e, pts=pts, sb2=sb2, bs=bs: e.activation(
                    out=pts[:], in_=sb2[:], func=AF.Copy, scale=bs),
                    reads=(sb2.name,), writes=(kpt,))
            else:
                ctx.vec(lambda e, pts=pts, sb2=sb2, bs=bs: e.tensor_scalar(
                    pts[:], sb2[:], bs, None, ALU.mult),
                    reads=(sb2.name,), writes=(kpt,))
        for ktb in range(nkt):
            kpt = f"s5pt{ktb}"
            pts = PT[:, ktb * 512:(ktb + 1) * 512]
            ctx.mm(psO[:], vtm[:, ktb * 128:(ktb + 1) * 128], pts,
                   start=(ktb == 0), stop=(ktb == nkt - 1),
                   reads=(kpt, f"s5vtm{ktb}"), writes=(psO.name,))
        ctx.vec(lambda e, qb=qb, psO=psO: e.tensor_copy(
            otm[:, qb * 512:(qb + 1) * 512], psO[:]),
            reads=(psO.name,), writes=(f"s5otm{qb}",))

    rot4 = (ps_a, ps_b, ps_O[0], ps_O[1])
    for i in range(NT):
        b = i % 4
        kx2 = f"s5x2{b}"
        ctx.dma("sync", x2[b][:], x_ap[i * 128:(i + 1) * 128, :],
                writes=(kx2, f"s5xt{b}"))
        for blk in range(2):
            nsl = slice(blk * 512, (blk + 1) * 512)
            ps = rot4[(2 * i + blk) % 4]
            pk = ps.name
            ctx.mm(ps[:], otm[:, i * 128:(i + 1) * 128], woT[:, nsl],
                   start=True, stop=False,
                   reads=(f"s5otm{i // 4}", "s5_woT"), writes=(pk,))
            ctx.mm(ps[:], ones[:, 0:128], bo[:, nsl], start=False, stop=True,
                   reads=("s5_bo", "ones_row_bf"), writes=(pk,))
            ctx.vec(lambda e, ps=ps, b=b, nsl=nsl: e.tensor_tensor(
                ysb[b][:, nsl], ps[:], x2[b][:, nsl], ALU.add),
                reads=(pk, kx2), writes=(f"s5y{b}_{blk}", f"s5l{b}xc"))
        ctx.dma("gpsimd", y_ap[i * 128:(i + 1) * 128, :], ysb[b][:],
                reads=(f"s5y{b}_0", f"s5y{b}_1"))



def s6_consts():
    sel = np.zeros((8, 8 * 128), dtype=bf16)
    for e in range(8):
        sel[e, e * 128:(e + 1) * 128] = 1.0
    return {"s6_sel": sel}


def emit_S6(ctx, aps, x_ap, y_ap, w):
    E, H = 8, 1408
    NH = H // 128
    w1_ap = ctx.ext_in("s6_w1", np.ascontiguousarray(w["w1"], np.float32))
    w2_ap = ctx.ext_in("s6_w2", np.ascontiguousarray(w["w2"], np.float32))
    w3_ap = ctx.ext_in("s6_w3", np.ascontiguousarray(w["w3"], np.float32))
    w1bf = ctx.scratch("s6_w1bf", [E, 1024, H], BF16)
    w2bf = ctx.scratch("s6_w2bf", [E, H, 1024], BF16)
    w3bf = ctx.scratch("s6_w3bf", [E, 1024, H], BF16)
    if not hasattr(ctx, "s6_cast_done"):
        ctx.s6_cast_done = True
        for e_i in range(E):
            ctx.dma("gpsimd", w1bf[e_i], w1_ap[e_i], writes=(f"s6c1{e_i}",))
            ctx.dma("gpsimd", w3bf[e_i], w3_ap[e_i], writes=(f"s6c3{e_i}",))
            ctx.dma("gpsimd", w2bf[e_i], w2_ap[e_i], writes=(f"s6c2{e_i}",))
    wrT = load_weight_bf(ctx, "s6_wrT", w["w_router"].T)
    ident = load_const_sb(ctx, "ident_bf", aps["ident_bf"], [128, 128], BF16)
    sel8 = load_const_sb(ctx, "s6_sel", aps["s6_sel"], [8, 8 * 128], BF16)

    xt = ctx.sb("s6_xt", [128, 1024], F32, n=2)
    xc = ctx.sb("s6_xc", [128, 1024], F32, n=2)
    sq = ctx.sb("s6_sq", [128, 1024], F32)
    st = ctx.sb("s6_st", [128, 8], F32, n=2)
    u_bf = ctx.sb("s6_u", [128, 1024], BF16, n=2)
    uT = ctx.sb("s6_uT", [128, 8 * 2048], BF16)
    yac = ctx.sb("s6_yac", [128, 8 * 2048], BF16)
    w1s = ctx.sb("s6_w1s", [128, 8 * H], BF16)
    w3s = ctx.sb("s6_w3s", [128, 8 * H], BF16)
    w2s = ctx.sb("s6_w2s", [128, NH * 1024], BF16)
    h1s = ctx.sb("s6_h1s", [128, NH * 512], BF16, n=2)
    gbf = ctx.sb("s6_gbf", [128, 512], BF16)
    tmp = ctx.sb("s6_tmp", [128, 512], BF16, n=2)
    cb = ctx.sb("s6_cb", [128, 512], BF16, n=2)
    prob = ctx.sb("s6_prob", [128, 8], F32)
    p2 = ctx.sb("s6_p2", [128, 8], F32)
    msk = ctx.sb("s6_msk", [128, 8], F32)
    comb = ctx.sb("s6_comb", [128, 8], BF16)
    combT = ctx.sb("s6_combT", [8, 2048], BF16)
    rst = ctx.sb("s6_rst", [128, 4], F32)
    ysb = xc
    x2 = xt

    ps_tr = ctx.ps("s6_ptr", [128, 1024], BF16, n=2)
    ps_g = ctx.ps("s6_pg", [128, 512], F32, n=2)
    ps_u = ctx.ps("s6_pu", [128, 512], F32, n=2)
    ps_y = ctx.ps("s6_py", [128, 512], F32, n=2)

    for half in range(2):
        t0 = half * 2048
        def emit_tile6(it, half=half):
            i = half * 16 + it
            b = it % 2
            kxt = f"s6xt{b}"
            ctx.dma("sync", xt[b][:], x_ap[i * 128:(i + 1) * 128, :], writes=(kxt,))
            kp = f"s6l{b}"
            emit_ln(ctx, kp, kxt, xt[b], xc[b], sq, st[b], u_bf[b],
                    uT=None, ps_tr=None, ident=None)
            for ch in range(8):
                p = ps_tr[ch % 2]
                pk = p.name
                ctx.trans(p[:, 0:128], u_bf[b][:, ch * 128:(ch + 1) * 128], ident[:],
                          reads=(kp + "u", "ident_bf"), writes=(pk,))
                ctx.vec(lambda e, p=p, ch=ch, it=it: e.tensor_copy(
                    uT[:, ch * 2048 + it * 128: ch * 2048 + (it + 1) * 128],
                    p[:, 0:128]),
                    reads=(pk,), writes=(f"s6uT{ch}_{it // 4}",))
            pg = ps_g[0]
            for ch in range(8):
                ctx.mm(pg[:, 0:8], uT[:, ch * 2048 + it * 128: ch * 2048 + (it + 1) * 128],
                       wrT[:, ch * 8:(ch + 1) * 8], start=(ch == 0), stop=(ch == 7),
                       reads=(f"s6uT{ch}_{it // 4}", "s6_wrT"), writes=(pg.name,))
            ctx.vec(lambda e, pg=pg: e.reduce_max(rst[:, 0:1], pg[:, 0:8], axis=AX.X),
                    reads=(pg.name,), writes=("s6rst",))
            ctx.vec(lambda e: e.tensor_scalar(rst[:, 1:2], rst[:, 0:1], -1.0, None,
                                              ALU.mult),
                    reads=("s6rst",), writes=("s6rst",))
            ctx.sca(lambda e, pg=pg: e.activation(out=prob[:], in_=pg[:, 0:8], func=AF.Exp,
                                           bias=rst[:, 1:2], scale=1.0,
                                           accum_out=rst[:, 2:3]),
                    reads=(pg.name, "s6rst"), writes=("s6prob", "s6rst"))
            ctx.vec(lambda e: e.reciprocal(rst[:, 3:4], rst[:, 2:3]),
                    reads=("s6rst",), writes=("s6rst",))
            ctx.vec(lambda e: e.tensor_scalar(prob[:], prob[:], rst[:, 3:4], None,
                                              ALU.mult),
                    reads=("s6prob", "s6rst"), writes=("s6prob",))
            ctx.vec(lambda e: e.reduce_max(rst[:, 0:1], prob[:], axis=AX.X),
                    reads=("s6prob",), writes=("s6rst",))
            ctx.vec(lambda e: e.tensor_scalar(msk[:], prob[:], rst[:, 0:1], None,
                                              ALU.is_equal),
                    reads=("s6prob", "s6rst"), writes=("s6msk",))
            ctx.vec(lambda e: e.scalar_tensor_tensor(p2[:], msk[:], -1e30, prob[:],
                                                     ALU.mult, ALU.add),
                    reads=("s6msk", "s6prob"), writes=("s6p2",))
            ctx.vec(lambda e: e.reduce_max(rst[:, 1:2], p2[:], axis=AX.X),
                    reads=("s6p2",), writes=("s6rst",))
            ctx.vec(lambda e: e.tensor_scalar(msk[:], prob[:], rst[:, 1:2], None,
                                              ALU.is_ge),
                    reads=("s6prob", "s6rst"), writes=("s6msk",))
            ctx.vec(lambda e: e.tensor_tensor(comb[:], prob[:], msk[:], ALU.mult),
                    reads=("s6prob", "s6msk"), writes=("s6comb",))
            p = ps_tr[0]
            ctx.trans(p[0:8, 0:128], comb[:], ident[:],
                      reads=("s6comb", "ident_bf"), writes=(p.name,))
            ctx.vec(lambda e, p=p, it=it: e.tensor_copy(
                combT[0:8, it * 128:(it + 1) * 128], p[0:8, 0:128]),
                reads=(p.name,), writes=(f"s6combT{it // 4}",))

        def s6_gu(g):
            e_i, tb = divmod(g, 4)
            gp = g % 2
            tsl = slice(tb * 512, (tb + 1) * 512)
            wload = ((("s6w1", w1bf, w1s, 8, "scalar", f"s6c1{e_i}"),
                      ("s6w3", w3bf, w3s, 8, "sync", f"s6c3{e_i}"))
                     if tb == 0 else
                     ((("s6w2", w2bf, w2s, NH, "sync", f"s6c2{e_i}"),)
                      if tb == 1 else ()))
            for nm, ap3, sbt, nch, q, ck in wload:
                src = ap3[e_i].rearrange("(c p) m -> p c m", p=128)
                dst = sbt[:].rearrange("p (c m) -> p c m", c=nch)
                ctx.dma(q, dst, src, reads=(ck,), writes=(nm,))
            pu = ps_u[0]
            ctx.mm(pu[:], sel8[:, e_i * 128:(e_i + 1) * 128], combT[:, tsl],
                   start=True, stop=True,
                   reads=("s6_sel", f"s6combT{tb}"), writes=(pu.name,))
            ctx.vec(lambda e, pu=pu, gp=gp: e.tensor_copy(cb[gp][:], pu[:]),
                    reads=(pu.name,), writes=(f"s6cb{gp}",))
            for hch in range(NH):
                pg = ps_g[hch % 2]
                pu2 = ps_u[hch % 2]
                for dch in range(8):
                    usl = slice(dch * 2048 + tb * 512, dch * 2048 + (tb + 1) * 512)
                    ctx.mm(pg[:], w1s[:, dch * H + hch * 128: dch * H + (hch + 1) * 128],
                           uT[:, usl], start=(dch == 0), stop=(dch == 7),
                           reads=(f"s6uT{dch}_{tb}", "s6w1"), writes=(pg.name,))
                for dch in range(8):
                    usl = slice(dch * 2048 + tb * 512, dch * 2048 + (tb + 1) * 512)
                    ctx.mm(pu2[:], w3s[:, dch * H + hch * 128: dch * H + (hch + 1) * 128],
                           uT[:, usl], start=(dch == 0), stop=(dch == 7),
                           reads=(f"s6uT{dch}_{tb}", "s6w3"), writes=(pu2.name,))
                ctx.sca(lambda e, pg=pg: e.activation(out=gbf[:], in_=pg[:],
                                                      func=AF.Silu),
                        reads=(pg.name,), writes=("s6gbf",))
                ctx.vec(lambda e, pu2=pu2, hch=hch, gp=gp: e.tensor_tensor(
                    h1s[gp][:, hch * 512:(hch + 1) * 512], gbf[:], pu2[:],
                    ALU.mult),
                    reads=("s6gbf", pu2.name), writes=(f"s6h1{gp}_{hch}",))

        def s6_down(g):
            e_i, tb = divmod(g, 4)
            gp = g % 2
            for ddch in range(8):
                py = ps_y[ddch % 2]
                for hch in range(NH):
                    ctx.mm(py[:], w2s[:, hch * 1024 + ddch * 128: hch * 1024 + (ddch + 1) * 128],
                           h1s[gp][:, hch * 512:(hch + 1) * 512],
                           start=(hch == 0), stop=(hch == NH - 1),
                           reads=(f"s6h1{gp}_{hch}", "s6w2"), writes=(py.name,))
                ysl = slice(ddch * 2048 + tb * 512, ddch * 2048 + (tb + 1) * 512)
                kya = f"s6ya{ddch}_{tb}"
                if e_i == 0:
                    ctx.vec(lambda e, py=py, ysl=ysl, gp=gp: e.tensor_tensor(
                        yac[:, ysl], py[:], cb[gp][:], ALU.mult),
                        reads=(py.name, f"s6cb{gp}"), writes=(kya,))
                else:
                    tm = tmp[ddch % 2]
                    ktm2 = f"s6tmp{ddch % 2}"
                    ctx.vec(lambda e, py=py, tm=tm, gp=gp: e.tensor_tensor(
                        tm[:], py[:], cb[gp][:], ALU.mult),
                        reads=(py.name, f"s6cb{gp}"), writes=(ktm2,))
                    ctx.vec(lambda e, tm=tm, ysl=ysl: e.tensor_tensor(
                        yac[:, ysl], yac[:, ysl], tm[:], ALU.add),
                        reads=(kya, ktm2), writes=(kya,))

        for it in range(4):
            emit_tile6(it)
        for g in range(4 * E):
            if 1 <= g <= 3:
                for it in range(4 * g, 4 * g + 4):
                    emit_tile6(it)
            s6_gu(g)
            if g >= 1:
                s6_down(g - 1)
        s6_down(4 * E - 1)

        for it in range(16):
            i = half * 16 + it
            b = it % 2
            kx2 = f"s6x2{b}"
            ctx.dma("sync", x2[b][:], x_ap[i * 128:(i + 1) * 128, :],
                    writes=(kx2, f"s6xt{b}"))
            for ch in range(8):
                p = ps_tr[ch % 2]
                ctx.trans(p[:, 0:128], yac[:, ch * 2048 + it * 128: ch * 2048 + (it + 1) * 128],
                          ident[:],
                          reads=(f"s6ya{ch}_{it // 4}", "ident_bf"), writes=(p.name,))
                ctx.vec(lambda e, p=p, ch=ch, b=b: e.tensor_tensor(
                    ysb[b][:, ch * 128:(ch + 1) * 128], p[:, 0:128],
                    x2[b][:, ch * 128:(ch + 1) * 128], ALU.add),
                    reads=(p.name, kx2, f"s6xt{b}"),
                    writes=(f"s6y{b}_{ch}", f"s6l{b}xc"))
            ctx.dma("gpsimd", y_ap[i * 128:(i + 1) * 128, :], ysb[b][:],
                    reads=tuple(f"s6y{b}_{ch}" for ch in range(8)) +
                          (f"s6l{b}xc",))


def w_s6(inputs):
    return {"w_router": inputs["L1_ffn_w_router"], "w1": inputs["L1_ffn_w1"],
            "w2": inputs["L1_ffn_w2"], "w3": inputs["L1_ffn_w3"]}



def s2_consts():
    sel = np.zeros((16, 16 * 128), dtype=bf16)
    for e in range(16):
        sel[e, e * 128:(e + 1) * 128] = 1.0
    return {"s2_sel": sel}


def emit_S2(ctx, aps, x_ap, y_ap, w):
    E, I = 16, 512
    NI = I // 128
    gw_ap = ctx.ext_in("s2_gw", np.ascontiguousarray(w["expert_gate_w"], np.float32))
    uw_ap = ctx.ext_in("s2_uw", np.ascontiguousarray(w["expert_up_w"], np.float32))
    dw_ap = ctx.ext_in("s2_dw", np.ascontiguousarray(w["expert_down_w"], np.float32))
    gwbf = ctx.scratch("s2_gwbf", [15, 1024, I], BF16)
    uwbf = ctx.scratch("s2_uwbf", [15, 1024, I], BF16)
    dwbf = ctx.scratch("s2_dwbf", [15, I, 1024], BF16)
    if not hasattr(ctx, "s2_cast_done"):
        ctx.s2_cast_done = True
        for e_i in range(15):
            ctx.dma("gpsimd", gwbf[e_i], gw_ap[e_i], writes=(f"s2cg{e_i}",))
            ctx.dma("gpsimd", uwbf[e_i], uw_ap[e_i], writes=(f"s2cu{e_i}",))
            ctx.dma("gpsimd", dwbf[e_i], dw_ap[e_i], writes=(f"s2cd{e_i}",))
    rw = load_weight_bf(ctx, "s2_rw", w["router_w"])
    rb = load_weight_bf(ctx, "s2_rb", w["router_bias"][None, :])
    convT = np.ascontiguousarray(np.transpose(w["conv_w"], (2, 1, 0)), np.float32)
    cv = [load_weight_bf(ctx, f"s2_cv{k}", convT[k]) for k in range(3)]
    sup = load_weight_bf(ctx, "s2_sup", w["shared_up_w"])
    sdn = load_weight_bf(ctx, "s2_sdn", w["shared_down_w"])
    ident = load_const_sb(ctx, "ident_bf", aps["ident_bf"], [128, 128], BF16)
    ones = load_const_sb(ctx, "ones_row_bf", aps["ones_row_bf"], [1, 512], BF16)
    sel16 = load_const_sb(ctx, "s2_sel", aps["s2_sel"], [16, 16 * 128], BF16)

    xt = ctx.sb("s2_xt", [128, 1024], F32, n=2)
    xc = ctx.sb("s2_xc", [128, 1024], F32, n=2)
    sq = ctx.sb("s2_sq", [128, 1024], F32)
    st = ctx.sb("s2_st", [128, 8], F32, n=2)
    u_bf = ctx.sb("s2_u", [128, 1024], BF16, n=2)
    TH = 2048 + 2
    uT = ctx.sb("s2_uT", [128, 8 * TH], BF16)
    yac = ctx.sb("s2_yac", [128, 8 * 2048], BF16)
    gws = ctx.sb("s2_gws", [128, 8 * I], BF16)
    uws = ctx.sb("s2_uws", [128, 8 * I], BF16)
    dws = ctx.sb("s2_dws", [128, NI * 1024], BF16)
    h1s = ctx.sb("s2_h1s", [128, NI * 512], BF16, n=2)
    gbf = ctx.sb("s2_gbf", [128, 512], BF16)
    tmp = ctx.sb("s2_tmp", [128, 512], BF16, n=2)
    cb = ctx.sb("s2_cb", [128, 512], BF16, n=2)
    cb15 = ctx.sb("s2_cb15", [128, 512], BF16, n=2)
    prob = ctx.sb("s2_prob", [128, 16], F32)
    p2 = ctx.sb("s2_p2", [128, 16], F32)
    msk = ctx.sb("s2_msk", [128, 16], F32)
    comb = ctx.sb("s2_comb", [128, 16], BF16)
    combT = ctx.sb("s2_combT", [16, 2048], BF16)
    rst = ctx.sb("s2_rst", [128, 8], F32)
    ysb = xc
    x2 = xt

    ps_tr = ctx.ps("s2_ptr", [128, 1024], BF16, n=2)
    ps_g = ctx.ps("s2_pg", [128, 512], F32, n=2)
    ps_u = ctx.ps("s2_pu", [128, 512], F32, n=2)
    ps_y = ctx.ps("s2_py", [128, 512], F32, n=2)

    def uslice(ch, t0, n):
        return uT[:, ch * TH + 2 + t0: ch * TH + 2 + t0 + n]

    for ch in range(8):
        ctx.vec(lambda e, ch=ch: e.memset(uslice(ch, -2, 2), 0.0),
                writes=(f"s2pad{ch}",))

    for half in range(2):
        if half == 1:
            for ch in range(8):
                ctx.vec(lambda e, ch=ch: e.tensor_copy(
                    uslice(ch, -2, 2), uslice(ch, 2046, 2)),
                    reads=(f"s2uT{ch}_3",), writes=(f"s2pad{ch}",))
        def emit_tile2(it, half=half):
            i = half * 16 + it
            b = it % 2
            kxt = f"s2xt{b}"
            ctx.dma("sync", xt[b][:], x_ap[i * 128:(i + 1) * 128, :], writes=(kxt,))
            kp = f"s2l{b}"
            emit_ln(ctx, kp, kxt, xt[b], xc[b], sq, st[b], u_bf[b])
            for ch in range(8):
                p = ps_tr[ch % 2]
                rd = (f"s2pad{ch}",) if it == 0 else ()
                ctx.trans(p[:, 0:128], u_bf[b][:, ch * 128:(ch + 1) * 128], ident[:],
                          reads=(kp + "u", "ident_bf"), writes=(p.name,))
                ctx.vec(lambda e, p=p, ch=ch, it=it: e.tensor_copy(
                    uslice(ch, it * 128, 128), p[:, 0:128]),
                    reads=(p.name,) + rd, writes=(f"s2uT{ch}_{it // 4}",))
            pg = ps_g[0]
            for ch in range(8):
                ctx.mm(pg[:, 0:16], uslice(ch, it * 128, 128),
                       rw[:, ch * 16:(ch + 1) * 16], start=(ch == 0), stop=False,
                       reads=(f"s2uT{ch}_{it // 4}", "s2_rw"), writes=(pg.name,))
            ctx.mm(pg[:, 0:16], ones[:, 0:128], rb[:], start=False, stop=True,
                   reads=("s2_rb", "ones_row_bf"), writes=(pg.name,))
            ctx.vec(lambda e, pg=pg: e.reduce_max(rst[:, 0:1], pg[:, 0:16], axis=AX.X),
                    reads=(pg.name,), writes=("s2rst",))
            ctx.vec(lambda e: e.tensor_scalar(rst[:, 1:2], rst[:, 0:1], -1.0, None,
                                              ALU.mult),
                    reads=("s2rst",), writes=("s2rst",))
            ctx.sca(lambda e, pg=pg: e.activation(out=prob[:], in_=pg[:, 0:16], func=AF.Exp,
                                           bias=rst[:, 1:2], scale=1.0,
                                           accum_out=rst[:, 2:3]),
                    reads=(pg.name, "s2rst"), writes=("s2prob", "s2rst"))
            ctx.vec(lambda e: e.reciprocal(rst[:, 3:4], rst[:, 2:3]),
                    reads=("s2rst",), writes=("s2rst",))
            ctx.vec(lambda e: e.tensor_scalar(prob[:], prob[:], rst[:, 3:4], None,
                                              ALU.mult),
                    reads=("s2prob", "s2rst"), writes=("s2prob",))
            ctx.vec(lambda e: e.reduce_max(rst[:, 0:1], prob[:], axis=AX.X),
                    reads=("s2prob",), writes=("s2rst",))
            ctx.vec(lambda e: e.tensor_scalar(msk[:], prob[:], rst[:, 0:1], None,
                                              ALU.is_equal),
                    reads=("s2prob", "s2rst"), writes=("s2msk",))
            ctx.vec(lambda e: e.scalar_tensor_tensor(p2[:], msk[:], -1e30, prob[:],
                                                     ALU.mult, ALU.add),
                    reads=("s2msk", "s2prob"), writes=("s2p2",))
            ctx.vec(lambda e: e.reduce_max(rst[:, 1:2], p2[:], axis=AX.X),
                    reads=("s2p2",), writes=("s2rst",))
            ctx.vec(lambda e: e.tensor_scalar(msk[:], prob[:], rst[:, 1:2], None,
                                              ALU.is_ge),
                    reads=("s2prob", "s2rst"), writes=("s2msk",))
            ctx.vec(lambda e: e.tensor_tensor(rst[:, 4:5], rst[:, 0:1], rst[:, 1:2],
                                              ALU.add),
                    reads=("s2rst",), writes=("s2rst",))
            ctx.vec(lambda e: e.tensor_scalar(rst[:, 5:6], rst[:, 4:5], 1e-6, None,
                                              ALU.add),
                    reads=("s2rst",), writes=("s2rst",))
            ctx.vec(lambda e: e.reciprocal(rst[:, 6:7], rst[:, 5:6]),
                    reads=("s2rst",), writes=("s2rst",))
            ctx.vec(lambda e: e.tensor_tensor(p2[:], prob[:], msk[:], ALU.mult),
                    reads=("s2prob", "s2msk"), writes=("s2p2",))
            ctx.vec(lambda e: e.tensor_scalar(comb[:], p2[:], rst[:, 6:7], None,
                                              ALU.mult),
                    reads=("s2p2", "s2rst"), writes=("s2comb",))
            p = ps_tr[0]
            ctx.trans(p[0:16, 0:128], comb[:], ident[:],
                      reads=("s2comb", "ident_bf"), writes=(p.name,))
            ctx.vec(lambda e, p=p, it=it: e.tensor_copy(
                combT[0:16, it * 128:(it + 1) * 128], p[0:16, 0:128]),
                reads=(p.name,), writes=(f"s2combT{it // 4}",))

        def s2_gu_shared(tb):
            t0 = tb * 512
            gp = tb % 2
            tsl = slice(tb * 512, (tb + 1) * 512)
            uTk = tuple(f"s2uT{ch}_{tb}" for ch in range(8))
            if tb > 0:
                uTkm = uTk + tuple(f"s2uT{ch}_{tb - 1}" for ch in range(8))
            else:
                uTkm = uTk + tuple(f"s2pad{ch}" for ch in range(8))
            pu = ps_u[0]
            ctx.mm(pu[:], sel16[:, 15 * 128:16 * 128], combT[:, tsl],
                   start=True, stop=True,
                   reads=("s2_sel", f"s2combT{tb}"), writes=(pu.name,))
            ctx.vec(lambda e, pu=pu, gp=gp: e.tensor_copy(cb15[gp][:], pu[:]),
                    reads=(pu.name,), writes=(f"s2cb15{gp}",))
            for ich in range(NI):
                pg = ps_g[ich % 2]
                pu2 = ps_u[ich % 2]
                first = True
                for k in range(3):
                    for dch in range(8):
                        ctx.mm(pg[:], cv[k][:, dch * I + ich * 128: dch * I + (ich + 1) * 128],
                               uslice(dch, t0 + k - 2, 512),
                               start=first, stop=(k == 2 and dch == 7),
                               reads=uTkm + (f"s2_cv{k}",), writes=(pg.name,))
                        first = False
                for dch in range(8):
                    ctx.mm(pu2[:], sup[:, dch * I + ich * 128: dch * I + (ich + 1) * 128],
                           uslice(dch, t0, 512), start=(dch == 0), stop=(dch == 7),
                           reads=uTk + ("s2_sup",), writes=(pu2.name,))
                ctx.sca(lambda e, pg=pg: e.activation(out=gbf[:], in_=pg[:],
                                                      func=AF.Silu),
                        reads=(pg.name,), writes=("s2gbf",))
                ctx.vec(lambda e, pu2=pu2, ich=ich, gp=gp: e.tensor_tensor(
                    h1s[gp][:, ich * 512:(ich + 1) * 512], gbf[:], pu2[:],
                    ALU.mult),
                    reads=("s2gbf", pu2.name), writes=(f"s2h1{gp}_{ich}",))

        def s2_down_shared(tb):
            t0 = tb * 512
            gp = tb % 2
            for ddch in range(8):
                py = ps_y[ddch % 2]
                for ich in range(NI):
                    ctx.mm(py[:], sdn[:, ich * 1024 + ddch * 128: ich * 1024 + (ddch + 1) * 128],
                           h1s[gp][:, ich * 512:(ich + 1) * 512],
                           start=(ich == 0), stop=(ich == NI - 1),
                           reads=(f"s2h1{gp}_{ich}", "s2_sdn"), writes=(py.name,))
                ysl = slice(ddch * 2048 + tb * 512, ddch * 2048 + (tb + 1) * 512)
                ctx.vec(lambda e, py=py, ysl=ysl: e.tensor_copy(yac[:, ysl], py[:]),
                        reads=(py.name,), writes=(f"s2ya{ddch}_{tb}",))
            for ddch in range(8):
                tm = tmp[ddch % 2]
                ktm2 = f"s2tmp{ddch % 2}"
                ysl = slice(ddch * 2048 + tb * 512, ddch * 2048 + (tb + 1) * 512)
                ctx.vec(lambda e, tm=tm, ddch=ddch, t0=t0, gp=gp: e.tensor_tensor(
                    tm[:], uslice(ddch, t0, 512), cb15[gp][:], ALU.mult),
                    reads=(f"s2uT{ddch}_{tb}", f"s2cb15{gp}"), writes=(ktm2,))
                ctx.vec(lambda e, tm=tm, ysl=ysl: e.tensor_tensor(
                    yac[:, ysl], yac[:, ysl], tm[:], ALU.add),
                    reads=(f"s2ya{ddch}_{tb}", ktm2), writes=(f"s2ya{ddch}_{tb}",))

        def s2_gu_exp(g):
            e_i, tb = divmod(g, 4)
            gp = g % 2
            t0 = tb * 512
            tsl = slice(tb * 512, (tb + 1) * 512)
            uTk = tuple(f"s2uT{ch}_{tb}" for ch in range(8))
            wload = ((("s2gw", gwbf, gws, 8, "scalar", f"s2cg{e_i}"),
                      ("s2uw", uwbf, uws, 8, "sync", f"s2cu{e_i}"))
                     if tb == 0 else
                     ((("s2dw", dwbf, dws, NI, "sync", f"s2cd{e_i}"),)
                      if tb == 1 else ()))
            for nm, ap3, sbt, nch, q, ck in wload:
                src = ap3[e_i].rearrange("(c p) m -> p c m", p=128)
                dst = sbt[:].rearrange("p (c m) -> p c m", c=nch)
                ctx.dma(q, dst, src, reads=(ck,), writes=(nm,))
            pu = ps_u[0]
            ctx.mm(pu[:], sel16[:, e_i * 128:(e_i + 1) * 128], combT[:, tsl],
                   start=True, stop=True,
                   reads=("s2_sel", f"s2combT{tb}"), writes=(pu.name,))
            ctx.vec(lambda e, pu=pu, gp=gp: e.tensor_copy(cb[gp][:], pu[:]),
                    reads=(pu.name,), writes=(f"s2cb{gp}",))
            for ich in range(NI):
                pg = ps_g[ich % 2]
                pu2 = ps_u[ich % 2]
                for dch in range(8):
                    ctx.mm(pg[:], gws[:, dch * I + ich * 128: dch * I + (ich + 1) * 128],
                           uslice(dch, t0, 512), start=(dch == 0), stop=(dch == 7),
                           reads=uTk + ("s2gw",), writes=(pg.name,))
                for dch in range(8):
                    ctx.mm(pu2[:], uws[:, dch * I + ich * 128: dch * I + (ich + 1) * 128],
                           uslice(dch, t0, 512), start=(dch == 0), stop=(dch == 7),
                           reads=uTk + ("s2uw",), writes=(pu2.name,))
                ctx.sca(lambda e, pg=pg: e.activation(out=gbf[:], in_=pg[:],
                                                      func=AF.Silu),
                        reads=(pg.name,), writes=("s2gbf",))
                ctx.vec(lambda e, pu2=pu2, ich=ich, gp=gp: e.tensor_tensor(
                    h1s[gp][:, ich * 512:(ich + 1) * 512], gbf[:], pu2[:],
                    ALU.mult),
                    reads=("s2gbf", pu2.name), writes=(f"s2h1{gp}_{ich}",))

        def s2_down_exp(g):
            e_i, tb = divmod(g, 4)
            gp = g % 2
            for ddch in range(8):
                py = ps_y[ddch % 2]
                for ich in range(NI):
                    ctx.mm(py[:], dws[:, ich * 1024 + ddch * 128: ich * 1024 + (ddch + 1) * 128],
                           h1s[gp][:, ich * 512:(ich + 1) * 512],
                           start=(ich == 0), stop=(ich == NI - 1),
                           reads=(f"s2h1{gp}_{ich}", "s2dw"), writes=(py.name,))
                tm = tmp[ddch % 2]
                ktm2 = f"s2tmp{ddch % 2}"
                ysl = slice(ddch * 2048 + tb * 512, ddch * 2048 + (tb + 1) * 512)
                ctx.vec(lambda e, py=py, tm=tm, gp=gp: e.tensor_tensor(
                    tm[:], py[:], cb[gp][:], ALU.mult),
                    reads=(py.name, f"s2cb{gp}"), writes=(ktm2,))
                ctx.vec(lambda e, tm=tm, ysl=ysl: e.tensor_tensor(
                    yac[:, ysl], yac[:, ysl], tm[:], ALU.add),
                    reads=(f"s2ya{ddch}_{tb}", ktm2), writes=(f"s2ya{ddch}_{tb}",))

        def s2_gu(g):
            if g < 4:
                s2_gu_shared(g)
            else:
                s2_gu_exp(g - 4)

        def s2_down(g):
            if g < 4:
                s2_down_shared(g)
            else:
                s2_down_exp(g - 4)

        for it in range(4):
            emit_tile2(it)
        for g in range(4 + 15 * 4):
            if 1 <= g <= 3:
                for it in range(4 * g, 4 * g + 4):
                    emit_tile2(it)
            s2_gu(g)
            if g >= 1:
                s2_down(g - 1)
        s2_down(4 + 15 * 4 - 1)

        for it in range(16):
            i = half * 16 + it
            b = it % 2
            kx2 = f"s2x2{b}"
            ctx.dma("sync", x2[b][:], x_ap[i * 128:(i + 1) * 128, :],
                    writes=(kx2, f"s2xt{b}"))
            for ch in range(8):
                p = ps_tr[ch % 2]
                ctx.trans(p[:, 0:128], yac[:, ch * 2048 + it * 128: ch * 2048 + (it + 1) * 128],
                          ident[:],
                          reads=(f"s2ya{ch}_{it // 4}", "ident_bf"), writes=(p.name,))
                ctx.vec(lambda e, p=p, ch=ch, b=b: e.tensor_tensor(
                    ysb[b][:, ch * 128:(ch + 1) * 128], p[:, 0:128],
                    x2[b][:, ch * 128:(ch + 1) * 128], ALU.add),
                    reads=(p.name, kx2, f"s2xt{b}"),
                    writes=(f"s2y{b}_{ch}", f"s2l{b}xc"))
            ctx.dma("gpsimd", y_ap[i * 128:(i + 1) * 128, :], ysb[b][:],
                    reads=tuple(f"s2y{b}_{ch}" for ch in range(8)) +
                          (f"s2l{b}xc",))


def w_s2(inputs):
    return {k: inputs["L0_ffn_" + k] for k in
            ("router_w", "router_bias", "expert_gate_w", "expert_up_w",
             "expert_down_w", "conv_w", "shared_up_w", "shared_down_w")}



def s3_consts():
    kt = np.arange(128)[:, None]
    jj = np.arange(896)[None, :]
    c = {"s3_W": ((jj - 384) >= kt).astype(np.float32)}
    c["ones_col_bf"] = np.ones((128, 1), dtype=bf16)
    return c


def emit_S3(ctx, aps, x_ap, y_ap, w):
    wqT = load_weight_bf(ctx, "s3_wqT", w["Wq"].T)
    wkT = load_weight_bf(ctx, "s3_wkT", w["Wk"].T)
    wvT = load_weight_bf(ctx, "s3_wvT", w["Wv"].T)
    woT = load_weight_bf(ctx, "s3_woT", w["Wo"].T)
    nw_ap = ctx.ext_in("s3_nwf", np.ascontiguousarray(w["norm_w"][:, None],
                                                      np.float32))
    nw = ctx.sb("s3_nw_sb", [128, 1], F32)
    ctx.dma("gpsimd", nw[:], nw_ap, writes=("s3_nw",))
    ident = load_const_sb(ctx, "ident_bf", aps["ident_bf"], [128, 128], BF16)
    ones = load_const_sb(ctx, "ones_row_bf", aps["ones_row_bf"], [1, 512], BF16)
    onesc = load_const_sb(ctx, "ones_col_bf", aps["ones_col_bf"], [128, 1], BF16)
    Wm = load_const_sb(ctx, "s3_W", aps["s3_W"], [128, 896], F32)

    qT_sc = ctx.scratch("s3_qTsc", [8, 128, T], BF16)
    kT_sc = ctx.scratch("s3_kTsc", [8, 128, T], BF16)
    vtm_sc = ctx.scratch("s3_vtmsc", [T, D], BF16)
    onT_sc = ctx.scratch("s3_onTsc", [8, 128, T], BF16)

    xt = ctx.sb("s3_xt", [128, 1024], F32, n=2)
    xc = ctx.sb("s3_xc", [128, 1024], F32, n=2)
    sq = ctx.sb("s3_sq", [128, 1024], F32)
    st = ctx.sb("s3_st", [128, 8], F32, n=2)
    u_bf = ctx.sb("s3_u", [128, 1024], BF16, n=2)
    uT = ctx.sb("s3_uT", [128, 1024], BF16, n=2)
    ktm = ctx.sb("s3_ktm", [128, 1024], BF16, n=2)
    vtm = ctx.sb("s3_vtm", [128, 1024], BF16, n=2)
    qtm = ctx.sb("s3_qtm", [128, 1024], BF16, n=2)
    qTt = ctx.sb("s3_qTt", [128, 1024], BF16, n=2)
    kTt = ctx.sb("s3_kTt", [128, 1024], BF16, n=2)
    qTh = ctx.sb("s3_qTh", [128, T], BF16, n=2)
    kTh = ctx.sb("s3_kTh", [128, T], BF16, n=2)
    vth = ctx.sb("s3_vth", [128, 32 * 128], BF16, n=2)
    PT = ctx.sb("s3_PT", [128, 10 * 512], BF16)
    obf = ctx.sb("s3_obf", [128, 512], BF16, n=2)
    o1 = ctx.sb("s3_o1", [128, 512], BF16, n=2)
    o2 = ctx.sb("s3_o2", [128, 512], BF16, n=2)
    t1 = ctx.sb("s3_t1", [128, 512], BF16, n=2)
    onb = ctx.sb("s3_onb", [128, 512], BF16, n=2)
    rowf = ctx.sb("s3_rowf", [1, 512], F32, n=2)
    rdrow = ctx.sb("s3_rdrow", [1, 512], BF16, n=2)
    rrrow = ctx.sb("s3_rrrow", [1, 512], BF16, n=1)
    onTt = qtm
    ysb = xc
    x2 = xt

    ps_tr = ctx.ps("s3_ptr", [128, 1024], BF16, n=2)
    ps_S = ctx.ps("s3_S", [128, 512], F32, n=2)
    ps_O = ctx.ps("s3_O", [128, 512], F32, n=2)
    ps_D = ctx.ps("s3_D", [128, 512], F32, n=1)
    ps_B = ctx.ps("s3_B", [128, 512], F32, n=1)

    for i in range(NT):
        b = i % 2
        kxt = f"s3xt{b}"
        ctx.dma("sync", xt[b][:], x_ap[i * 128:(i + 1) * 128, :], writes=(kxt,))
        kp = f"s3l{b}"
        emit_ln(ctx, kp, kxt, xt[b], xc[b], sq, st[b], u_bf[b],
                uT=uT[b], ps_tr=ps_tr, ident=ident)
        kuT = kp + "uT"

        def proj_tm(wT, wkey, dest, dkey, b=b, kuT=kuT):
            for blk in range(2):
                ps = ps_S[blk]
                for ch in range(8):
                    ctx.mm(ps[:], uT[b][:, ch * 128:(ch + 1) * 128],
                           wT[:, ch * 1024 + blk * 512: ch * 1024 + blk * 512 + 512],
                           start=(ch == 0), stop=(ch == 7),
                           reads=(kuT, wkey), writes=(ps.name,))
                ctx.vec(lambda e, ps=ps, blk=blk, dest=dest: e.tensor_copy(
                    dest[:, blk * 512:(blk + 1) * 512], ps[:]),
                    reads=(ps.name,), writes=(dkey,))

        proj_tm(wqT, "s3_wqT", qtm[b], f"s3qtm{b}")
        proj_tm(wkT, "s3_wkT", ktm[b], f"s3ktm{b}")
        proj_tm(wvT, "s3_wvT", vtm[b], f"s3vtm{b}")
        for src, dest, dkey, skey in ((qtm[b], qTt[b], f"s3qTt{b}", f"s3qtm{b}"),
                                      (ktm[b], kTt[b], f"s3kTt{b}", f"s3ktm{b}")):
            for ch in range(8):
                p = ps_tr[ch % 2]
                ctx.trans(p[:, 0:128], src[:, ch * 128:(ch + 1) * 128], ident[:],
                          reads=(skey, "ident_bf"), writes=(p.name,))
                ctx.vec(lambda e, p=p, ch=ch, dest=dest: e.tensor_copy(
                    dest[:, ch * 128:(ch + 1) * 128], p[:, 0:128]),
                    reads=(p.name,), writes=(dkey,))
        tsl = slice(i * 128, (i + 1) * 128)
        ctx.dma("gpsimd", qT_sc[:, :, tsl].rearrange("c p m -> p c m"),
                qTt[b][:].rearrange("p (c m) -> p c m", c=8), reads=(f"s3qTt{b}",),
                writes=(f"sc3qT{i}",))
        ctx.dma("gpsimd", kT_sc[:, :, tsl].rearrange("c p m -> p c m"),
                kTt[b][:].rearrange("p (c m) -> p c m", c=8), reads=(f"s3kTt{b}",),
                writes=(f"sc3kT{i}",))
        ctx.dma("gpsimd", vtm_sc[tsl, :], vtm[b][:], reads=(f"s3vtm{b}",),
                writes=(f"sc3v{i}",))

    allq = tuple(f"sc3qT{i}" for i in range(NT))
    allk = tuple(f"sc3kT{i}" for i in range(NT))
    allv = tuple(f"sc3v{i}" for i in range(NT))

    def load_head(h):
        p = h % 2
        ctx.dma("sync", qTh[p][:], qT_sc[h], reads=allq, writes=(f"s3qTh{p}",))
        ctx.dma("sync", kTh[p][:], kT_sc[h], reads=allk, writes=(f"s3kTh{p}",))
        ctx.dma("sync", vth[p][:].rearrange("p (n m) -> p n m", n=32),
                vtm_sc[:, h * 128:(h + 1) * 128].rearrange("(n p) m -> p n m", p=128),
                reads=allv, writes=(f"s3vth{p}",))

    LAG = 8
    load_head(0)
    load_head(1)
    for h in range(8):
        p = h % 2
        kq, kk, kv = f"s3qTh{p}", f"s3kTh{p}", f"s3vth{p}"

        def post_bcast1(qb):
            q2 = qb % 2
            kd = f"s3dr{q2}"
            ctx.mm(ps_B[:], ones[:, 0:128], rdrow[q2][:], start=True, stop=True,
                   reads=(kd + "c", "ones_row_bf"), writes=(ps_B.name,))

        def post_ss(qb):
            q2 = qb % 2
            psO = ps_O[q2]
            kd, ko = f"s3dr{q2}", f"s3o{q2}"
            ctx.sca(lambda e, q2=q2, psO=psO: e.copy(obf[q2][:], psO[:]),
                    reads=(psO.name,), writes=(ko,))
            ctx.vec(lambda e, q2=q2: e.tensor_tensor(
                o1[q2][:], obf[q2][:], ps_B[:], ALU.mult),
                reads=(ko, ps_B.name), writes=(ko + "1",))
            ctx.sca(lambda e, q2=q2: e.activation(
                out=o2[q2][:], in_=o1[q2][:], func=AF.Square),
                reads=(ko + "1",), writes=(ko + "2",))
            ctx.mm(psO[0:1, 0:512], onesc[:, 0:1], o2[q2][:], start=True, stop=True,
                   reads=(ko + "2", "ones_col_bf"), writes=(psO.name,))
            ctx.vec(lambda e, q2=q2, psO=psO: e.tensor_scalar(
                rowf[q2][:], psO[0:1, 0:512], 1.0 / 128, EPS_LN, ALU.mult, ALU.add),
                reads=(psO.name, kd + "c"), writes=(kd + "d",))
            ctx.sca(lambda e, q2=q2: e.activation(
                out=rowf[q2][:], in_=rowf[q2][:], func=AF.Ln),
                reads=(kd + "d",), writes=(kd + "d2",))
            ctx.sca(lambda e, q2=q2: e.activation(
                out=rrrow[:], in_=rowf[q2][:], func=AF.Exp, scale=-0.5),
                reads=(kd + "d2", "s3rrtok"), writes=(kd + "e",))

        def post_fin(qb, h=h):
            q2 = qb % 2
            kd, ko = f"s3dr{q2}", f"s3o{q2}"
            ctx.mm(ps_B[:], ones[:, 0:128], rrrow[:], start=True, stop=True,
                   reads=(kd + "e", "ones_row_bf"), writes=(ps_B.name, "s3rrtok"))
            ctx.vec(lambda e, q2=q2: e.tensor_tensor(
                t1[q2][:], o1[q2][:], ps_B[:], ALU.mult),
                reads=(ko + "1", ps_B.name), writes=(ko + "3",))
            ctx.gps(lambda e, q2=q2: e.tensor_scalar(
                onb[q2][:], t1[q2][:], nw[:, 0:1], None, ALU.mult),
                reads=(ko + "3", "s3_nw"), writes=(ko + "4",))
            ctx.dma("sync", onT_sc[h][:, qb * 512:(qb + 1) * 512], onb[q2][:],
                    reads=(ko + "4",), writes=(f"s3onT{h}_{qb}",))

        def emit_block(qb, p=p, kq=kq, kk=kk, kv=kv):
            nkt = 4 * qb + 4
            psO = ps_O[qb % 2]
            psD = ps_D

            def qk(ktb):
                di = ktb - 4 * qb
                c0 = max(0, di) * 128
                sb2 = ps_S[ktb % 2]
                ctx.mm(sb2[:, c0:512], kTh[p][:, ktb * 128:(ktb + 1) * 128],
                       qTh[p][:, qb * 512 + c0:(qb + 1) * 512],
                       start=True, stop=True, reads=(kq, kk), writes=(sb2.name,))
                kpt = f"s3pt{ktb % 10}"
                pts = PT[:, (ktb % 10) * 512:(ktb % 10 + 1) * 512]
                if di >= 0:
                    if c0 > 0:
                        ctx.gps(lambda e, pts=pts, c0=c0: e.memset(
                            pts[:, 0:c0], 0.0), writes=(kpt,))
                    ctx.vec(lambda e, pts=pts, sb2=sb2, c0=c0, di=di:
                            e.tensor_tensor(pts[:, c0:512], sb2[:, c0:512],
                                            Wm[:, 384:896 - di * 128], ALU.mult),
                            reads=(sb2.name, "s3_W"), writes=(kpt,))
                elif ktb % 2 == 0:
                    ctx.sca(lambda e, pts=pts, sb2=sb2: e.copy(pts[:], sb2[:]),
                            reads=(sb2.name,), writes=(kpt,))
                else:
                    ctx.vec(lambda e, pts=pts, sb2=sb2: e.tensor_copy(
                        pts[:], sb2[:]), reads=(sb2.name,), writes=(kpt,))

            def den(ktb):
                kpt = f"s3pt{ktb % 10}"
                pts = PT[:, (ktb % 10) * 512:(ktb % 10 + 1) * 512]
                ctx.mm(psD[0:1, 0:512], onesc[:, 0:1], pts,
                       start=(ktb == 0), stop=(ktb == nkt - 1),
                       reads=(kpt, "ones_col_bf"), writes=(psD.name,))

            def av(ktb):
                kpt = f"s3pt{ktb % 10}"
                pts = PT[:, (ktb % 10) * 512:(ktb % 10 + 1) * 512]
                ctx.mm(psO[:], vth[p][:, ktb * 128:(ktb + 1) * 128], pts,
                       start=(ktb == 0), stop=(ktb == nkt - 1),
                       reads=(kpt, kv), writes=(psO.name,))

            kfin = max(10, nkt - 4)
            for k in range(nkt + LAG):
                if k < nkt:
                    qk(k)
                if k == 5 and qb >= 1:
                    post_ss(qb - 1)
                if 2 <= k < nkt + 2:
                    den(k - 2)
                if k == kfin and qb >= 1:
                    post_fin(qb - 1)
                if k >= LAG:
                    av(k - LAG)
            q2 = qb % 2
            kd = f"s3dr{q2}"
            ctx.vec(lambda e, q2=q2: e.tensor_scalar(
                rowf[q2][:], psD[0:1, 0:512], -1.0, 1.0, ALU.mult, ALU.max),
                reads=(psD.name, kd + "e"), writes=(kd,))
            ctx.vec(lambda e, q2=q2: e.tensor_tensor(
                rowf[q2][:], psD[0:1, 0:512], rowf[q2][:], ALU.max),
                reads=(psD.name, kd), writes=(kd + "b",))
            ctx.sca(lambda e, q2=q2: e.activation(
                out=rowf[q2][:], in_=rowf[q2][:], func=AF.Ln),
                reads=(kd + "b",), writes=(kd + "b2",))
            ctx.sca(lambda e, q2=q2: e.activation(
                out=rdrow[q2][:], in_=rowf[q2][:], func=AF.Exp, scale=-1.0),
                reads=(kd + "b2",), writes=(kd + "c",))

        for qb in range(8):
            if qb >= 1:
                post_bcast1(qb - 1)
            emit_block(qb)
        post_bcast1(7)
        post_ss(7)
        post_fin(7)
        if h + 2 < 8:
            load_head(h + 2)

    for i in range(NT):
        b = i % 2
        kx2 = f"s3x2{b}"
        tsl = slice(i * 128, (i + 1) * 128)
        qb = i // 4
        ctx.dma("sync", x2[b][:], x_ap[tsl, :], writes=(kx2, f"s3xt{b}"))
        ctx.dma("sync", onTt[b][:].rearrange("p (c m) -> p c m", c=8),
                onT_sc[:, :, tsl].rearrange("c p m -> p c m"),
                reads=tuple(f"s3onT{hh}_{qb}" for hh in range(8)),
                writes=(f"s3onTt{b}", f"s3qtm{b}"))
        for blk in range(2):
            ps = ps_S[blk]
            for ch in range(8):
                ctx.mm(ps[:], onTt[b][:, ch * 128:(ch + 1) * 128],
                       woT[:, ch * 1024 + blk * 512: ch * 1024 + blk * 512 + 512],
                       start=(ch == 0), stop=(ch == 7),
                       reads=(f"s3onTt{b}", "s3_woT"), writes=(ps.name,))
            ctx.vec(lambda e, ps=ps, b=b, blk=blk: e.tensor_tensor(
                ysb[b][:, blk * 512:(blk + 1) * 512], ps[:],
                x2[b][:, blk * 512:(blk + 1) * 512], ALU.add),
                reads=(ps.name, kx2), writes=(f"s3y{b}_{blk}", f"s3l{b}xc"))
        ctx.dma("gpsimd", y_ap[tsl, :], ysb[b][:],
                reads=(f"s3y{b}_0", f"s3y{b}_1"))



def w_s3(inputs):
    return {k: inputs["L0_attn_b_" + k] for k in ("Wq", "Wk", "Wv", "Wo", "norm_w")}



EXP_SHIFT = 20.0


def s7_consts():
    kt = np.arange(128)[:, None]
    q = np.arange(128)[None, :]
    return {"s7_triadd": np.where(q >= kt, 0.0, -1e30).astype(np.float32)}


def emit_attn_core(ctx, pref, aps, scale, ym_sc, load_pair_fn):
    onesq = load_const_sb(ctx, "ones_sq_bf", aps["ones_sq_bf"], [128, 128], BF16)
    tri = ctx.s7_tri
    LAG, NPT = 4, 8
    QTp = ctx.sb(pref + "_QTp", [128, T], BF16, n=2)
    KTp = ctx.sb(pref + "_KTp", [128, T], BF16, n=2)
    vA = ctx.sb(pref + "_vA", [128, 32 * 65], BF16, n=2)
    vB = ctx.sb(pref + "_vB", [128, 32 * 65], BF16, n=2)
    PT = ctx.sb(pref + "_PT", [128, NPT * 512], BF16)
    rbrow = ctx.sb(pref + "_rbrow", [128, 512], BF16, n=2)
    rb = ctx.sb(pref + "_rb", [64, 512], BF16, n=2)
    ym = ctx.sb(pref + "_ym", [64, 512], BF16, n=2)
    shift = ctx.sb(pref + "_shift", [128, 2], F32)
    ctx.vec(lambda e: e.memset(shift[:, 0:1], -EXP_SHIFT), writes=(pref + "shift",))
    ctx.vec(lambda e: e.memset(shift[:, 1:2], scale), writes=(pref + "shift",))
    ps_S = (ctx.attn_ps_s[0], ctx.attn_ps_s[1],
            ctx.ps(pref + "_pS2", [128, 512], F32))
    ps_O = ctx.ps(pref + "_pO", [128, 512], F32, n=2)
    ps_B = ctx.ps(pref + "_pB", [128, 512], F32)
    for par in range(2):
        for vv, kv in ((vA[par], f"{pref}vA{par}"), (vB[par], f"{pref}vB{par}")):
            ctx.vec(lambda e, vv=vv: e.memset(
                vv[:].rearrange("p (n m) -> p n m", m=65)[:, :, 64:65], 1.0),
                writes=(kv + "_ones",))

    load_pair_fn(0, QTp[0], KTp[0], vA[0], vB[0], 0, 0)
    load_pair_fn(0, QTp[0], KTp[0], vA[0], vB[0], 0, 1)
    for hp in range(8):
        par = hp % 2
        kQ, kK = f"{pref}QTp{par}", f"{pref}KTp{par}"
        blocks = []
        jobs = []
        for qb in range(8):
            for nm in ("A", "B"):
                j = len(jobs)
                nkt = 4 * qb + 4
                for ktb in range(nkt):
                    blocks.append((j, qb, nm, ktb, ktb == 0, ktb == nkt - 1))
                jobs.append((qb, nm, len(blocks) - 1))
        nblocks = len(blocks)
        end2job = {e: j for j, (_, _, e) in enumerate(jobs)}

        def emit_qk_exp(t):
            j, qb, nm, ktb, _, _ = blocks[t]
            di = ktb - 4 * qb
            c0 = max(0, di) * 128
            base = 0 if nm == "A" else 64
            psl = slice(base, base + 64)
            ps = ps_S[t % 3]
            ctx.mm(ps[:, c0:512], KTp[par][psl, ktb * 128:(ktb + 1) * 128],
                   QTp[par][psl, qb * 512 + c0:(qb + 1) * 512],
                   start=True, stop=True, reads=(kQ, kK), writes=(ps.name,))
            if di >= 0:
                ctx.vec(lambda e, ps=ps, c0=c0: e.tensor_tensor(
                    ps[:, c0:c0 + 128], ps[:, c0:c0 + 128], tri[:], ALU.add),
                    reads=(ps.name, "s7_triadd"), writes=(ps.name,))
            kpt = f"{pref}pt{t % NPT}"
            pt = PT[:, (t % NPT) * 512:(t % NPT + 1) * 512]
            if c0 > 0:
                ctx.vec(lambda e, pt=pt, c0=c0: e.memset(pt[:, 0:c0], 0.0),
                        writes=(kpt,))
            ctx.sca(lambda e, ps=ps, pt=pt, c0=c0: e.activation(
                out=pt[:, c0:512], in_=ps[:, c0:512], func=AF.Exp,
                bias=shift[:, 0:1], scale=shift[:, 1:2]),
                reads=(ps.name, pref + "shift"), writes=(kpt,))

        def emit_av(u):
            j, qb, nm, ktb, st0, st1 = blocks[u]
            vv = vA[par] if nm == "A" else vB[par]
            kv = f"{pref}v{nm}{par}"
            kpt = f"{pref}pt{u % NPT}"
            pt = PT[:, (u % NPT) * 512:(u % NPT + 1) * 512]
            po = ps_O[j % 2]
            ctx.mm(po[0:65, :], vv[:, ktb * 65:(ktb + 1) * 65], pt[:],
                   start=st0, stop=st1,
                   reads=(kpt, kv, kv + "_ones"), writes=(po.name,))

        def post_r(j):
            j2 = j % 2
            po = ps_O[j2]
            krb = f"{pref}rb{j2}"
            ctx.vec(lambda e, j2=j2, po=po: e.reciprocal(
                rbrow[j2][64:65, 0:512], po[64:65, 0:512]),
                reads=(po.name,), writes=(krb + "r",))

        def post_m(j, hp=hp):
            qb, nm, _ = jobs[j]
            j2 = j % 2
            po = ps_O[j2]
            krb, kym = f"{pref}rb{j2}", f"{pref}ym{j2}"
            ctx.mm(ps_B[:], onesq[64:65, 0:128], rbrow[j2][64:65, 0:512],
                   start=True, stop=True,
                   reads=(krb + "r", "ones_sq_bf"), writes=(ps_B.name,))
            ctx.vec(lambda e, j2=j2: e.tensor_copy(rb[j2][:], ps_B[0:64, 0:512]),
                    reads=(ps_B.name,), writes=(krb,))
            ctx.vec(lambda e, j2=j2, po=po: e.tensor_tensor(
                ym[j2][:], po[0:64, 0:512], rb[j2][:], ALU.mult),
                reads=(po.name, krb), writes=(kym,))
            ctx.dma("gpsimd", ym_sc[2 * hp + (0 if nm == "A" else 1), :,
                                    qb * 512:(qb + 1) * 512],
                    ym[j2][:], reads=(kym,), writes=(f"sc_ym{hp}_{qb}{nm}",))

        for t in range(nblocks + LAG + 5):
            if t < nblocks:
                emit_qk_exp(t)
            if hp + 1 < 8 and t in (16, 112):
                load_pair_fn(hp + 1, QTp[1 - par], KTp[1 - par], vA[1 - par],
                             vB[1 - par], 1 - par, 0 if t == 16 else 1)
            u = t - LAG
            if 0 <= u < nblocks:
                emit_av(u)
            e1 = t - LAG - 1
            if 0 <= e1 < nblocks and e1 in end2job:
                post_r(end2job[e1])
            e2 = t - LAG - 4
            if 0 <= e2 < nblocks and e2 in end2job:
                post_m(end2job[e2])


def emit_S7(ctx, aps, x_ap, y_ap, w):
    wqT = load_weight_bf(ctx, "s7_wqT", w["wq"].T)
    wkT = load_weight_bf(ctx, "s7_wkT", w["wk"].T)
    wvT = load_weight_bf(ctx, "s7_wvT", w["wv"].T)
    wpT = load_weight_bf(ctx, "s7_wpT", w["wp"].T)
    bq = load_weight_bf(ctx, "s7_bq", w["bq"][None, :])
    bk = load_weight_bf(ctx, "s7_bk", w["bk"][None, :])
    bv = load_weight_bf(ctx, "s7_bv", w["bv"][None, :])
    bp = load_weight_bf(ctx, "s7_bp", w["bp"][None, :])
    ident = load_const_sb(ctx, "ident_bf", aps["ident_bf"], [128, 128], BF16)
    ones = load_const_sb(ctx, "ones_row_bf", aps["ones_row_bf"], [1, 512], BF16)
    onesc = load_const_sb(ctx, "ones_col_bf", aps["ones_col_bf"], [128, 1], BF16)
    tri = load_const_sb(ctx, "s7_triadd", aps["s7_triadd"], [128, 128], F32)
    ctx.s7_ident, ctx.s7_ones, ctx.s7_onesc, ctx.s7_tri = ident, ones, onesc, tri

    qT_sc = ctx.scratch("s7_qTsc", [8, 128, T], BF16)
    kT_sc = ctx.scratch("s7_kTsc", [8, 128, T], BF16)
    v_sc = ctx.scratch("s7_vsc", [T, D], BF16)
    ym_sc = ctx.scratch("s7_ymsc", [16, 64, T], BF16)

    xt = ctx.sb("s7_xt", [128, 1024], F32, n=2)
    xc = ctx.sb("s7_xc", [128, 1024], F32, n=2)
    sq = ctx.sb("s7_sq", [128, 1024], F32)
    st = ctx.sb("s7_st", [128, 8], F32, n=2)
    u_bf = ctx.sb("s7_u", [128, 1024], BF16, n=2)
    uT = ctx.sb("s7_uT", [128, 1024], BF16, n=2)
    qtm = ctx.sb("s7_qtm", [128, 1024], BF16, n=2)
    ktm = ctx.sb("s7_ktm", [128, 1024], BF16, n=2)
    vtm = ctx.sb("s7_vtm", [128, 1024], BF16, n=2)
    qTt = ctx.sb("s7_qTt", [128, 1024], BF16, n=2)
    kTt = ctx.sb("s7_kTt", [128, 1024], BF16, n=2)
    ymTt = qtm
    ysb = xc
    x2 = xt

    ps_tr = ctx.ps("s7_ptr", [128, 1024], BF16, n=2)
    ps_a = ctx.ps("s7_pa", [128, 512], F32)
    ps_b = ctx.ps("s7_pb", [128, 512], F32)

    for i in range(NT):
        b = i % 2
        kxt = f"s7xt{b}"
        ctx.dma("sync", xt[b][:], x_ap[i * 128:(i + 1) * 128, :], writes=(kxt,))
        kp = f"s7l{b}"
        emit_ln(ctx, kp, kxt, xt[b], xc[b], sq, st[b], u_bf[b],
                uT=uT[b], ps_tr=ps_tr, ident=ident)
        kuT = kp + "uT"

        def proj_tm(wT, wkey, bias, bkey, dest, dkey, b=b, kuT=kuT):
            for blk in range(2):
                ps = ps_a if blk == 0 else ps_b
                for ch in range(8):
                    ctx.mm(ps[:], uT[b][:, ch * 128:(ch + 1) * 128],
                           wT[:, ch * 1024 + blk * 512: ch * 1024 + blk * 512 + 512],
                           start=(ch == 0), stop=False,
                           reads=(kuT, wkey), writes=(ps.name,))
                ctx.mm(ps[:], ones[:, 0:128], bias[:, blk * 512:(blk + 1) * 512],
                       start=False, stop=True,
                       reads=(bkey, "ones_row_bf"), writes=(ps.name,))
                ctx.vec(lambda e, ps=ps, blk=blk, dest=dest: e.tensor_copy(
                    dest[:, blk * 512:(blk + 1) * 512], ps[:]),
                    reads=(ps.name,), writes=(dkey,))

        proj_tm(wqT, "s7_wqT", bq, "s7_bq", qtm[b], f"s7qtm{b}")
        proj_tm(wkT, "s7_wkT", bk, "s7_bk", ktm[b], f"s7ktm{b}")
        proj_tm(wvT, "s7_wvT", bv, "s7_bv", vtm[b], f"s7vtm{b}")
        for src, dest, dkey, skey in ((qtm[b], qTt[b], f"s7qTt{b}", f"s7qtm{b}"),
                                      (ktm[b], kTt[b], f"s7kTt{b}", f"s7ktm{b}")):
            for ch in range(8):
                p = ps_tr[ch % 2]
                ctx.trans(p[:, 0:128], src[:, ch * 128:(ch + 1) * 128], ident[:],
                          reads=(skey, "ident_bf"), writes=(p.name,))
                ctx.vec(lambda e, p=p, ch=ch, dest=dest: e.tensor_copy(
                    dest[:, ch * 128:(ch + 1) * 128], p[:, 0:128]),
                    reads=(p.name,), writes=(dkey,))
        tsl = slice(i * 128, (i + 1) * 128)
        ctx.dma("gpsimd", qT_sc[:, :, tsl].rearrange("c p m -> p c m"),
                qTt[b][:].rearrange("p (c m) -> p c m", c=8), reads=(f"s7qTt{b}",),
                writes=(f"sc7qT{i}",))
        ctx.dma("gpsimd", kT_sc[:, :, tsl].rearrange("c p m -> p c m"),
                kTt[b][:].rearrange("p (c m) -> p c m", c=8), reads=(f"s7kTt{b}",),
                writes=(f"sc7kT{i}",))
        ctx.dma("gpsimd", v_sc[tsl, :], vtm[b][:], reads=(f"s7vtm{b}",),
                writes=(f"sc7v{i}",))

    allq7 = tuple(f"sc7qT{i}" for i in range(NT))
    allk7 = tuple(f"sc7kT{i}" for i in range(NT))
    allv7 = tuple(f"sc7v{i}" for i in range(NT))

    def load_pair(hp, QTp, KTp, vA, vB, par, stage):
        if stage != 0:
            return
        ctx.dma("sync", QTp[:], qT_sc[hp], reads=allq7, writes=(f"s7QTp{par}",))
        ctx.dma("sync", KTp[:], kT_sc[hp], reads=allk7, writes=(f"s7KTp{par}",))
        for vv, h, kv in ((vA, 2 * hp, f"s7vA{par}"), (vB, 2 * hp + 1,
                                                       f"s7vB{par}")):
            ctx.dma("sync", vv[:].rearrange("p (n m) -> p n m", m=65)[:, :, 0:64],
                    v_sc.rearrange("(n p) m -> p n m", p=128)
                    [:, :, h * 64:(h + 1) * 64],
                    reads=allv7, writes=(kv,))

    ctx.attn_ps_s = (ps_a, ps_b)
    emit_attn_core(ctx, "s7", aps, 1.0 / 64.0, ym_sc, load_pair)

    for i in range(NT):
        b = i % 2
        kx2 = f"s7x2{b}"
        tsl = slice(i * 128, (i + 1) * 128)
        ctx.dma("sync", x2[b][:], x_ap[tsl, :], writes=(kx2, f"s7xt{b}"))
        ctx.dma("sync", ymTt[b][:].rearrange("p (c m) -> p c m", c=8),
                ym_sc.rearrange("(c h) p t -> (h p) c t", h=2)[:, :, tsl],
                reads=tuple(f"sc_ym{hp}_{i // 4}{nm}" for hp in range(8)
                            for nm in "AB"),
                writes=(f"s7ymTt{b}", f"s7qtm{b}"))
        for blk in range(2):
            ps = ps_a if blk == 0 else ps_b
            for ch in range(8):
                ctx.mm(ps[:], ymTt[b][:, ch * 128:(ch + 1) * 128],
                       wpT[:, ch * 1024 + blk * 512: ch * 1024 + blk * 512 + 512],
                       start=(ch == 0), stop=False,
                       reads=(f"s7ymTt{b}", "s7_wpT"), writes=(ps.name,))
            ctx.mm(ps[:], ones[:, 0:128], bp[:, blk * 512:(blk + 1) * 512],
                   start=False, stop=True,
                   reads=("s7_bp", "ones_row_bf"), writes=(ps.name,))
            ctx.vec(lambda e, ps=ps, b=b, blk=blk: e.tensor_tensor(
                ysb[b][:, blk * 512:(blk + 1) * 512], ps[:],
                x2[b][:, blk * 512:(blk + 1) * 512], ALU.add),
                reads=(ps.name, kx2), writes=(f"s7y{b}_{blk}", f"s7l{b}xc"))
        ctx.dma("gpsimd", y_ap[tsl, :], ysb[b][:],
                reads=(f"s7y{b}_0", f"s7y{b}_1"))


def w_s7(inputs):
    return {k: inputs["L1_attn_b_" + k] for k in
            ("wq", "bq", "wk", "bk", "wv", "bv", "wp", "bp")}



def emit_S1(ctx, aps, x_ap, y_ap, w):
    wqkv = load_weight_bf(ctx, "s1_wqkv", w["Wqkv"])
    bqkv = load_weight_bf(ctx, "s1_bqkv", w["bqkv"][None, :])
    wproj = load_weight_bf(ctx, "s1_wproj", w["Wproj"])
    bproj = load_weight_bf(ctx, "s1_bproj", w["bproj"][None, :])
    ident = load_const_sb(ctx, "ident_bf", aps["ident_bf"], [128, 128], BF16)
    ones = load_const_sb(ctx, "ones_row_bf", aps["ones_row_bf"], [1, 512], BF16)
    onesc = load_const_sb(ctx, "ones_col_bf", aps["ones_col_bf"], [128, 1], BF16)
    tri = load_const_sb(ctx, "s7_triadd", aps["s7_triadd"], [128, 128], F32)
    ctx.s7_ident, ctx.s7_ones, ctx.s7_onesc, ctx.s7_tri = ident, ones, onesc, tri

    xp_sc = ctx.scratch("s1_xpsc", [T, 3 * D], BF16)
    ym_sc = ctx.scratch("s1_ymsc", [16, 64, T], BF16)

    xt = ctx.sb("s1_xt", [128, 1024], F32, n=2)
    xc = ctx.sb("s1_xc", [128, 1024], F32, n=2)
    sq = ctx.sb("s1_sq", [128, 1024], F32)
    st = ctx.sb("s1_st", [128, 8], F32, n=2)
    u_bf = ctx.sb("s1_u", [128, 1024], BF16, n=2)
    uT = ctx.sb("s1_uT", [128, 1024], BF16, n=2)
    xp = ctx.sb("s1_xp", [128, 3072], BF16, n=2)
    qAB = ctx.sb("s1_qAB", [128, 32 * 128], BF16)
    kAB = ctx.sb("s1_kAB", [128, 32 * 128], BF16)
    ymTt = uT
    ysb = xc
    x2 = xt

    ps_tr = ctx.ps("s1_ptr", [128, 1024], BF16, n=2)
    ps_a = ctx.ps("s1_pa", [128, 512], F32)
    ps_b = ctx.ps("s1_pb", [128, 512], F32)

    for i in range(NT):
        b = i % 2
        kxt = f"s1xt{b}"
        ctx.dma("sync", xt[b][:], x_ap[i * 128:(i + 1) * 128, :], writes=(kxt,))
        kp = f"s1l{b}"
        emit_ln(ctx, kp, kxt, xt[b], xc[b], sq, st[b], u_bf[b],
                uT=uT[b], ps_tr=ps_tr, ident=ident)
        kuT = kp + "uT"
        for blk in range(6):
            ps = ps_a if blk % 2 == 0 else ps_b
            for ch in range(8):
                ctx.mm(ps[:], uT[b][:, ch * 128:(ch + 1) * 128],
                       wqkv[:, ch * 3072 + blk * 512: ch * 3072 + blk * 512 + 512],
                       start=(ch == 0), stop=False,
                       reads=(kuT, "s1_wqkv"), writes=(ps.name,))
            ctx.mm(ps[:], ones[:, 0:128], bqkv[:, blk * 512:(blk + 1) * 512],
                   start=False, stop=True,
                   reads=("s1_bqkv", "ones_row_bf"), writes=(ps.name,))
            ctx.vec(lambda e, ps=ps, blk=blk, b=b: e.tensor_copy(
                xp[b][:, blk * 512:(blk + 1) * 512], ps[:]),
                reads=(ps.name,), writes=(f"s1xp{b}",))
        ctx.dma("gpsimd", xp_sc[i * 128:(i + 1) * 128, :], xp[b][:],
                reads=(f"s1xp{b}",), writes=(f"sc1xp{i}",))

    xp_r = xp_sc.rearrange("(h kb phi) (s plo d) -> h s phi plo kb d",
                           h=16, kb=32, phi=8, s=3, plo=16, d=64)

    allxp = tuple(f"sc1xp{i}" for i in range(NT))

    def load_pair(hp, QTp, KTp, vA, vB, par, stage):
        if stage == 0:
            for sect, buf, bk in ((0, qAB, "s1qAB"), (1, kAB, "s1kAB")):
                for half, h in ((0, 2 * hp), (1, 2 * hp + 1)):
                    bview = buf[:].rearrange("p (n m) -> p n m", m=128)
                    for phi in range(8):
                        ctx.dma("sync",
                                bview[phi * 16:(phi + 1) * 16, :,
                                      half * 64:(half + 1) * 64],
                                xp_r[h, sect, phi], reads=allxp, writes=(bk,))
            for vv, h, kv in ((vA, 2 * hp, f"s1vA{par}"), (vB, 2 * hp + 1,
                                                           f"s1vB{par}")):
                vview = vv[:].rearrange("p (n m) -> p n m", m=65)
                for phi in range(8):
                    ctx.dma("sync", vview[phi * 16:(phi + 1) * 16, :, 0:64],
                            xp_r[h, 2, phi], reads=allxp, writes=(kv,))
            return
        for sect, buf, bk in ((0, qAB, "s1qAB"), (1, kAB, "s1kAB")):
            dst = QTp if sect == 0 else KTp
            dk = f"s1QTp{par}" if sect == 0 else f"s1KTp{par}"
            for kb in range(32):
                p = ps_tr[kb % 2]
                ctx.trans(p[:, 0:128], buf[:, kb * 128:(kb + 1) * 128], ident[:],
                          reads=(bk, "ident_bf"), writes=(p.name,))
                ctx.vec(lambda e, p=p, kb=kb, dst=dst: e.tensor_copy(
                    dst[:, kb * 128:(kb + 1) * 128], p[:, 0:128]),
                    reads=(p.name,), writes=(dk,))

    ctx.attn_ps_s = (ps_a, ps_b)
    emit_attn_core(ctx, "s1", aps, 1.0 / 8.0, ym_sc, load_pair)

    for i in range(NT):
        b = i % 2
        kx2 = f"s1x2{b}"
        tsl = slice(i * 128, (i + 1) * 128)
        ctx.dma("sync", x2[b][:], x_ap[tsl, :], writes=(kx2, f"s1xt{b}"))
        ctx.dma("sync", ymTt[b][:].rearrange("p (c m) -> p c m", c=8),
                ym_sc.rearrange("(c h) p t -> (h p) c t", h=2)[:, :, tsl],
                reads=tuple(f"sc_ym{hp}_{i // 4}{nm}" for hp in range(8)
                            for nm in "AB"),
                writes=(f"s1ymTt{b}", f"s1l{b}uT"))
        for blk in range(2):
            ps = ps_a if blk == 0 else ps_b
            for ch in range(8):
                ctx.mm(ps[:], ymTt[b][:, ch * 128:(ch + 1) * 128],
                       wproj[:, ch * 1024 + blk * 512: ch * 1024 + blk * 512 + 512],
                       start=(ch == 0), stop=False,
                       reads=(f"s1ymTt{b}", "s1_wproj"), writes=(ps.name,))
            ctx.mm(ps[:], ones[:, 0:128], bproj[:, blk * 512:(blk + 1) * 512],
                   start=False, stop=True,
                   reads=("s1_bproj", "ones_row_bf"), writes=(ps.name,))
            ctx.vec(lambda e, ps=ps, b=b, blk=blk: e.tensor_tensor(
                ysb[b][:, blk * 512:(blk + 1) * 512], ps[:],
                x2[b][:, blk * 512:(blk + 1) * 512], ALU.add),
                reads=(ps.name, kx2), writes=(f"s1y{b}_{blk}", f"s1l{b}xc"))
        ctx.dma("gpsimd", y_ap[tsl, :], ysb[b][:],
                reads=(f"s1y{b}_0", f"s1y{b}_1"))


def w_s1(inputs):
    return {k: inputs["L0_attn_a_" + k] for k in ("Wqkv", "bqkv", "Wproj", "bproj")}



def build_program(name, emitters, const_fns, weights):
    ctx = Ctx(name)
    consts = base_consts()
    for f in const_fns:
        consts.update(f())
    x_ap = ctx.ext_in_percore("x", [T, D])
    y_ap = ctx.ext_out("y", [T, D])
    with ctx.stack:
        aps = declare_consts(ctx, consts)
        block = ctx.stack.enter_context(ctx.nc.Block())
        for emit_fn, w in emitters:
            emit_fn(ctx, aps, x_ap, y_ap, w)
        ctx.sch.finish(block, ctx.stack)
    return ctx


def run_program(ctx, x_all):
    in_maps = []
    for i in range(N_CORES):
        m = dict(ctx.inputs)
        m["x"] = np.ascontiguousarray(x_all[i])
        in_maps.append(m)
    res = run_bass_kernel_spmd(ctx.nc, in_maps, core_ids=list(range(N_CORES)))
    return np.stack([r["y"] for r in res.results])


def w_s5(inputs):
    return {k: inputs["L1_attn_a_" + k] for k in
            ("Wk", "bk", "Wv", "bv", "Wq", "bq", "Wo", "bo")}


SEQ = (("s1", emit_S1, w_s1), ("s2", emit_S2, w_s2), ("s3", emit_S3, w_s3),
       ("s2", emit_S2, w_s2), ("s5", emit_S5, w_s5), ("s6", emit_S6, w_s6),
       ("s7", emit_S7, w_s7), ("s6", emit_S6, w_s6))


def build_fused(inputs):
    ctx = Ctx("mk", fused=True)
    consts = base_consts()
    for f in (s5_consts, s6_consts, s2_consts, s3_consts, s7_consts):
        consts.update(f())
    x_ap = ctx.ext_in_percore("x", [T, D])
    y_ap = ctx.ext_out("y", [T, D])
    t0_ap = ctx.scratch("chain0", [T, D], F32)
    t1_ap = ctx.scratch("chain1", [T, D], F32)
    with ctx.stack:
        ctx.init_fused()
        aps = declare_consts(ctx, consts)
        block = ctx.stack.enter_context(ctx.nc.Block())
        n = len(SEQ)
        for i, (nm, emit_fn, wf) in enumerate(SEQ):
            src_ap = x_ap if i == 0 else (t0_ap if i % 2 == 1 else t1_ap)
            dst_ap = y_ap if i == n - 1 else (t0_ap if i % 2 == 0 else t1_ap)
            emit_fn(ctx, aps, src_ap, dst_ap, wf(inputs))
            if i < n - 1:
                ctx.new_sublayer()
        ctx.sch.finish(block, ctx.stack)
    return ctx


def kernel(**inputs):
    x = inputs["L0_attn_a_x"]
    ctx = build_fused(inputs)
    return run_program(ctx, x)


# version 24
# speedup vs baseline: 1.1352x; 1.1352x over previous
import math
from contextlib import ExitStack

import numpy as np
import ml_dtypes

import concourse.bass as bass
import concourse.mybir as mybir
from concourse.bass_utils import run_bass_kernel_spmd

F32 = mybir.dt.float32
BF16 = mybir.dt.bfloat16
AF = mybir.ActivationFunctionType
ALU = mybir.AluOpType
AX = mybir.AxisListType
bf16 = ml_dtypes.bfloat16

N_CORES = 8
T = 4096
D = 1024
NT = T // 128
EPS_LN = 1e-5

ENGINES = ("tensor", "vector", "scalar", "gpsimd", "sync")


class Op:
    __slots__ = ("eng", "fn", "gid", "exec_idx", "dma_idx", "dma", "deps", "waits")


class Sched:

    def __init__(self, nc):
        self.nc = nc
        self.ops = []
        self.per_eng = {e: [] for e in ENGINES}
        self.n_exec = {e: 0 for e in ENGINES}
        self.n_dma = {e: 0 for e in ENGINES}
        self.last = {}
        self.last_dma_slot = {e: {} for e in ENGINES}
        self.last_exec_op = {e: None for e in ENGINES}
        self.pending_barrier = None
        self.barrier_armed = set()

    def barrier(self):
        deps = set()
        for e in ENGINES:
            if self.last_exec_op[e] is not None:
                deps.add(self.last_exec_op[e])
            deps.update(self.last_dma_slot[e].values())
        self.pending_barrier = deps
        self.barrier_armed = set(ENGINES)

    def op(self, eng, fn, reads=(), writes=(), dma=False):
        o = Op()
        o.eng = eng
        o.fn = fn
        o.dma = dma
        o.gid = len(self.ops)
        deps = set()
        if self.pending_barrier is not None and eng in self.barrier_armed:
            deps |= self.pending_barrier
            self.barrier_armed.discard(eng)
            if not self.barrier_armed:
                self.pending_barrier = None
        for k in reads:
            e = self.last.get(k)
            if e and e["w"] is not None:
                deps.add(e["w"])
        for k in writes:
            e = self.last.get(k)
            if e:
                if e["w"] is not None:
                    deps.add(e["w"])
                deps.update(e["r"])
        o.deps = deps
        self.ops.append(o)
        self.per_eng[eng].append(o)
        if dma:
            o.dma_idx = self.n_dma[eng]
            self.n_dma[eng] += 1
            o.exec_idx = None
            self.last_dma_slot[eng][o.dma_idx % self.R_SLOTS] = o
        else:
            o.exec_idx = self.n_exec[eng]
            self.n_exec[eng] += 1
            o.dma_idx = None
            self.last_exec_op[eng] = o
        for k in reads:
            self.last.setdefault(k, {"w": None, "r": []})["r"].append(o)
        for k in writes:
            self.last[k] = {"w": o, "r": []}
        return o

    R_SLOTS = 8

    def finish(self, block, stack):
        R = self.R_SLOTS
        exec_sem = {}
        dma_sem = {}
        for e in ENGINES:
            if self.n_exec[e]:
                exec_sem[e] = stack.enter_context(self.nc.semaphore("sx_" + e))
            if self.n_dma[e]:
                dma_sem[e] = [stack.enter_context(self.nc.semaphore(f"sd_{e}{i}"))
                              for i in range(R)]
        observed = {e: {} for e in ENGINES}
        for o in self.ops:
            o.waits = []
            obs = observed[o.eng]
            if o.dma and o.dma_idx >= R:
                j = o.dma_idx
                key = ("d", o.eng, j % R)
                need = 16 * (j // R)
                if obs.get(key, 0) < need:
                    obs[key] = need
                    o.waits.append((dma_sem[o.eng][j % R], need))
            for d in sorted(o.deps, key=lambda d: d.gid):
                if d.dma:
                    j = d.dma_idx
                    key = ("d", d.eng, j % R)
                    need = 16 * (j // R + 1)
                    sem = dma_sem[d.eng][j % R]
                else:
                    if d.eng == o.eng == "tensor":
                        continue
                    key = ("x", d.eng)
                    need = d.exec_idx + 1
                    sem = exec_sem[d.eng]
                if obs.get(key, 0) >= need:
                    continue
                obs[key] = need
                o.waits.append((sem, need))

        stack.enter_context(self.nc.allow_low_precision(
            "bf16 operands with fp32 accumulation per problem tolerance"))

        def make_body(name):
            def body(eng):
                for o in self.per_eng[name]:
                    for sem, val in o.waits:
                        eng.wait_ge(sem, val)
                    inst = o.fn(eng)
                    if o.dma:
                        inst.then_inc(dma_sem[name][o.dma_idx % R], 16)
                    else:
                        inst.then_inc(exec_sem[name], 1)
                if name == "gpsimd" and self.n_dma["gpsimd"]:
                    n = self.n_dma["gpsimd"]
                    for i in range(min(R, n)):
                        eng.wait_ge(dma_sem["gpsimd"][i], 16 * ((n - 1 - i) // R + 1))
            return body

        for name in ENGINES:
            if self.per_eng[name]:
                getattr(block, name)(make_body(name))


class Ctx:

    ARENA_BYTES = 194 * 1024

    def __init__(self, name, fused=False):
        self.nc = bass.Bass("TRN2", target_bir_lowering=False, debug=False)
        self.sch = Sched(self.nc)
        self.stack = ExitStack()
        self.inputs = {}
        self.name = name
        self.fused = fused
        self.inv = 0
        self.ext_cache = {}
        self.scratch_cache = {}
        self.const_cache = {}
        self.arena_base = None
        self.arena_ptr = 0
        self.arena_peak = 0
        self.pool_f = []
        self.pool_b = []
        self.pool_f_used = 0
        self.pool_b_used = 0

    def init_fused(self):
        r = self.nc.bump_sbuf(self.ARENA_BYTES, align_bytes=64)
        assert r is not None, "arena reservation failed"
        self.arena_base = r[0]
        self.pool_f = [self.stack.enter_context(
            self.nc.psum_tensor(f"poolf{i}", [128, 512], F32)) for i in range(6)]
        self.pool_b = [self.stack.enter_context(
            self.nc.psum_tensor(f"poolb{i}", [128, 1024], BF16)) for i in range(2)]

    def new_sublayer(self):
        self.sch.barrier()
        self.arena_peak = max(self.arena_peak, self.arena_ptr)
        self.arena_ptr = 0
        self.pool_f_used = 0
        self.pool_b_used = 0
        self.inv += 1

    def ext_in(self, name, arr):
        if name in self.ext_cache:
            return self.ext_cache[name]
        dt = {np.dtype(np.float32): F32, np.dtype(bf16): BF16}[arr.dtype]
        ap = self.nc.dram_tensor(name, list(arr.shape), dt, kind="ExternalInput").ap()
        self.inputs[name] = arr
        self.ext_cache[name] = ap
        return ap

    def ext_in_percore(self, name, shape, dtyp=F32):
        return self.nc.dram_tensor(name, list(shape), dtyp, kind="ExternalInput").ap()

    def ext_out(self, name, shape, dtyp=F32):
        return self.nc.dram_tensor(name, list(shape), dtyp, kind="ExternalOutput").ap()

    def scratch(self, name, shape, dtyp):
        if name in self.scratch_cache:
            return self.scratch_cache[name]
        ap = self.nc.dram_tensor(name, list(shape), dtyp, kind="Internal").ap()
        self.scratch_cache[name] = ap
        return ap

    def sb(self, name, shape, dtyp, n=1, persist=False):
        if n != 1:
            return [self.sb(f"{name}{i}", shape, dtyp, persist=persist)
                    for i in range(n)]
        if not self.fused or persist:
            return self.stack.enter_context(self.nc.sbuf_tensor(name, shape, dtyp))
        nbytes = shape[1] * mybir.dt.size(dtyp)
        nbytes = (nbytes + 63) & ~63
        assert self.arena_ptr + nbytes <= self.ARENA_BYTES, \
            f"arena overflow at {name}: {self.arena_ptr + nbytes}"
        h = self.nc.alloc_sbuf_tensor_at(f"{name}_v{self.inv}", shape, dtyp,
                                         offset=self.arena_base + self.arena_ptr)
        self.arena_ptr += nbytes
        return h

    def ps(self, name, shape, dtyp=F32, n=1):
        if n != 1:
            return [self.ps(f"{name}{i}", shape, dtyp) for i in range(n)]
        if not self.fused:
            return self.stack.enter_context(self.nc.psum_tensor(name, shape, dtyp))
        if dtyp == F32:
            assert list(shape) == [128, 512], f"psum pool shape {shape}"
            t = self.pool_f[self.pool_f_used]
            self.pool_f_used += 1
        else:
            assert list(shape) == [128, 1024], f"psum pool shape {shape}"
            t = self.pool_b[self.pool_b_used]
            self.pool_b_used += 1
        return t

    def dma(self, queue, out, in_, reads=(), writes=()):
        self.sch.op(queue, lambda e, o=out, i=in_: e.dma_start(out=o, in_=i),
                    reads=reads, writes=writes, dma=True)

    def mm(self, out, lhsT, rhs, start, stop, reads=(), writes=()):
        self.sch.op("tensor",
                    lambda e, o=out, l=lhsT, r=rhs, s0=start, s1=stop:
                        e.matmul(o, l, r, start=s0, stop=s1),
                    reads=reads, writes=writes)

    def trans(self, out, in_, ident, reads=(), writes=()):
        self.sch.op("tensor",
                    lambda e, o=out, i=in_, d=ident: e.transpose(o, i, d),
                    reads=reads, writes=writes)

    def vec(self, fn, reads=(), writes=()):
        self.sch.op("vector", fn, reads=reads, writes=writes)

    def sca(self, fn, reads=(), writes=()):
        self.sch.op("scalar", fn, reads=reads, writes=writes)

    def gps(self, fn, reads=(), writes=()):
        self.sch.op("gpsimd", fn, reads=reads, writes=writes)



def base_consts():
    return {
        "ident_bf": np.eye(128, dtype=bf16),
        "ones_row_bf": np.ones((1, 512), dtype=bf16),
        "ones_sq_bf": np.ones((128, 128), dtype=bf16),
    }


def declare_consts(ctx, consts):
    return {k: ctx.ext_in(k, v) for k, v in consts.items()}


def load_const_sb(ctx, name, ap, shape, dtyp):
    if name in ctx.const_cache:
        return ctx.const_cache[name]
    sbt = ctx.sb(name + "_sb", shape, dtyp, persist=True)
    ctx.dma("sync", sbt[:], ap, writes=(name,))
    ctx.const_cache[name] = sbt
    return sbt


def load_weight_bf(ctx, name, host_arr):
    arr = np.ascontiguousarray(host_arr, dtype=np.float32)
    ap = ctx.ext_in(name, arr)
    R, C = arr.shape
    if R % 128 == 0:
        nch = R // 128
        sbt = ctx.sb(name + "_sb", [128, nch * C], BF16)
        in_ = ap.rearrange("(c p) m -> p c m", p=128)
        out = sbt[:].rearrange("p (c m) -> p c m", c=nch)
        ctx.dma("gpsimd", out, in_, writes=(name,))
    else:
        assert R < 128
        sbt = ctx.sb(name + "_sb", [R, C], BF16)
        ctx.dma("gpsimd", sbt[:], ap, writes=(name,))
    return sbt



def emit_ln(ctx, kp, kx, xt, xc, sq, st, u_bf, uT=None, ps_tr=None, ident=None):
    ctx.sca(lambda e: e.activation(out=sq[:], in_=xt[:], func=AF.Copy,
                                   accum_out=st[:, 0:1]),
            reads=(kx,), writes=(kp + "sq", kp + "st0"))
    ctx.sca(lambda e: e.activation(out=sq[:], in_=xt[:], func=AF.Square,
                                   accum_out=st[:, 1:2]),
            reads=(kx, kp + "st0"), writes=(kp + "sq", kp + "st1"))
    ctx.vec(lambda e: e.tensor_scalar(st[:, 2:3], st[:, 0:1], 1.0 / D, None,
                                      ALU.mult),
            reads=(kp + "st0",), writes=(kp + "st2",))
    ctx.vec(lambda e: e.tensor_tensor(st[:, 4:5], st[:, 2:3], st[:, 2:3],
                                      ALU.mult),
            reads=(kp + "st2",), writes=(kp + "st4",))
    ctx.vec(lambda e: e.scalar_tensor_tensor(st[:, 3:4], st[:, 1:2], 1.0 / D,
                                             st[:, 4:5], ALU.mult, ALU.subtract),
            reads=(kp + "st1", kp + "st4"), writes=(kp + "st3",))
    ctx.vec(lambda e: e.tensor_scalar(st[:, 3:4], st[:, 3:4], 1.0, EPS_LN,
                                      ALU.mult, ALU.add),
            reads=(kp + "st3",), writes=(kp + "st3b",))
    ctx.sca(lambda e: e.activation(out=st[:, 5:6], in_=st[:, 3:4], func=AF.Sqrt),
            reads=(kp + "st3b",), writes=(kp + "st5",))
    ctx.vec(lambda e: e.reciprocal(st[:, 6:7], st[:, 5:6]),
            reads=(kp + "st5",), writes=(kp + "st6",))
    ctx.vec(lambda e: e.tensor_tensor(st[:, 7:8], st[:, 2:3], st[:, 6:7],
                                      ALU.mult),
            reads=(kp + "st2", kp + "st6"), writes=(kp + "st7",))
    ctx.vec(lambda e: e.tensor_scalar(u_bf[:], xt[:], st[:, 6:7], st[:, 7:8],
                                      ALU.mult, ALU.subtract),
            reads=(kx, kp + "st6", kp + "st7"), writes=(kp + "u",))
    if uT is not None:
        for ch in range(8):
            p = ps_tr[ch % 2]
            pk = f"pstr{ch % 2}"
            ctx.trans(p[:, 0:128], u_bf[:, ch * 128:(ch + 1) * 128], ident[:],
                      reads=(kp + "u", "ident_bf"), writes=(pk,))
            ctx.vec(lambda e, p=p, ch=ch: e.tensor_copy(
                uT[:, ch * 128:(ch + 1) * 128], p[:, 0:128]),
                reads=(pk,), writes=(kp + "uT",))



DECAY = 0.99
LR = 0.01


def s5_consts():
    lam = DECAY
    j = np.arange(512)
    c = {}
    c["s5_colmap"] = np.tile((lam ** j).astype(bf16)[None, :], (128, 1))
    c["s5_rowmap"] = np.tile((lam ** (-(j % 128).astype(np.float64))
                              ).astype(bf16)[None, :], (128, 1))
    return c


def emit_S5(ctx, aps, x_ap, y_ap, w):
    wqT = load_weight_bf(ctx, "s5_wqT", w["Wq"].T)
    wkT = load_weight_bf(ctx, "s5_wkT", w["Wk"].T)
    wvT = load_weight_bf(ctx, "s5_wvT", w["Wv"].T)
    woT = load_weight_bf(ctx, "s5_woT", w["Wo"].T)
    bq = load_weight_bf(ctx, "s5_bq", w["bq"][None, :])
    bk = load_weight_bf(ctx, "s5_bk", w["bk"][None, :])
    bv = load_weight_bf(ctx, "s5_bv", w["bv"][None, :])
    bo = load_weight_bf(ctx, "s5_bo", w["bo"][None, :])
    ident = load_const_sb(ctx, "ident_bf", aps["ident_bf"], [128, 128], BF16)
    ones = load_const_sb(ctx, "ones_row_bf", aps["ones_row_bf"], [1, 512], BF16)
    Wm = load_const_sb(ctx, "s3_W", aps["s3_W"], [128, 896], F32)
    colmap = load_const_sb(ctx, "s5_colmap", aps["s5_colmap"], [128, 512], BF16)
    rowmap = load_const_sb(ctx, "s5_rowmap", aps["s5_rowmap"], [128, 512], BF16)

    xt = ctx.sb("s5_xt", [128, 1024], F32, n=4)
    xc = ctx.sb("s5_xc", [128, 1024], F32, n=4)
    sq = ctx.sb("s5_sq", [128, 1024], F32, n=2)
    st = ctx.sb("s5_st", [128, 8], F32, n=4)
    u_bf = ctx.sb("s5_u", [128, 1024], BF16, n=4)
    uT = ctx.sb("s5_uT", [128, 1024], BF16, n=4)
    qf = ctx.sb("s5_qf", [128, T], BF16)
    kf = ctx.sb("s5_kf", [128, T], BF16)
    vtm = ctx.sb("s5_vtm", [128, T], BF16)
    PT = ctx.sb("s5_PT", [128, 32 * 512], BF16)
    otm = ctx.sb("s5_otT", [128, T], BF16)
    ysb = xc
    x2 = xt

    ps_tr = ctx.ps("s5_ptr", [128, 1024], BF16, n=2)
    ps_a = ctx.ps("s5_pa", [128, 512], F32)
    ps_b = ctx.ps("s5_pb", [128, 512], F32)
    ps_O = ctx.ps("s5_pO", [128, 512], F32, n=2)

    rot4 = None
    for i in range(NT):
        b = i % 4
        kxt = f"s5xt{b}"
        ctx.dma("sync", xt[b][:], x_ap[i * 128:(i + 1) * 128, :], writes=(kxt,))
        kp = f"s5l{b}"
        emit_ln(ctx, kp, kxt, xt[b], xc[b], sq[b % 2], st[b], u_bf[b],
                uT=uT[b], ps_tr=ps_tr, ident=ident)
        kuT = kp + "uT"
        rot4 = (ps_a, ps_b, ps_O[0], ps_O[1])

        def proj(ps, wT, wkey, bias, bkey, dest, dkey, fm, i=i, kuT=kuT, b=b):
            pk = ps.name
            for ch in range(8):
                if fm:
                    lhsT = wT[:, ch * 128:(ch + 1) * 128]
                    rhs = uT[b][:, ch * 128:(ch + 1) * 128]
                else:
                    lhsT = uT[b][:, ch * 128:(ch + 1) * 128]
                    rhs = wT[:, ch * 128:(ch + 1) * 128]
                ctx.mm(ps[:, 0:128], lhsT, rhs, start=(ch == 0), stop=False,
                       reads=(kuT, wkey), writes=(pk,))
            if fm:
                ctx.mm(ps[:, 0:128], bias[:], ones[:, 0:128], start=False, stop=True,
                       reads=(bkey, "ones_row_bf"), writes=(pk,))
            else:
                ctx.mm(ps[:, 0:128], ones[:, 0:128], bias[:], start=False, stop=True,
                       reads=(bkey, "ones_row_bf"), writes=(pk,))
            ctx.vec(lambda e, ps=ps, i=i, dest=dest: e.tensor_copy(
                dest[:, i * 128:(i + 1) * 128], ps[:, 0:128]),
                reads=(pk,), writes=(f"{dkey}{i}",))

        proj(rot4[(3 * i) % 4], wqT, "s5_wqT", bq, "s5_bq", qf, "s5qf", fm=True)
        proj(rot4[(3 * i + 1) % 4], wkT, "s5_wkT", bk, "s5_bk", kf, "s5kf",
             fm=True)
        proj(rot4[(3 * i + 2) % 4], wvT, "s5_wvT", bv, "s5_bv", vtm, "s5vtm",
             fm=False)

    for s in range(8):
        sl = slice(s * 512, (s + 1) * 512)
        tk = tuple(f"s5qf{i}" for i in range(4 * s, 4 * s + 4))
        ctx.vec(lambda e, sl=sl: e.tensor_tensor(qf[:, sl], qf[:, sl],
                                                 colmap[:], ALU.mult),
                reads=tk + ("s5_colmap",), writes=(f"s5qs{s}",))
        tk = tuple(f"s5kf{i}" for i in range(4 * s, 4 * s + 4))
        ctx.gps(lambda e, sl=sl: e.tensor_tensor(kf[:, sl], kf[:, sl],
                                                 rowmap[:], ALU.mult),
                reads=tk + ("s5_rowmap",), writes=(f"s5ks{s}",))

    allq = tuple(f"s5qs{s}" for s in range(8))
    allk = tuple(f"s5ks{s}" for s in range(8))
    lam = DECAY
    for qb in range(8):
        nkt = 4 * qb + 4
        psO = ps_O[qb % 2]
        for ktb in range(nkt):
            di = ktb - 4 * qb
            c0 = max(0, di) * 128
            bs = float(LR * lam ** (qb * 512 - ktb * 128 - 1))
            sb2 = ps_a if ktb % 2 == 0 else ps_b
            ctx.mm(sb2[:, c0:512], kf[:, ktb * 128:(ktb + 1) * 128],
                   qf[:, qb * 512 + c0:(qb + 1) * 512],
                   start=True, stop=True, reads=(allq[qb], allk[ktb // 4]),
                   writes=(sb2.name,))
            kpt = f"s5pt{ktb}"
            pts = PT[:, ktb * 512:(ktb + 1) * 512]
            if di >= 0:
                if c0 > 0:
                    ctx.gps(lambda e, pts=pts, c0=c0: e.memset(pts[:, 0:c0], 0.0),
                            writes=(kpt,))
                ctx.vec(lambda e, pts=pts, sb2=sb2, c0=c0, di=di, bs=bs:
                        e.scalar_tensor_tensor(
                            pts[:, c0:512], sb2[:, c0:512], bs,
                            Wm[:, 383:895 - di * 128], ALU.mult, ALU.mult),
                        reads=(sb2.name, "s3_W"), writes=(kpt,))
            elif ktb % 2 == 0:
                ctx.sca(lambda e, pts=pts, sb2=sb2, bs=bs: e.activation(
                    out=pts[:], in_=sb2[:], func=AF.Copy, scale=bs),
                    reads=(sb2.name,), writes=(kpt,))
            else:
                ctx.vec(lambda e, pts=pts, sb2=sb2, bs=bs: e.tensor_scalar(
                    pts[:], sb2[:], bs, None, ALU.mult),
                    reads=(sb2.name,), writes=(kpt,))
        for ktb in range(nkt):
            kpt = f"s5pt{ktb}"
            pts = PT[:, ktb * 512:(ktb + 1) * 512]
            ctx.mm(psO[:], vtm[:, ktb * 128:(ktb + 1) * 128], pts,
                   start=(ktb == 0), stop=(ktb == nkt - 1),
                   reads=(kpt, f"s5vtm{ktb}"), writes=(psO.name,))
        ctx.vec(lambda e, qb=qb, psO=psO: e.tensor_copy(
            otm[:, qb * 512:(qb + 1) * 512], psO[:]),
            reads=(psO.name,), writes=(f"s5otm{qb}",))

    rot4 = (ps_a, ps_b, ps_O[0], ps_O[1])
    for i in range(NT):
        b = i % 4
        kx2 = f"s5x2{b}"
        ctx.dma("sync", x2[b][:], x_ap[i * 128:(i + 1) * 128, :],
                writes=(kx2, f"s5xt{b}"))
        for blk in range(2):
            nsl = slice(blk * 512, (blk + 1) * 512)
            ps = rot4[(2 * i + blk) % 4]
            pk = ps.name
            ctx.mm(ps[:], otm[:, i * 128:(i + 1) * 128], woT[:, nsl],
                   start=True, stop=False,
                   reads=(f"s5otm{i // 4}", "s5_woT"), writes=(pk,))
            ctx.mm(ps[:], ones[:, 0:128], bo[:, nsl], start=False, stop=True,
                   reads=("s5_bo", "ones_row_bf"), writes=(pk,))
            ctx.vec(lambda e, ps=ps, b=b, nsl=nsl: e.tensor_tensor(
                ysb[b][:, nsl], ps[:], x2[b][:, nsl], ALU.add),
                reads=(pk, kx2), writes=(f"s5y{b}_{blk}", f"s5l{b}xc"))
        ctx.dma("gpsimd", y_ap[i * 128:(i + 1) * 128, :], ysb[b][:],
                reads=(f"s5y{b}_0", f"s5y{b}_1"))



def s6_consts():
    sel = np.zeros((8, 8 * 128), dtype=bf16)
    for e in range(8):
        sel[e, e * 128:(e + 1) * 128] = 1.0
    return {"s6_sel": sel}


def emit_S6(ctx, aps, x_ap, y_ap, w):
    E, H = 8, 1408
    NH = H // 128
    w1_ap = ctx.ext_in("s6_w1", np.ascontiguousarray(w["w1"], np.float32))
    w2_ap = ctx.ext_in("s6_w2", np.ascontiguousarray(w["w2"], np.float32))
    w3_ap = ctx.ext_in("s6_w3", np.ascontiguousarray(w["w3"], np.float32))
    w1bf = ctx.scratch("s6_w1bf", [E, 1024, H], BF16)
    w2bf = ctx.scratch("s6_w2bf", [E, H, 1024], BF16)
    w3bf = ctx.scratch("s6_w3bf", [E, 1024, H], BF16)
    if not hasattr(ctx, "s6_cast_done"):
        ctx.s6_cast_done = True
        for e_i in range(E):
            ctx.dma("gpsimd", w1bf[e_i], w1_ap[e_i], writes=(f"s6c1{e_i}",))
            ctx.dma("gpsimd", w3bf[e_i], w3_ap[e_i], writes=(f"s6c3{e_i}",))
            ctx.dma("gpsimd", w2bf[e_i], w2_ap[e_i], writes=(f"s6c2{e_i}",))
    wrT = load_weight_bf(ctx, "s6_wrT", w["w_router"].T)
    ident = load_const_sb(ctx, "ident_bf", aps["ident_bf"], [128, 128], BF16)
    sel8 = load_const_sb(ctx, "s6_sel", aps["s6_sel"], [8, 8 * 128], BF16)

    xt = ctx.sb("s6_xt", [128, 1024], F32, n=2)
    xc = ctx.sb("s6_xc", [128, 1024], F32, n=2)
    sq = ctx.sb("s6_sq", [128, 1024], F32)
    st = ctx.sb("s6_st", [128, 8], F32, n=2)
    u_bf = ctx.sb("s6_u", [128, 1024], BF16, n=2)
    uT = ctx.sb("s6_uT", [128, 8 * 2048], BF16)
    yac = ctx.sb("s6_yac", [128, 8 * 2048], BF16)
    w1s = ctx.sb("s6_w1s", [128, 8 * H], BF16)
    w3s = ctx.sb("s6_w3s", [128, 8 * H], BF16)
    w2s = ctx.sb("s6_w2s", [128, NH * 1024], BF16)
    h1s = ctx.sb("s6_h1s", [128, NH * 512], BF16, n=2)
    gbf = ctx.sb("s6_gbf", [128, 512], BF16)
    tmp = ctx.sb("s6_tmp", [128, 512], BF16, n=2)
    cb = ctx.sb("s6_cb", [128, 512], BF16, n=2)
    prob = ctx.sb("s6_prob", [128, 8], F32)
    p2 = ctx.sb("s6_p2", [128, 8], F32)
    msk = ctx.sb("s6_msk", [128, 8], F32)
    comb = ctx.sb("s6_comb", [128, 8], BF16)
    combT = ctx.sb("s6_combT", [8, 2048], BF16)
    rst = ctx.sb("s6_rst", [128, 4], F32)
    ysb = xc
    x2 = xt

    ps_tr = ctx.ps("s6_ptr", [128, 1024], BF16, n=2)
    ps_g = ctx.ps("s6_pg", [128, 512], F32, n=2)
    ps_u = ctx.ps("s6_pu", [128, 512], F32, n=2)
    ps_y = ctx.ps("s6_py", [128, 512], F32, n=2)

    for half in range(2):
        t0 = half * 2048
        def emit_tile6(it, half=half):
            i = half * 16 + it
            b = it % 2
            kxt = f"s6xt{b}"
            ctx.dma("sync", xt[b][:], x_ap[i * 128:(i + 1) * 128, :], writes=(kxt,))
            kp = f"s6l{b}"
            emit_ln(ctx, kp, kxt, xt[b], xc[b], sq, st[b], u_bf[b],
                    uT=None, ps_tr=None, ident=None)
            for ch in range(8):
                p = ps_tr[ch % 2]
                pk = p.name
                ctx.trans(p[:, 0:128], u_bf[b][:, ch * 128:(ch + 1) * 128], ident[:],
                          reads=(kp + "u", "ident_bf"), writes=(pk,))
                ctx.vec(lambda e, p=p, ch=ch, it=it: e.tensor_copy(
                    uT[:, ch * 2048 + it * 128: ch * 2048 + (it + 1) * 128],
                    p[:, 0:128]),
                    reads=(pk,), writes=(f"s6uT{ch}_{it // 4}",))
            pg = ps_g[0]
            for ch in range(8):
                ctx.mm(pg[:, 0:8], uT[:, ch * 2048 + it * 128: ch * 2048 + (it + 1) * 128],
                       wrT[:, ch * 8:(ch + 1) * 8], start=(ch == 0), stop=(ch == 7),
                       reads=(f"s6uT{ch}_{it // 4}", "s6_wrT"), writes=(pg.name,))
            ctx.vec(lambda e, pg=pg: e.reduce_max(rst[:, 0:1], pg[:, 0:8], axis=AX.X),
                    reads=(pg.name,), writes=("s6rst",))
            ctx.vec(lambda e: e.tensor_scalar(rst[:, 1:2], rst[:, 0:1], -1.0, None,
                                              ALU.mult),
                    reads=("s6rst",), writes=("s6rst",))
            ctx.sca(lambda e, pg=pg: e.activation(out=prob[:], in_=pg[:, 0:8], func=AF.Exp,
                                           bias=rst[:, 1:2], scale=1.0,
                                           accum_out=rst[:, 2:3]),
                    reads=(pg.name, "s6rst"), writes=("s6prob", "s6rst"))
            ctx.vec(lambda e: e.reciprocal(rst[:, 3:4], rst[:, 2:3]),
                    reads=("s6rst",), writes=("s6rst",))
            ctx.vec(lambda e: e.tensor_scalar(prob[:], prob[:], rst[:, 3:4], None,
                                              ALU.mult),
                    reads=("s6prob", "s6rst"), writes=("s6prob",))
            ctx.vec(lambda e: e.reduce_max(rst[:, 0:1], prob[:], axis=AX.X),
                    reads=("s6prob",), writes=("s6rst",))
            ctx.vec(lambda e: e.tensor_scalar(msk[:], prob[:], rst[:, 0:1], None,
                                              ALU.is_equal),
                    reads=("s6prob", "s6rst"), writes=("s6msk",))
            ctx.vec(lambda e: e.scalar_tensor_tensor(p2[:], msk[:], -1e30, prob[:],
                                                     ALU.mult, ALU.add),
                    reads=("s6msk", "s6prob"), writes=("s6p2",))
            ctx.vec(lambda e: e.reduce_max(rst[:, 1:2], p2[:], axis=AX.X),
                    reads=("s6p2",), writes=("s6rst",))
            ctx.vec(lambda e: e.tensor_scalar(msk[:], prob[:], rst[:, 1:2], None,
                                              ALU.is_ge),
                    reads=("s6prob", "s6rst"), writes=("s6msk",))
            ctx.vec(lambda e: e.tensor_tensor(comb[:], prob[:], msk[:], ALU.mult),
                    reads=("s6prob", "s6msk"), writes=("s6comb",))
            p = ps_tr[0]
            ctx.trans(p[0:8, 0:128], comb[:], ident[:],
                      reads=("s6comb", "ident_bf"), writes=(p.name,))
            ctx.vec(lambda e, p=p, it=it: e.tensor_copy(
                combT[0:8, it * 128:(it + 1) * 128], p[0:8, 0:128]),
                reads=(p.name,), writes=(f"s6combT{it // 4}",))

        def s6_gu(g):
            e_i, tb = divmod(g, 4)
            gp = g % 2
            tsl = slice(tb * 512, (tb + 1) * 512)
            wload = ((("s6w1", w1bf, w1s, 8, "scalar", f"s6c1{e_i}"),
                      ("s6w3", w3bf, w3s, 8, "sync", f"s6c3{e_i}"))
                     if tb == 0 else
                     ((("s6w2", w2bf, w2s, NH, "sync", f"s6c2{e_i}"),)
                      if tb == 1 else ()))
            for nm, ap3, sbt, nch, q, ck in wload:
                src = ap3[e_i].rearrange("(c p) m -> p c m", p=128)
                dst = sbt[:].rearrange("p (c m) -> p c m", c=nch)
                ctx.dma(q, dst, src, reads=(ck,), writes=(nm,))
            pu = ps_u[0]
            ctx.mm(pu[:], sel8[:, e_i * 128:(e_i + 1) * 128], combT[:, tsl],
                   start=True, stop=True,
                   reads=("s6_sel", f"s6combT{tb}"), writes=(pu.name,))
            ctx.vec(lambda e, pu=pu, gp=gp: e.tensor_copy(cb[gp][:], pu[:]),
                    reads=(pu.name,), writes=(f"s6cb{gp}",))
            for hch in range(NH):
                pg = ps_g[hch % 2]
                pu2 = ps_u[hch % 2]
                for dch in range(8):
                    usl = slice(dch * 2048 + tb * 512, dch * 2048 + (tb + 1) * 512)
                    ctx.mm(pg[:], w1s[:, dch * H + hch * 128: dch * H + (hch + 1) * 128],
                           uT[:, usl], start=(dch == 0), stop=(dch == 7),
                           reads=(f"s6uT{dch}_{tb}", "s6w1"), writes=(pg.name,))
                for dch in range(8):
                    usl = slice(dch * 2048 + tb * 512, dch * 2048 + (tb + 1) * 512)
                    ctx.mm(pu2[:], w3s[:, dch * H + hch * 128: dch * H + (hch + 1) * 128],
                           uT[:, usl], start=(dch == 0), stop=(dch == 7),
                           reads=(f"s6uT{dch}_{tb}", "s6w3"), writes=(pu2.name,))
                ctx.sca(lambda e, pg=pg: e.activation(out=gbf[:], in_=pg[:],
                                                      func=AF.Silu),
                        reads=(pg.name,), writes=("s6gbf",))
                ctx.vec(lambda e, pu2=pu2, hch=hch, gp=gp: e.tensor_tensor(
                    h1s[gp][:, hch * 512:(hch + 1) * 512], gbf[:], pu2[:],
                    ALU.mult),
                    reads=("s6gbf", pu2.name), writes=(f"s6h1{gp}_{hch}",))

        def s6_down(g):
            e_i, tb = divmod(g, 4)
            gp = g % 2
            for ddch in range(8):
                py = ps_y[ddch % 2]
                for hch in range(NH):
                    ctx.mm(py[:], w2s[:, hch * 1024 + ddch * 128: hch * 1024 + (ddch + 1) * 128],
                           h1s[gp][:, hch * 512:(hch + 1) * 512],
                           start=(hch == 0), stop=(hch == NH - 1),
                           reads=(f"s6h1{gp}_{hch}", "s6w2"), writes=(py.name,))
                ysl = slice(ddch * 2048 + tb * 512, ddch * 2048 + (tb + 1) * 512)
                kya = f"s6ya{ddch}_{tb}"
                if e_i == 0:
                    ctx.vec(lambda e, py=py, ysl=ysl, gp=gp: e.tensor_tensor(
                        yac[:, ysl], py[:], cb[gp][:], ALU.mult),
                        reads=(py.name, f"s6cb{gp}"), writes=(kya,))
                else:
                    tm = tmp[ddch % 2]
                    ktm2 = f"s6tmp{ddch % 2}"
                    ctx.vec(lambda e, py=py, tm=tm, gp=gp: e.tensor_tensor(
                        tm[:], py[:], cb[gp][:], ALU.mult),
                        reads=(py.name, f"s6cb{gp}"), writes=(ktm2,))
                    ctx.vec(lambda e, tm=tm, ysl=ysl: e.tensor_tensor(
                        yac[:, ysl], yac[:, ysl], tm[:], ALU.add),
                        reads=(kya, ktm2), writes=(kya,))

        for it in range(4):
            emit_tile6(it)
        for g in range(4 * E):
            if 1 <= g <= 3:
                for it in range(4 * g, 4 * g + 4):
                    emit_tile6(it)
            s6_gu(g)
            if g >= 1:
                s6_down(g - 1)
        s6_down(4 * E - 1)

        for it in range(16):
            i = half * 16 + it
            b = it % 2
            kx2 = f"s6x2{b}"
            ctx.dma("sync", x2[b][:], x_ap[i * 128:(i + 1) * 128, :],
                    writes=(kx2, f"s6xt{b}"))
            for ch in range(8):
                p = ps_tr[ch % 2]
                ctx.trans(p[:, 0:128], yac[:, ch * 2048 + it * 128: ch * 2048 + (it + 1) * 128],
                          ident[:],
                          reads=(f"s6ya{ch}_{it // 4}", "ident_bf"), writes=(p.name,))
                ctx.vec(lambda e, p=p, ch=ch, b=b: e.tensor_tensor(
                    ysb[b][:, ch * 128:(ch + 1) * 128], p[:, 0:128],
                    x2[b][:, ch * 128:(ch + 1) * 128], ALU.add),
                    reads=(p.name, kx2, f"s6xt{b}"),
                    writes=(f"s6y{b}_{ch}", f"s6l{b}xc"))
            ctx.dma("gpsimd", y_ap[i * 128:(i + 1) * 128, :], ysb[b][:],
                    reads=tuple(f"s6y{b}_{ch}" for ch in range(8)) +
                          (f"s6l{b}xc",))


def w_s6(inputs):
    return {"w_router": inputs["L1_ffn_w_router"], "w1": inputs["L1_ffn_w1"],
            "w2": inputs["L1_ffn_w2"], "w3": inputs["L1_ffn_w3"]}



def s2_consts():
    sel = np.zeros((16, 16 * 128), dtype=bf16)
    for e in range(16):
        sel[e, e * 128:(e + 1) * 128] = 1.0
    return {"s2_sel": sel}


def emit_S2(ctx, aps, x_ap, y_ap, w):
    E, I = 16, 512
    NI = I // 128
    gw_ap = ctx.ext_in("s2_gw", np.ascontiguousarray(w["expert_gate_w"], np.float32))
    uw_ap = ctx.ext_in("s2_uw", np.ascontiguousarray(w["expert_up_w"], np.float32))
    dw_ap = ctx.ext_in("s2_dw", np.ascontiguousarray(w["expert_down_w"], np.float32))
    gwbf = ctx.scratch("s2_gwbf", [15, 1024, I], BF16)
    uwbf = ctx.scratch("s2_uwbf", [15, 1024, I], BF16)
    dwbf = ctx.scratch("s2_dwbf", [15, I, 1024], BF16)
    if not hasattr(ctx, "s2_cast_done"):
        ctx.s2_cast_done = True
        for e_i in range(15):
            ctx.dma("gpsimd", gwbf[e_i], gw_ap[e_i], writes=(f"s2cg{e_i}",))
            ctx.dma("gpsimd", uwbf[e_i], uw_ap[e_i], writes=(f"s2cu{e_i}",))
            ctx.dma("gpsimd", dwbf[e_i], dw_ap[e_i], writes=(f"s2cd{e_i}",))
    rw = load_weight_bf(ctx, "s2_rw", w["router_w"])
    rb = load_weight_bf(ctx, "s2_rb", w["router_bias"][None, :])
    convT = np.ascontiguousarray(np.transpose(w["conv_w"], (2, 1, 0)), np.float32)
    cv = [load_weight_bf(ctx, f"s2_cv{k}", convT[k]) for k in range(3)]
    sup = load_weight_bf(ctx, "s2_sup", w["shared_up_w"])
    sdn = load_weight_bf(ctx, "s2_sdn", w["shared_down_w"])
    ident = load_const_sb(ctx, "ident_bf", aps["ident_bf"], [128, 128], BF16)
    ones = load_const_sb(ctx, "ones_row_bf", aps["ones_row_bf"], [1, 512], BF16)
    sel16 = load_const_sb(ctx, "s2_sel", aps["s2_sel"], [16, 16 * 128], BF16)

    xt = ctx.sb("s2_xt", [128, 1024], F32, n=2)
    xc = ctx.sb("s2_xc", [128, 1024], F32, n=2)
    sq = ctx.sb("s2_sq", [128, 1024], F32)
    st = ctx.sb("s2_st", [128, 8], F32, n=2)
    u_bf = ctx.sb("s2_u", [128, 1024], BF16, n=2)
    TH = 2048 + 2
    uT = ctx.sb("s2_uT", [128, 8 * TH], BF16)
    yac = ctx.sb("s2_yac", [128, 8 * 2048], BF16)
    gws = ctx.sb("s2_gws", [128, 8 * I], BF16)
    uws = ctx.sb("s2_uws", [128, 8 * I], BF16)
    dws = ctx.sb("s2_dws", [128, NI * 1024], BF16)
    h1s = ctx.sb("s2_h1s", [128, NI * 512], BF16, n=2)
    gbf = ctx.sb("s2_gbf", [128, 512], BF16)
    tmp = ctx.sb("s2_tmp", [128, 512], BF16, n=2)
    cb = ctx.sb("s2_cb", [128, 512], BF16, n=2)
    cb15 = ctx.sb("s2_cb15", [128, 512], BF16, n=2)
    prob = ctx.sb("s2_prob", [128, 16], F32)
    p2 = ctx.sb("s2_p2", [128, 16], F32)
    msk = ctx.sb("s2_msk", [128, 16], F32)
    comb = ctx.sb("s2_comb", [128, 16], BF16)
    combT = ctx.sb("s2_combT", [16, 2048], BF16)
    rst = ctx.sb("s2_rst", [128, 8], F32)
    ysb = xc
    x2 = xt

    ps_tr = ctx.ps("s2_ptr", [128, 1024], BF16, n=2)
    ps_g = ctx.ps("s2_pg", [128, 512], F32, n=2)
    ps_u = ctx.ps("s2_pu", [128, 512], F32, n=2)
    ps_y = ctx.ps("s2_py", [128, 512], F32, n=2)

    def uslice(ch, t0, n):
        return uT[:, ch * TH + 2 + t0: ch * TH + 2 + t0 + n]

    for ch in range(8):
        ctx.vec(lambda e, ch=ch: e.memset(uslice(ch, -2, 2), 0.0),
                writes=(f"s2pad{ch}",))

    for half in range(2):
        if half == 1:
            for ch in range(8):
                ctx.vec(lambda e, ch=ch: e.tensor_copy(
                    uslice(ch, -2, 2), uslice(ch, 2046, 2)),
                    reads=(f"s2uT{ch}_3",), writes=(f"s2pad{ch}",))
        def emit_tile2(it, half=half):
            i = half * 16 + it
            b = it % 2
            kxt = f"s2xt{b}"
            ctx.dma("sync", xt[b][:], x_ap[i * 128:(i + 1) * 128, :], writes=(kxt,))
            kp = f"s2l{b}"
            emit_ln(ctx, kp, kxt, xt[b], xc[b], sq, st[b], u_bf[b])
            for ch in range(8):
                p = ps_tr[ch % 2]
                rd = (f"s2pad{ch}",) if it == 0 else ()
                ctx.trans(p[:, 0:128], u_bf[b][:, ch * 128:(ch + 1) * 128], ident[:],
                          reads=(kp + "u", "ident_bf"), writes=(p.name,))
                ctx.vec(lambda e, p=p, ch=ch, it=it: e.tensor_copy(
                    uslice(ch, it * 128, 128), p[:, 0:128]),
                    reads=(p.name,) + rd, writes=(f"s2uT{ch}_{it // 4}",))
            pg = ps_g[0]
            for ch in range(8):
                ctx.mm(pg[:, 0:16], uslice(ch, it * 128, 128),
                       rw[:, ch * 16:(ch + 1) * 16], start=(ch == 0), stop=False,
                       reads=(f"s2uT{ch}_{it // 4}", "s2_rw"), writes=(pg.name,))
            ctx.mm(pg[:, 0:16], ones[:, 0:128], rb[:], start=False, stop=True,
                   reads=("s2_rb", "ones_row_bf"), writes=(pg.name,))
            ctx.vec(lambda e, pg=pg: e.reduce_max(rst[:, 0:1], pg[:, 0:16], axis=AX.X),
                    reads=(pg.name,), writes=("s2rst",))
            ctx.vec(lambda e: e.tensor_scalar(rst[:, 1:2], rst[:, 0:1], -1.0, None,
                                              ALU.mult),
                    reads=("s2rst",), writes=("s2rst",))
            ctx.sca(lambda e, pg=pg: e.activation(out=prob[:], in_=pg[:, 0:16], func=AF.Exp,
                                           bias=rst[:, 1:2], scale=1.0,
                                           accum_out=rst[:, 2:3]),
                    reads=(pg.name, "s2rst"), writes=("s2prob", "s2rst"))
            ctx.vec(lambda e: e.reciprocal(rst[:, 3:4], rst[:, 2:3]),
                    reads=("s2rst",), writes=("s2rst",))
            ctx.vec(lambda e: e.tensor_scalar(prob[:], prob[:], rst[:, 3:4], None,
                                              ALU.mult),
                    reads=("s2prob", "s2rst"), writes=("s2prob",))
            ctx.vec(lambda e: e.reduce_max(rst[:, 0:1], prob[:], axis=AX.X),
                    reads=("s2prob",), writes=("s2rst",))
            ctx.vec(lambda e: e.tensor_scalar(msk[:], prob[:], rst[:, 0:1], None,
                                              ALU.is_equal),
                    reads=("s2prob", "s2rst"), writes=("s2msk",))
            ctx.vec(lambda e: e.scalar_tensor_tensor(p2[:], msk[:], -1e30, prob[:],
                                                     ALU.mult, ALU.add),
                    reads=("s2msk", "s2prob"), writes=("s2p2",))
            ctx.vec(lambda e: e.reduce_max(rst[:, 1:2], p2[:], axis=AX.X),
                    reads=("s2p2",), writes=("s2rst",))
            ctx.vec(lambda e: e.tensor_scalar(msk[:], prob[:], rst[:, 1:2], None,
                                              ALU.is_ge),
                    reads=("s2prob", "s2rst"), writes=("s2msk",))
            ctx.vec(lambda e: e.tensor_tensor(rst[:, 4:5], rst[:, 0:1], rst[:, 1:2],
                                              ALU.add),
                    reads=("s2rst",), writes=("s2rst",))
            ctx.vec(lambda e: e.tensor_scalar(rst[:, 5:6], rst[:, 4:5], 1e-6, None,
                                              ALU.add),
                    reads=("s2rst",), writes=("s2rst",))
            ctx.vec(lambda e: e.reciprocal(rst[:, 6:7], rst[:, 5:6]),
                    reads=("s2rst",), writes=("s2rst",))
            ctx.vec(lambda e: e.tensor_tensor(p2[:], prob[:], msk[:], ALU.mult),
                    reads=("s2prob", "s2msk"), writes=("s2p2",))
            ctx.vec(lambda e: e.tensor_scalar(comb[:], p2[:], rst[:, 6:7], None,
                                              ALU.mult),
                    reads=("s2p2", "s2rst"), writes=("s2comb",))
            p = ps_tr[0]
            ctx.trans(p[0:16, 0:128], comb[:], ident[:],
                      reads=("s2comb", "ident_bf"), writes=(p.name,))
            ctx.vec(lambda e, p=p, it=it: e.tensor_copy(
                combT[0:16, it * 128:(it + 1) * 128], p[0:16, 0:128]),
                reads=(p.name,), writes=(f"s2combT{it // 4}",))

        def s2_gu_shared(tb):
            t0 = tb * 512
            gp = tb % 2
            tsl = slice(tb * 512, (tb + 1) * 512)
            uTk = tuple(f"s2uT{ch}_{tb}" for ch in range(8))
            if tb > 0:
                uTkm = uTk + tuple(f"s2uT{ch}_{tb - 1}" for ch in range(8))
            else:
                uTkm = uTk + tuple(f"s2pad{ch}" for ch in range(8))
            pu = ps_u[0]
            ctx.mm(pu[:], sel16[:, 15 * 128:16 * 128], combT[:, tsl],
                   start=True, stop=True,
                   reads=("s2_sel", f"s2combT{tb}"), writes=(pu.name,))
            ctx.vec(lambda e, pu=pu, gp=gp: e.tensor_copy(cb15[gp][:], pu[:]),
                    reads=(pu.name,), writes=(f"s2cb15{gp}",))
            for ich in range(NI):
                pg = ps_g[ich % 2]
                pu2 = ps_u[ich % 2]
                first = True
                for k in range(3):
                    for dch in range(8):
                        ctx.mm(pg[:], cv[k][:, dch * I + ich * 128: dch * I + (ich + 1) * 128],
                               uslice(dch, t0 + k - 2, 512),
                               start=first, stop=(k == 2 and dch == 7),
                               reads=uTkm + (f"s2_cv{k}",), writes=(pg.name,))
                        first = False
                for dch in range(8):
                    ctx.mm(pu2[:], sup[:, dch * I + ich * 128: dch * I + (ich + 1) * 128],
                           uslice(dch, t0, 512), start=(dch == 0), stop=(dch == 7),
                           reads=uTk + ("s2_sup",), writes=(pu2.name,))
                ctx.sca(lambda e, pg=pg: e.activation(out=gbf[:], in_=pg[:],
                                                      func=AF.Silu),
                        reads=(pg.name,), writes=("s2gbf",))
                ctx.vec(lambda e, pu2=pu2, ich=ich, gp=gp: e.tensor_tensor(
                    h1s[gp][:, ich * 512:(ich + 1) * 512], gbf[:], pu2[:],
                    ALU.mult),
                    reads=("s2gbf", pu2.name), writes=(f"s2h1{gp}_{ich}",))

        def s2_down_shared(tb):
            t0 = tb * 512
            gp = tb % 2
            for ddch in range(8):
                py = ps_y[ddch % 2]
                for ich in range(NI):
                    ctx.mm(py[:], sdn[:, ich * 1024 + ddch * 128: ich * 1024 + (ddch + 1) * 128],
                           h1s[gp][:, ich * 512:(ich + 1) * 512],
                           start=(ich == 0), stop=(ich == NI - 1),
                           reads=(f"s2h1{gp}_{ich}", "s2_sdn"), writes=(py.name,))
                ysl = slice(ddch * 2048 + tb * 512, ddch * 2048 + (tb + 1) * 512)
                ctx.vec(lambda e, py=py, ysl=ysl: e.tensor_copy(yac[:, ysl], py[:]),
                        reads=(py.name,), writes=(f"s2ya{ddch}_{tb}",))
            for ddch in range(8):
                tm = tmp[ddch % 2]
                ktm2 = f"s2tmp{ddch % 2}"
                ysl = slice(ddch * 2048 + tb * 512, ddch * 2048 + (tb + 1) * 512)
                ctx.vec(lambda e, tm=tm, ddch=ddch, t0=t0, gp=gp: e.tensor_tensor(
                    tm[:], uslice(ddch, t0, 512), cb15[gp][:], ALU.mult),
                    reads=(f"s2uT{ddch}_{tb}", f"s2cb15{gp}"), writes=(ktm2,))
                ctx.vec(lambda e, tm=tm, ysl=ysl: e.tensor_tensor(
                    yac[:, ysl], yac[:, ysl], tm[:], ALU.add),
                    reads=(f"s2ya{ddch}_{tb}", ktm2), writes=(f"s2ya{ddch}_{tb}",))

        def s2_gu_exp(g):
            e_i, tb = divmod(g, 4)
            gp = g % 2
            t0 = tb * 512
            tsl = slice(tb * 512, (tb + 1) * 512)
            uTk = tuple(f"s2uT{ch}_{tb}" for ch in range(8))
            wload = ((("s2gw", gwbf, gws, 8, "scalar", f"s2cg{e_i}"),
                      ("s2uw", uwbf, uws, 8, "sync", f"s2cu{e_i}"))
                     if tb == 0 else
                     ((("s2dw", dwbf, dws, NI, "sync", f"s2cd{e_i}"),)
                      if tb == 1 else ()))
            for nm, ap3, sbt, nch, q, ck in wload:
                src = ap3[e_i].rearrange("(c p) m -> p c m", p=128)
                dst = sbt[:].rearrange("p (c m) -> p c m", c=nch)
                ctx.dma(q, dst, src, reads=(ck,), writes=(nm,))
            pu = ps_u[0]
            ctx.mm(pu[:], sel16[:, e_i * 128:(e_i + 1) * 128], combT[:, tsl],
                   start=True, stop=True,
                   reads=("s2_sel", f"s2combT{tb}"), writes=(pu.name,))
            ctx.vec(lambda e, pu=pu, gp=gp: e.tensor_copy(cb[gp][:], pu[:]),
                    reads=(pu.name,), writes=(f"s2cb{gp}",))
            for ich in range(NI):
                pg = ps_g[ich % 2]
                pu2 = ps_u[ich % 2]
                for dch in range(8):
                    ctx.mm(pg[:], gws[:, dch * I + ich * 128: dch * I + (ich + 1) * 128],
                           uslice(dch, t0, 512), start=(dch == 0), stop=(dch == 7),
                           reads=uTk + ("s2gw",), writes=(pg.name,))
                for dch in range(8):
                    ctx.mm(pu2[:], uws[:, dch * I + ich * 128: dch * I + (ich + 1) * 128],
                           uslice(dch, t0, 512), start=(dch == 0), stop=(dch == 7),
                           reads=uTk + ("s2uw",), writes=(pu2.name,))
                ctx.sca(lambda e, pg=pg: e.activation(out=gbf[:], in_=pg[:],
                                                      func=AF.Silu),
                        reads=(pg.name,), writes=("s2gbf",))
                ctx.vec(lambda e, pu2=pu2, ich=ich, gp=gp: e.tensor_tensor(
                    h1s[gp][:, ich * 512:(ich + 1) * 512], gbf[:], pu2[:],
                    ALU.mult),
                    reads=("s2gbf", pu2.name), writes=(f"s2h1{gp}_{ich}",))

        def s2_down_exp(g):
            e_i, tb = divmod(g, 4)
            gp = g % 2
            for ddch in range(8):
                py = ps_y[ddch % 2]
                for ich in range(NI):
                    ctx.mm(py[:], dws[:, ich * 1024 + ddch * 128: ich * 1024 + (ddch + 1) * 128],
                           h1s[gp][:, ich * 512:(ich + 1) * 512],
                           start=(ich == 0), stop=(ich == NI - 1),
                           reads=(f"s2h1{gp}_{ich}", "s2dw"), writes=(py.name,))
                tm = tmp[ddch % 2]
                ktm2 = f"s2tmp{ddch % 2}"
                ysl = slice(ddch * 2048 + tb * 512, ddch * 2048 + (tb + 1) * 512)
                ctx.vec(lambda e, py=py, tm=tm, gp=gp: e.tensor_tensor(
                    tm[:], py[:], cb[gp][:], ALU.mult),
                    reads=(py.name, f"s2cb{gp}"), writes=(ktm2,))
                ctx.vec(lambda e, tm=tm, ysl=ysl: e.tensor_tensor(
                    yac[:, ysl], yac[:, ysl], tm[:], ALU.add),
                    reads=(f"s2ya{ddch}_{tb}", ktm2), writes=(f"s2ya{ddch}_{tb}",))

        def s2_gu(g):
            if g < 4:
                s2_gu_shared(g)
            else:
                s2_gu_exp(g - 4)

        def s2_down(g):
            if g < 4:
                s2_down_shared(g)
            else:
                s2_down_exp(g - 4)

        for it in range(4):
            emit_tile2(it)
        for g in range(4 + 15 * 4):
            if 1 <= g <= 3:
                for it in range(4 * g, 4 * g + 4):
                    emit_tile2(it)
            s2_gu(g)
            if g >= 1:
                s2_down(g - 1)
        s2_down(4 + 15 * 4 - 1)

        for it in range(16):
            i = half * 16 + it
            b = it % 2
            kx2 = f"s2x2{b}"
            ctx.dma("sync", x2[b][:], x_ap[i * 128:(i + 1) * 128, :],
                    writes=(kx2, f"s2xt{b}"))
            for ch in range(8):
                p = ps_tr[ch % 2]
                ctx.trans(p[:, 0:128], yac[:, ch * 2048 + it * 128: ch * 2048 + (it + 1) * 128],
                          ident[:],
                          reads=(f"s2ya{ch}_{it // 4}", "ident_bf"), writes=(p.name,))
                ctx.vec(lambda e, p=p, ch=ch, b=b: e.tensor_tensor(
                    ysb[b][:, ch * 128:(ch + 1) * 128], p[:, 0:128],
                    x2[b][:, ch * 128:(ch + 1) * 128], ALU.add),
                    reads=(p.name, kx2, f"s2xt{b}"),
                    writes=(f"s2y{b}_{ch}", f"s2l{b}xc"))
            ctx.dma("gpsimd", y_ap[i * 128:(i + 1) * 128, :], ysb[b][:],
                    reads=tuple(f"s2y{b}_{ch}" for ch in range(8)) +
                          (f"s2l{b}xc",))


def w_s2(inputs):
    return {k: inputs["L0_ffn_" + k] for k in
            ("router_w", "router_bias", "expert_gate_w", "expert_up_w",
             "expert_down_w", "conv_w", "shared_up_w", "shared_down_w")}



def s3_consts():
    kt = np.arange(128)[:, None]
    jj = np.arange(896)[None, :]
    c = {"s3_W": ((jj - 384) >= kt).astype(np.float32)}
    c["ones_col_bf"] = np.ones((128, 1), dtype=bf16)
    return c


def emit_S3(ctx, aps, x_ap, y_ap, w):
    wqT = load_weight_bf(ctx, "s3_wqT", w["Wq"].T)
    wkT = load_weight_bf(ctx, "s3_wkT", w["Wk"].T)
    wvT = load_weight_bf(ctx, "s3_wvT", w["Wv"].T)
    woT = load_weight_bf(ctx, "s3_woT", w["Wo"].T)
    nw_ap = ctx.ext_in("s3_nwf", np.ascontiguousarray(w["norm_w"][:, None],
                                                      np.float32))
    nw = ctx.sb("s3_nw_sb", [128, 1], F32)
    ctx.dma("gpsimd", nw[:], nw_ap, writes=("s3_nw",))
    ident = load_const_sb(ctx, "ident_bf", aps["ident_bf"], [128, 128], BF16)
    ones = load_const_sb(ctx, "ones_row_bf", aps["ones_row_bf"], [1, 512], BF16)
    onesc = load_const_sb(ctx, "ones_col_bf", aps["ones_col_bf"], [128, 1], BF16)
    Wm = load_const_sb(ctx, "s3_W", aps["s3_W"], [128, 896], F32)

    qT_sc = ctx.scratch("s3_qTsc", [8, 128, T], BF16)
    kT_sc = ctx.scratch("s3_kTsc", [8, 128, T], BF16)
    vtm_sc = ctx.scratch("s3_vtmsc", [T, D], BF16)
    onT_sc = ctx.scratch("s3_onTsc", [8, 128, T], BF16)

    xt = ctx.sb("s3_xt", [128, 1024], F32, n=2)
    xc = ctx.sb("s3_xc", [128, 1024], F32, n=2)
    sq = ctx.sb("s3_sq", [128, 1024], F32)
    st = ctx.sb("s3_st", [128, 8], F32, n=2)
    u_bf = ctx.sb("s3_u", [128, 1024], BF16, n=2)
    uT = ctx.sb("s3_uT", [128, 1024], BF16, n=2)
    ktm = ctx.sb("s3_ktm", [128, 1024], BF16, n=2)
    vtm = ctx.sb("s3_vtm", [128, 1024], BF16, n=2)
    qtm = ctx.sb("s3_qtm", [128, 1024], BF16, n=2)
    qTt = ctx.sb("s3_qTt", [128, 1024], BF16, n=2)
    kTt = ctx.sb("s3_kTt", [128, 1024], BF16, n=2)
    qTh = ctx.sb("s3_qTh", [128, T], BF16, n=2)
    kTh = ctx.sb("s3_kTh", [128, T], BF16, n=2)
    vth = ctx.sb("s3_vth", [128, 32 * 128], BF16, n=2)
    PT = ctx.sb("s3_PT", [128, 10 * 512], BF16)
    obf = ctx.sb("s3_obf", [128, 512], BF16, n=2)
    o1 = ctx.sb("s3_o1", [128, 512], BF16, n=2)
    o2 = ctx.sb("s3_o2", [128, 512], BF16, n=2)
    t1 = ctx.sb("s3_t1", [128, 512], BF16, n=2)
    onb = ctx.sb("s3_onb", [128, 512], BF16, n=2)
    rowf = ctx.sb("s3_rowf", [1, 512], F32, n=2)
    rdrow = ctx.sb("s3_rdrow", [1, 512], BF16, n=2)
    rrrow = ctx.sb("s3_rrrow", [1, 512], BF16, n=1)
    onTt = qtm
    ysb = xc
    x2 = xt

    ps_tr = ctx.ps("s3_ptr", [128, 1024], BF16, n=2)
    ps_S = ctx.ps("s3_S", [128, 512], F32, n=2)
    ps_O = ctx.ps("s3_O", [128, 512], F32, n=2)
    ps_D = ctx.ps("s3_D", [128, 512], F32, n=1)
    ps_B = ctx.ps("s3_B", [128, 512], F32, n=1)

    for i in range(NT):
        b = i % 2
        kxt = f"s3xt{b}"
        ctx.dma("sync", xt[b][:], x_ap[i * 128:(i + 1) * 128, :], writes=(kxt,))
        kp = f"s3l{b}"
        emit_ln(ctx, kp, kxt, xt[b], xc[b], sq, st[b], u_bf[b],
                uT=uT[b], ps_tr=ps_tr, ident=ident)
        kuT = kp + "uT"

        def proj_tm(wT, wkey, dest, dkey, b=b, kuT=kuT):
            for blk in range(2):
                ps = ps_S[blk]
                for ch in range(8):
                    ctx.mm(ps[:], uT[b][:, ch * 128:(ch + 1) * 128],
                           wT[:, ch * 1024 + blk * 512: ch * 1024 + blk * 512 + 512],
                           start=(ch == 0), stop=(ch == 7),
                           reads=(kuT, wkey), writes=(ps.name,))
                ctx.vec(lambda e, ps=ps, blk=blk, dest=dest: e.tensor_copy(
                    dest[:, blk * 512:(blk + 1) * 512], ps[:]),
                    reads=(ps.name,), writes=(dkey,))

        proj_tm(wqT, "s3_wqT", qtm[b], f"s3qtm{b}")
        proj_tm(wkT, "s3_wkT", ktm[b], f"s3ktm{b}")
        proj_tm(wvT, "s3_wvT", vtm[b], f"s3vtm{b}")
        for src, dest, dkey, skey in ((qtm[b], qTt[b], f"s3qTt{b}", f"s3qtm{b}"),
                                      (ktm[b], kTt[b], f"s3kTt{b}", f"s3ktm{b}")):
            for ch in range(8):
                p = ps_tr[ch % 2]
                ctx.trans(p[:, 0:128], src[:, ch * 128:(ch + 1) * 128], ident[:],
                          reads=(skey, "ident_bf"), writes=(p.name,))
                ctx.vec(lambda e, p=p, ch=ch, dest=dest: e.tensor_copy(
                    dest[:, ch * 128:(ch + 1) * 128], p[:, 0:128]),
                    reads=(p.name,), writes=(dkey,))
        tsl = slice(i * 128, (i + 1) * 128)
        ctx.dma("gpsimd", qT_sc[:, :, tsl].rearrange("c p m -> p c m"),
                qTt[b][:].rearrange("p (c m) -> p c m", c=8), reads=(f"s3qTt{b}",),
                writes=(f"sc3qT{i}",))
        ctx.dma("gpsimd", kT_sc[:, :, tsl].rearrange("c p m -> p c m"),
                kTt[b][:].rearrange("p (c m) -> p c m", c=8), reads=(f"s3kTt{b}",),
                writes=(f"sc3kT{i}",))
        ctx.dma("gpsimd", vtm_sc[tsl, :], vtm[b][:], reads=(f"s3vtm{b}",),
                writes=(f"sc3v{i}",))

    allq = tuple(f"sc3qT{i}" for i in range(NT))
    allk = tuple(f"sc3kT{i}" for i in range(NT))
    allv = tuple(f"sc3v{i}" for i in range(NT))

    def load_head(h):
        p = h % 2
        ctx.dma("sync", qTh[p][:], qT_sc[h], reads=allq, writes=(f"s3qTh{p}",))
        ctx.dma("sync", kTh[p][:], kT_sc[h], reads=allk, writes=(f"s3kTh{p}",))
        ctx.dma("sync", vth[p][:].rearrange("p (n m) -> p n m", n=32),
                vtm_sc[:, h * 128:(h + 1) * 128].rearrange("(n p) m -> p n m", p=128),
                reads=allv, writes=(f"s3vth{p}",))

    LAG = 8
    load_head(0)
    load_head(1)
    for h in range(8):
        p = h % 2
        kq, kk, kv = f"s3qTh{p}", f"s3kTh{p}", f"s3vth{p}"

        def post_bcast1(qb):
            q2 = qb % 2
            kd = f"s3dr{q2}"
            ctx.mm(ps_B[:], ones[:, 0:128], rdrow[q2][:], start=True, stop=True,
                   reads=(kd + "c", "ones_row_bf"), writes=(ps_B.name,))

        def post_ss(qb):
            q2 = qb % 2
            psO = ps_O[q2]
            kd, ko = f"s3dr{q2}", f"s3o{q2}"
            ctx.sca(lambda e, q2=q2, psO=psO: e.copy(obf[q2][:], psO[:]),
                    reads=(psO.name,), writes=(ko,))
            ctx.vec(lambda e, q2=q2: e.tensor_tensor(
                o1[q2][:], obf[q2][:], ps_B[:], ALU.mult),
                reads=(ko, ps_B.name), writes=(ko + "1",))
            ctx.sca(lambda e, q2=q2: e.activation(
                out=o2[q2][:], in_=o1[q2][:], func=AF.Square),
                reads=(ko + "1",), writes=(ko + "2",))
            ctx.mm(psO[0:1, 0:512], onesc[:, 0:1], o2[q2][:], start=True, stop=True,
                   reads=(ko + "2", "ones_col_bf"), writes=(psO.name,))
            ctx.vec(lambda e, q2=q2, psO=psO: e.tensor_scalar(
                rowf[q2][:], psO[0:1, 0:512], 1.0 / 128, EPS_LN, ALU.mult, ALU.add),
                reads=(psO.name, kd + "c"), writes=(kd + "d",))
            ctx.sca(lambda e, q2=q2: e.activation(
                out=rowf[q2][:], in_=rowf[q2][:], func=AF.Ln),
                reads=(kd + "d",), writes=(kd + "d2",))
            ctx.sca(lambda e, q2=q2: e.activation(
                out=rrrow[:], in_=rowf[q2][:], func=AF.Exp, scale=-0.5),
                reads=(kd + "d2", "s3rrtok"), writes=(kd + "e",))

        def post_fin(qb, h=h):
            q2 = qb % 2
            kd, ko = f"s3dr{q2}", f"s3o{q2}"
            ctx.mm(ps_B[:], ones[:, 0:128], rrrow[:], start=True, stop=True,
                   reads=(kd + "e", "ones_row_bf"), writes=(ps_B.name, "s3rrtok"))
            ctx.vec(lambda e, q2=q2: e.tensor_tensor(
                t1[q2][:], o1[q2][:], ps_B[:], ALU.mult),
                reads=(ko + "1", ps_B.name), writes=(ko + "3",))
            ctx.gps(lambda e, q2=q2: e.tensor_scalar(
                onb[q2][:], t1[q2][:], nw[:, 0:1], None, ALU.mult),
                reads=(ko + "3", "s3_nw"), writes=(ko + "4",))
            ctx.dma("sync", onT_sc[h][:, qb * 512:(qb + 1) * 512], onb[q2][:],
                    reads=(ko + "4",), writes=(f"s3onT{h}_{qb}",))

        def emit_block(qb, p=p, kq=kq, kk=kk, kv=kv):
            nkt = 4 * qb + 4
            psO = ps_O[qb % 2]
            psD = ps_D

            def qk(ktb):
                di = ktb - 4 * qb
                c0 = max(0, di) * 128
                sb2 = ps_S[ktb % 2]
                ctx.mm(sb2[:, c0:512], kTh[p][:, ktb * 128:(ktb + 1) * 128],
                       qTh[p][:, qb * 512 + c0:(qb + 1) * 512],
                       start=True, stop=True, reads=(kq, kk), writes=(sb2.name,))
                kpt = f"s3pt{ktb % 10}"
                pts = PT[:, (ktb % 10) * 512:(ktb % 10 + 1) * 512]
                if di >= 0:
                    if c0 > 0:
                        ctx.gps(lambda e, pts=pts, c0=c0: e.memset(
                            pts[:, 0:c0], 0.0), writes=(kpt,))
                    ctx.vec(lambda e, pts=pts, sb2=sb2, c0=c0, di=di:
                            e.tensor_tensor(pts[:, c0:512], sb2[:, c0:512],
                                            Wm[:, 384:896 - di * 128], ALU.mult),
                            reads=(sb2.name, "s3_W"), writes=(kpt,))
                else:
                    ctx.sca(lambda e, pts=pts, sb2=sb2: e.copy(pts[:], sb2[:]),
                            reads=(sb2.name,), writes=(kpt,))

            def den(ktb):
                kpt = f"s3pt{ktb % 10}"
                pts = PT[:, (ktb % 10) * 512:(ktb % 10 + 1) * 512]
                ctx.mm(psD[0:1, 0:512], onesc[:, 0:1], pts,
                       start=(ktb == 0), stop=(ktb == nkt - 1),
                       reads=(kpt, "ones_col_bf"), writes=(psD.name,))

            def av(ktb):
                kpt = f"s3pt{ktb % 10}"
                pts = PT[:, (ktb % 10) * 512:(ktb % 10 + 1) * 512]
                ctx.mm(psO[:], vth[p][:, ktb * 128:(ktb + 1) * 128], pts,
                       start=(ktb == 0), stop=(ktb == nkt - 1),
                       reads=(kpt, kv), writes=(psO.name,))

            kfin = max(10, nkt - 4)
            for k in range(nkt + LAG):
                if k < nkt:
                    qk(k)
                if k == 5 and qb >= 1:
                    post_ss(qb - 1)
                if 2 <= k < nkt + 2:
                    den(k - 2)
                if k == kfin and qb >= 1:
                    post_fin(qb - 1)
                if k >= LAG:
                    av(k - LAG)
            q2 = qb % 2
            kd = f"s3dr{q2}"
            ctx.vec(lambda e, q2=q2: e.tensor_scalar(
                rowf[q2][:], psD[0:1, 0:512], -1.0, 1.0, ALU.mult, ALU.max),
                reads=(psD.name, kd + "e"), writes=(kd,))
            ctx.vec(lambda e, q2=q2: e.tensor_tensor(
                rowf[q2][:], psD[0:1, 0:512], rowf[q2][:], ALU.max),
                reads=(psD.name, kd), writes=(kd + "b",))
            ctx.sca(lambda e, q2=q2: e.activation(
                out=rowf[q2][:], in_=rowf[q2][:], func=AF.Ln),
                reads=(kd + "b",), writes=(kd + "b2",))
            ctx.sca(lambda e, q2=q2: e.activation(
                out=rdrow[q2][:], in_=rowf[q2][:], func=AF.Exp, scale=-1.0),
                reads=(kd + "b2",), writes=(kd + "c",))

        for qb in range(8):
            if qb >= 1:
                post_bcast1(qb - 1)
            emit_block(qb)
        post_bcast1(7)
        post_ss(7)
        post_fin(7)
        if h + 2 < 8:
            load_head(h + 2)

    for i in range(NT):
        b = i % 2
        kx2 = f"s3x2{b}"
        tsl = slice(i * 128, (i + 1) * 128)
        qb = i // 4
        ctx.dma("sync", x2[b][:], x_ap[tsl, :], writes=(kx2, f"s3xt{b}"))
        ctx.dma("sync", onTt[b][:].rearrange("p (c m) -> p c m", c=8),
                onT_sc[:, :, tsl].rearrange("c p m -> p c m"),
                reads=tuple(f"s3onT{hh}_{qb}" for hh in range(8)),
                writes=(f"s3onTt{b}", f"s3qtm{b}"))
        for blk in range(2):
            ps = ps_S[blk]
            for ch in range(8):
                ctx.mm(ps[:], onTt[b][:, ch * 128:(ch + 1) * 128],
                       woT[:, ch * 1024 + blk * 512: ch * 1024 + blk * 512 + 512],
                       start=(ch == 0), stop=(ch == 7),
                       reads=(f"s3onTt{b}", "s3_woT"), writes=(ps.name,))
            ctx.vec(lambda e, ps=ps, b=b, blk=blk: e.tensor_tensor(
                ysb[b][:, blk * 512:(blk + 1) * 512], ps[:],
                x2[b][:, blk * 512:(blk + 1) * 512], ALU.add),
                reads=(ps.name, kx2), writes=(f"s3y{b}_{blk}", f"s3l{b}xc"))
        ctx.dma("gpsimd", y_ap[tsl, :], ysb[b][:],
                reads=(f"s3y{b}_0", f"s3y{b}_1"))



def w_s3(inputs):
    return {k: inputs["L0_attn_b_" + k] for k in ("Wq", "Wk", "Wv", "Wo", "norm_w")}



EXP_SHIFT = 20.0


def s7_consts():
    kt = np.arange(128)[:, None]
    q = np.arange(128)[None, :]
    return {"s7_triadd": np.where(q >= kt, 0.0, -1e30).astype(np.float32)}


def emit_attn_core(ctx, pref, aps, scale, ym_sc, load_pair_fn):
    onesq = load_const_sb(ctx, "ones_sq_bf", aps["ones_sq_bf"], [128, 128], BF16)
    tri = ctx.s7_tri
    LAG, NPT = 4, 8
    QTp = ctx.sb(pref + "_QTp", [128, T], BF16, n=2)
    KTp = ctx.sb(pref + "_KTp", [128, T], BF16, n=2)
    vA = ctx.sb(pref + "_vA", [128, 32 * 65], BF16, n=2)
    vB = ctx.sb(pref + "_vB", [128, 32 * 65], BF16, n=2)
    PT = ctx.sb(pref + "_PT", [128, NPT * 512], BF16)
    rbrow = ctx.sb(pref + "_rbrow", [128, 512], BF16, n=2)
    rb = ctx.sb(pref + "_rb", [64, 512], BF16, n=2)
    ym = ctx.sb(pref + "_ym", [64, 512], BF16, n=2)
    shift = ctx.sb(pref + "_shift", [128, 2], F32)
    ctx.vec(lambda e: e.memset(shift[:, 0:1], -EXP_SHIFT), writes=(pref + "shift",))
    ctx.vec(lambda e: e.memset(shift[:, 1:2], scale), writes=(pref + "shift",))
    ps_S = ctx.attn_ps_s
    ps_O = ctx.ps(pref + "_pO", [128, 512], F32, n=2)
    ps_B = ctx.ps(pref + "_pB", [128, 512], F32)
    for par in range(2):
        for vv, kv in ((vA[par], f"{pref}vA{par}"), (vB[par], f"{pref}vB{par}")):
            ctx.vec(lambda e, vv=vv: e.memset(
                vv[:].rearrange("p (n m) -> p n m", m=65)[:, :, 64:65], 1.0),
                writes=(kv + "_ones",))

    load_pair_fn(0, QTp[0], KTp[0], vA[0], vB[0], 0, 0)
    load_pair_fn(0, QTp[0], KTp[0], vA[0], vB[0], 0, 1)
    for hp in range(8):
        par = hp % 2
        kQ, kK = f"{pref}QTp{par}", f"{pref}KTp{par}"
        blocks = []
        jobs = []
        for qb in range(8):
            for nm in ("A", "B"):
                j = len(jobs)
                nkt = 4 * qb + 4
                for ktb in range(nkt):
                    blocks.append((j, qb, nm, ktb, ktb == 0, ktb == nkt - 1))
                jobs.append((qb, nm, len(blocks) - 1))
        nblocks = len(blocks)
        end2job = {e: j for j, (_, _, e) in enumerate(jobs)}

        def emit_qk_exp(t):
            j, qb, nm, ktb, _, _ = blocks[t]
            di = ktb - 4 * qb
            c0 = max(0, di) * 128
            base = 0 if nm == "A" else 64
            psl = slice(base, base + 64)
            ps = ps_S[t % 2]
            ctx.mm(ps[:, c0:512], KTp[par][psl, ktb * 128:(ktb + 1) * 128],
                   QTp[par][psl, qb * 512 + c0:(qb + 1) * 512],
                   start=True, stop=True, reads=(kQ, kK), writes=(ps.name,))
            if di >= 0:
                ctx.vec(lambda e, ps=ps, c0=c0: e.tensor_tensor(
                    ps[:, c0:c0 + 128], ps[:, c0:c0 + 128], tri[:], ALU.add),
                    reads=(ps.name, "s7_triadd"), writes=(ps.name,))
            kpt = f"{pref}pt{t % NPT}"
            pt = PT[:, (t % NPT) * 512:(t % NPT + 1) * 512]
            if c0 > 0:
                ctx.vec(lambda e, pt=pt, c0=c0: e.memset(pt[:, 0:c0], 0.0),
                        writes=(kpt,))
            ctx.sca(lambda e, ps=ps, pt=pt, c0=c0: e.activation(
                out=pt[:, c0:512], in_=ps[:, c0:512], func=AF.Exp,
                bias=shift[:, 0:1], scale=shift[:, 1:2]),
                reads=(ps.name, pref + "shift"), writes=(kpt,))

        def emit_av(u):
            j, qb, nm, ktb, st0, st1 = blocks[u]
            vv = vA[par] if nm == "A" else vB[par]
            kv = f"{pref}v{nm}{par}"
            kpt = f"{pref}pt{u % NPT}"
            pt = PT[:, (u % NPT) * 512:(u % NPT + 1) * 512]
            po = ps_O[j % 2]
            ctx.mm(po[0:65, :], vv[:, ktb * 65:(ktb + 1) * 65], pt[:],
                   start=st0, stop=st1,
                   reads=(kpt, kv, kv + "_ones"), writes=(po.name,))

        def post_r(j):
            j2 = j % 2
            po = ps_O[j2]
            krb = f"{pref}rb{j2}"
            ctx.vec(lambda e, j2=j2, po=po: e.reciprocal(
                rbrow[j2][64:65, 0:512], po[64:65, 0:512]),
                reads=(po.name,), writes=(krb + "r",))

        def post_m(j, hp=hp):
            qb, nm, _ = jobs[j]
            j2 = j % 2
            po = ps_O[j2]
            krb, kym = f"{pref}rb{j2}", f"{pref}ym{j2}"
            ctx.mm(ps_B[:], onesq[64:65, 0:128], rbrow[j2][64:65, 0:512],
                   start=True, stop=True,
                   reads=(krb + "r", "ones_sq_bf"), writes=(ps_B.name,))
            ctx.vec(lambda e, j2=j2: e.tensor_copy(rb[j2][:], ps_B[0:64, 0:512]),
                    reads=(ps_B.name,), writes=(krb,))
            ctx.vec(lambda e, j2=j2, po=po: e.tensor_tensor(
                ym[j2][:], po[0:64, 0:512], rb[j2][:], ALU.mult),
                reads=(po.name, krb), writes=(kym,))
            ctx.dma("gpsimd", ym_sc[2 * hp + (0 if nm == "A" else 1), :,
                                    qb * 512:(qb + 1) * 512],
                    ym[j2][:], reads=(kym,), writes=(f"sc_ym{hp}_{qb}{nm}",))

        for t in range(nblocks + LAG + 5):
            if t < nblocks:
                emit_qk_exp(t)
            if hp + 1 < 8 and t in (16, 112):
                load_pair_fn(hp + 1, QTp[1 - par], KTp[1 - par], vA[1 - par],
                             vB[1 - par], 1 - par, 0 if t == 16 else 1)
            u = t - LAG
            if 0 <= u < nblocks:
                emit_av(u)
            e1 = t - LAG - 1
            if 0 <= e1 < nblocks and e1 in end2job:
                post_r(end2job[e1])
            e2 = t - LAG - 4
            if 0 <= e2 < nblocks and e2 in end2job:
                post_m(end2job[e2])


def emit_S7(ctx, aps, x_ap, y_ap, w):
    wqT = load_weight_bf(ctx, "s7_wqT", w["wq"].T)
    wkT = load_weight_bf(ctx, "s7_wkT", w["wk"].T)
    wvT = load_weight_bf(ctx, "s7_wvT", w["wv"].T)
    wpT = load_weight_bf(ctx, "s7_wpT", w["wp"].T)
    bq = load_weight_bf(ctx, "s7_bq", w["bq"][None, :])
    bk = load_weight_bf(ctx, "s7_bk", w["bk"][None, :])
    bv = load_weight_bf(ctx, "s7_bv", w["bv"][None, :])
    bp = load_weight_bf(ctx, "s7_bp", w["bp"][None, :])
    ident = load_const_sb(ctx, "ident_bf", aps["ident_bf"], [128, 128], BF16)
    ones = load_const_sb(ctx, "ones_row_bf", aps["ones_row_bf"], [1, 512], BF16)
    onesc = load_const_sb(ctx, "ones_col_bf", aps["ones_col_bf"], [128, 1], BF16)
    tri = load_const_sb(ctx, "s7_triadd", aps["s7_triadd"], [128, 128], F32)
    ctx.s7_ident, ctx.s7_ones, ctx.s7_onesc, ctx.s7_tri = ident, ones, onesc, tri

    qT_sc = ctx.scratch("s7_qTsc", [8, 128, T], BF16)
    kT_sc = ctx.scratch("s7_kTsc", [8, 128, T], BF16)
    v_sc = ctx.scratch("s7_vsc", [T, D], BF16)
    ym_sc = ctx.scratch("s7_ymsc", [16, 64, T], BF16)

    xt = ctx.sb("s7_xt", [128, 1024], F32, n=2)
    xc = ctx.sb("s7_xc", [128, 1024], F32, n=2)
    sq = ctx.sb("s7_sq", [128, 1024], F32)
    st = ctx.sb("s7_st", [128, 8], F32, n=2)
    u_bf = ctx.sb("s7_u", [128, 1024], BF16, n=2)
    uT = ctx.sb("s7_uT", [128, 1024], BF16, n=2)
    qtm = ctx.sb("s7_qtm", [128, 1024], BF16, n=2)
    ktm = ctx.sb("s7_ktm", [128, 1024], BF16, n=2)
    vtm = ctx.sb("s7_vtm", [128, 1024], BF16, n=2)
    qTt = ctx.sb("s7_qTt", [128, 1024], BF16, n=2)
    kTt = ctx.sb("s7_kTt", [128, 1024], BF16, n=2)
    ymTt = qtm
    ysb = xc
    x2 = xt

    ps_tr = ctx.ps("s7_ptr", [128, 1024], BF16, n=2)
    ps_a = ctx.ps("s7_pa", [128, 512], F32)
    ps_b = ctx.ps("s7_pb", [128, 512], F32)

    for i in range(NT):
        b = i % 2
        kxt = f"s7xt{b}"
        ctx.dma("sync", xt[b][:], x_ap[i * 128:(i + 1) * 128, :], writes=(kxt,))
        kp = f"s7l{b}"
        emit_ln(ctx, kp, kxt, xt[b], xc[b], sq, st[b], u_bf[b],
                uT=uT[b], ps_tr=ps_tr, ident=ident)
        kuT = kp + "uT"

        def proj_tm(wT, wkey, bias, bkey, dest, dkey, b=b, kuT=kuT):
            for blk in range(2):
                ps = ps_a if blk == 0 else ps_b
                for ch in range(8):
                    ctx.mm(ps[:], uT[b][:, ch * 128:(ch + 1) * 128],
                           wT[:, ch * 1024 + blk * 512: ch * 1024 + blk * 512 + 512],
                           start=(ch == 0), stop=False,
                           reads=(kuT, wkey), writes=(ps.name,))
                ctx.mm(ps[:], ones[:, 0:128], bias[:, blk * 512:(blk + 1) * 512],
                       start=False, stop=True,
                       reads=(bkey, "ones_row_bf"), writes=(ps.name,))
                ctx.vec(lambda e, ps=ps, blk=blk, dest=dest: e.tensor_copy(
                    dest[:, blk * 512:(blk + 1) * 512], ps[:]),
                    reads=(ps.name,), writes=(dkey,))

        proj_tm(wqT, "s7_wqT", bq, "s7_bq", qtm[b], f"s7qtm{b}")
        proj_tm(wkT, "s7_wkT", bk, "s7_bk", ktm[b], f"s7ktm{b}")
        proj_tm(wvT, "s7_wvT", bv, "s7_bv", vtm[b], f"s7vtm{b}")
        for src, dest, dkey, skey in ((qtm[b], qTt[b], f"s7qTt{b}", f"s7qtm{b}"),
                                      (ktm[b], kTt[b], f"s7kTt{b}", f"s7ktm{b}")):
            for ch in range(8):
                p = ps_tr[ch % 2]
                ctx.trans(p[:, 0:128], src[:, ch * 128:(ch + 1) * 128], ident[:],
                          reads=(skey, "ident_bf"), writes=(p.name,))
                ctx.vec(lambda e, p=p, ch=ch, dest=dest: e.tensor_copy(
                    dest[:, ch * 128:(ch + 1) * 128], p[:, 0:128]),
                    reads=(p.name,), writes=(dkey,))
        tsl = slice(i * 128, (i + 1) * 128)
        ctx.dma("gpsimd", qT_sc[:, :, tsl].rearrange("c p m -> p c m"),
                qTt[b][:].rearrange("p (c m) -> p c m", c=8), reads=(f"s7qTt{b}",),
                writes=(f"sc7qT{i}",))
        ctx.dma("gpsimd", kT_sc[:, :, tsl].rearrange("c p m -> p c m"),
                kTt[b][:].rearrange("p (c m) -> p c m", c=8), reads=(f"s7kTt{b}",),
                writes=(f"sc7kT{i}",))
        ctx.dma("gpsimd", v_sc[tsl, :], vtm[b][:], reads=(f"s7vtm{b}",),
                writes=(f"sc7v{i}",))

    allq7 = tuple(f"sc7qT{i}" for i in range(NT))
    allk7 = tuple(f"sc7kT{i}" for i in range(NT))
    allv7 = tuple(f"sc7v{i}" for i in range(NT))

    def load_pair(hp, QTp, KTp, vA, vB, par, stage):
        if stage != 0:
            return
        ctx.dma("sync", QTp[:], qT_sc[hp], reads=allq7, writes=(f"s7QTp{par}",))
        ctx.dma("sync", KTp[:], kT_sc[hp], reads=allk7, writes=(f"s7KTp{par}",))
        for vv, h, kv in ((vA, 2 * hp, f"s7vA{par}"), (vB, 2 * hp + 1,
                                                       f"s7vB{par}")):
            ctx.dma("sync", vv[:].rearrange("p (n m) -> p n m", m=65)[:, :, 0:64],
                    v_sc.rearrange("(n p) m -> p n m", p=128)
                    [:, :, h * 64:(h + 1) * 64],
                    reads=allv7, writes=(kv,))

    ctx.attn_ps_s = (ps_a, ps_b)
    emit_attn_core(ctx, "s7", aps, 1.0 / 64.0, ym_sc, load_pair)

    for i in range(NT):
        b = i % 2
        kx2 = f"s7x2{b}"
        tsl = slice(i * 128, (i + 1) * 128)
        ctx.dma("sync", x2[b][:], x_ap[tsl, :], writes=(kx2, f"s7xt{b}"))
        ctx.dma("sync", ymTt[b][:].rearrange("p (c m) -> p c m", c=8),
                ym_sc.rearrange("(c h) p t -> (h p) c t", h=2)[:, :, tsl],
                reads=tuple(f"sc_ym{hp}_{i // 4}{nm}" for hp in range(8)
                            for nm in "AB"),
                writes=(f"s7ymTt{b}", f"s7qtm{b}"))
        for blk in range(2):
            ps = ps_a if blk == 0 else ps_b
            for ch in range(8):
                ctx.mm(ps[:], ymTt[b][:, ch * 128:(ch + 1) * 128],
                       wpT[:, ch * 1024 + blk * 512: ch * 1024 + blk * 512 + 512],
                       start=(ch == 0), stop=False,
                       reads=(f"s7ymTt{b}", "s7_wpT"), writes=(ps.name,))
            ctx.mm(ps[:], ones[:, 0:128], bp[:, blk * 512:(blk + 1) * 512],
                   start=False, stop=True,
                   reads=("s7_bp", "ones_row_bf"), writes=(ps.name,))
            ctx.vec(lambda e, ps=ps, b=b, blk=blk: e.tensor_tensor(
                ysb[b][:, blk * 512:(blk + 1) * 512], ps[:],
                x2[b][:, blk * 512:(blk + 1) * 512], ALU.add),
                reads=(ps.name, kx2), writes=(f"s7y{b}_{blk}", f"s7l{b}xc"))
        ctx.dma("gpsimd", y_ap[tsl, :], ysb[b][:],
                reads=(f"s7y{b}_0", f"s7y{b}_1"))


def w_s7(inputs):
    return {k: inputs["L1_attn_b_" + k] for k in
            ("wq", "bq", "wk", "bk", "wv", "bv", "wp", "bp")}



def emit_S1(ctx, aps, x_ap, y_ap, w):
    wqkv = load_weight_bf(ctx, "s1_wqkv", w["Wqkv"])
    bqkv = load_weight_bf(ctx, "s1_bqkv", w["bqkv"][None, :])
    wproj = load_weight_bf(ctx, "s1_wproj", w["Wproj"])
    bproj = load_weight_bf(ctx, "s1_bproj", w["bproj"][None, :])
    ident = load_const_sb(ctx, "ident_bf", aps["ident_bf"], [128, 128], BF16)
    ones = load_const_sb(ctx, "ones_row_bf", aps["ones_row_bf"], [1, 512], BF16)
    onesc = load_const_sb(ctx, "ones_col_bf", aps["ones_col_bf"], [128, 1], BF16)
    tri = load_const_sb(ctx, "s7_triadd", aps["s7_triadd"], [128, 128], F32)
    ctx.s7_ident, ctx.s7_ones, ctx.s7_onesc, ctx.s7_tri = ident, ones, onesc, tri

    xp_sc = ctx.scratch("s1_xpsc", [T, 3 * D], BF16)
    ym_sc = ctx.scratch("s1_ymsc", [16, 64, T], BF16)

    xt = ctx.sb("s1_xt", [128, 1024], F32, n=2)
    xc = ctx.sb("s1_xc", [128, 1024], F32, n=2)
    sq = ctx.sb("s1_sq", [128, 1024], F32)
    st = ctx.sb("s1_st", [128, 8], F32, n=2)
    u_bf = ctx.sb("s1_u", [128, 1024], BF16, n=2)
    uT = ctx.sb("s1_uT", [128, 1024], BF16, n=2)
    xp = ctx.sb("s1_xp", [128, 3072], BF16, n=2)
    qAB = ctx.sb("s1_qAB", [128, 32 * 128], BF16)
    kAB = ctx.sb("s1_kAB", [128, 32 * 128], BF16)
    ymTt = uT
    ysb = xc
    x2 = xt

    ps_tr = ctx.ps("s1_ptr", [128, 1024], BF16, n=2)
    ps_a = ctx.ps("s1_pa", [128, 512], F32)
    ps_b = ctx.ps("s1_pb", [128, 512], F32)

    for i in range(NT):
        b = i % 2
        kxt = f"s1xt{b}"
        ctx.dma("sync", xt[b][:], x_ap[i * 128:(i + 1) * 128, :], writes=(kxt,))
        kp = f"s1l{b}"
        emit_ln(ctx, kp, kxt, xt[b], xc[b], sq, st[b], u_bf[b],
                uT=uT[b], ps_tr=ps_tr, ident=ident)
        kuT = kp + "uT"
        for blk in range(6):
            ps = ps_a if blk % 2 == 0 else ps_b
            for ch in range(8):
                ctx.mm(ps[:], uT[b][:, ch * 128:(ch + 1) * 128],
                       wqkv[:, ch * 3072 + blk * 512: ch * 3072 + blk * 512 + 512],
                       start=(ch == 0), stop=False,
                       reads=(kuT, "s1_wqkv"), writes=(ps.name,))
            ctx.mm(ps[:], ones[:, 0:128], bqkv[:, blk * 512:(blk + 1) * 512],
                   start=False, stop=True,
                   reads=("s1_bqkv", "ones_row_bf"), writes=(ps.name,))
            ctx.vec(lambda e, ps=ps, blk=blk, b=b: e.tensor_copy(
                xp[b][:, blk * 512:(blk + 1) * 512], ps[:]),
                reads=(ps.name,), writes=(f"s1xp{b}",))
        ctx.dma("gpsimd", xp_sc[i * 128:(i + 1) * 128, :], xp[b][:],
                reads=(f"s1xp{b}",), writes=(f"sc1xp{i}",))

    xp_r = xp_sc.rearrange("(h kb phi) (s plo d) -> h s phi plo kb d",
                           h=16, kb=32, phi=8, s=3, plo=16, d=64)

    allxp = tuple(f"sc1xp{i}" for i in range(NT))

    def load_pair(hp, QTp, KTp, vA, vB, par, stage):
        if stage == 0:
            for sect, buf, bk in ((0, qAB, "s1qAB"), (1, kAB, "s1kAB")):
                for half, h in ((0, 2 * hp), (1, 2 * hp + 1)):
                    bview = buf[:].rearrange("p (n m) -> p n m", m=128)
                    for phi in range(8):
                        ctx.dma("sync",
                                bview[phi * 16:(phi + 1) * 16, :,
                                      half * 64:(half + 1) * 64],
                                xp_r[h, sect, phi], reads=allxp, writes=(bk,))
            for vv, h, kv in ((vA, 2 * hp, f"s1vA{par}"), (vB, 2 * hp + 1,
                                                           f"s1vB{par}")):
                vview = vv[:].rearrange("p (n m) -> p n m", m=65)
                for phi in range(8):
                    ctx.dma("sync", vview[phi * 16:(phi + 1) * 16, :, 0:64],
                            xp_r[h, 2, phi], reads=allxp, writes=(kv,))
            return
        for sect, buf, bk in ((0, qAB, "s1qAB"), (1, kAB, "s1kAB")):
            dst = QTp if sect == 0 else KTp
            dk = f"s1QTp{par}" if sect == 0 else f"s1KTp{par}"
            for kb in range(32):
                p = ps_tr[kb % 2]
                ctx.trans(p[:, 0:128], buf[:, kb * 128:(kb + 1) * 128], ident[:],
                          reads=(bk, "ident_bf"), writes=(p.name,))
                ctx.vec(lambda e, p=p, kb=kb, dst=dst: e.tensor_copy(
                    dst[:, kb * 128:(kb + 1) * 128], p[:, 0:128]),
                    reads=(p.name,), writes=(dk,))

    ctx.attn_ps_s = (ps_a, ps_b)
    emit_attn_core(ctx, "s1", aps, 1.0 / 8.0, ym_sc, load_pair)

    for i in range(NT):
        b = i % 2
        kx2 = f"s1x2{b}"
        tsl = slice(i * 128, (i + 1) * 128)
        ctx.dma("sync", x2[b][:], x_ap[tsl, :], writes=(kx2, f"s1xt{b}"))
        ctx.dma("sync", ymTt[b][:].rearrange("p (c m) -> p c m", c=8),
                ym_sc.rearrange("(c h) p t -> (h p) c t", h=2)[:, :, tsl],
                reads=tuple(f"sc_ym{hp}_{i // 4}{nm}" for hp in range(8)
                            for nm in "AB"),
                writes=(f"s1ymTt{b}", f"s1l{b}uT"))
        for blk in range(2):
            ps = ps_a if blk == 0 else ps_b
            for ch in range(8):
                ctx.mm(ps[:], ymTt[b][:, ch * 128:(ch + 1) * 128],
                       wproj[:, ch * 1024 + blk * 512: ch * 1024 + blk * 512 + 512],
                       start=(ch == 0), stop=False,
                       reads=(f"s1ymTt{b}", "s1_wproj"), writes=(ps.name,))
            ctx.mm(ps[:], ones[:, 0:128], bproj[:, blk * 512:(blk + 1) * 512],
                   start=False, stop=True,
                   reads=("s1_bproj", "ones_row_bf"), writes=(ps.name,))
            ctx.vec(lambda e, ps=ps, b=b, blk=blk: e.tensor_tensor(
                ysb[b][:, blk * 512:(blk + 1) * 512], ps[:],
                x2[b][:, blk * 512:(blk + 1) * 512], ALU.add),
                reads=(ps.name, kx2), writes=(f"s1y{b}_{blk}", f"s1l{b}xc"))
        ctx.dma("gpsimd", y_ap[tsl, :], ysb[b][:],
                reads=(f"s1y{b}_0", f"s1y{b}_1"))


def w_s1(inputs):
    return {k: inputs["L0_attn_a_" + k] for k in ("Wqkv", "bqkv", "Wproj", "bproj")}



def build_program(name, emitters, const_fns, weights):
    ctx = Ctx(name)
    consts = base_consts()
    for f in const_fns:
        consts.update(f())
    x_ap = ctx.ext_in_percore("x", [T, D])
    y_ap = ctx.ext_out("y", [T, D])
    with ctx.stack:
        aps = declare_consts(ctx, consts)
        block = ctx.stack.enter_context(ctx.nc.Block())
        for emit_fn, w in emitters:
            emit_fn(ctx, aps, x_ap, y_ap, w)
        ctx.sch.finish(block, ctx.stack)
    return ctx


def run_program(ctx, x_all):
    in_maps = []
    for i in range(N_CORES):
        m = dict(ctx.inputs)
        m["x"] = np.ascontiguousarray(x_all[i])
        in_maps.append(m)
    res = run_bass_kernel_spmd(ctx.nc, in_maps, core_ids=list(range(N_CORES)))
    return np.stack([r["y"] for r in res.results])


def w_s5(inputs):
    return {k: inputs["L1_attn_a_" + k] for k in
            ("Wk", "bk", "Wv", "bv", "Wq", "bq", "Wo", "bo")}


SEQ = (("s1", emit_S1, w_s1), ("s2", emit_S2, w_s2), ("s3", emit_S3, w_s3),
       ("s2", emit_S2, w_s2), ("s5", emit_S5, w_s5), ("s6", emit_S6, w_s6),
       ("s7", emit_S7, w_s7), ("s6", emit_S6, w_s6))


def build_fused(inputs):
    ctx = Ctx("mk", fused=True)
    consts = base_consts()
    for f in (s5_consts, s6_consts, s2_consts, s3_consts, s7_consts):
        consts.update(f())
    x_ap = ctx.ext_in_percore("x", [T, D])
    y_ap = ctx.ext_out("y", [T, D])
    t0_ap = ctx.scratch("chain0", [T, D], F32)
    t1_ap = ctx.scratch("chain1", [T, D], F32)
    with ctx.stack:
        ctx.init_fused()
        aps = declare_consts(ctx, consts)
        block = ctx.stack.enter_context(ctx.nc.Block())
        n = len(SEQ)
        for i, (nm, emit_fn, wf) in enumerate(SEQ):
            src_ap = x_ap if i == 0 else (t0_ap if i % 2 == 1 else t1_ap)
            dst_ap = y_ap if i == n - 1 else (t0_ap if i % 2 == 0 else t1_ap)
            emit_fn(ctx, aps, src_ap, dst_ap, wf(inputs))
            if i < n - 1:
                ctx.new_sublayer()
        ctx.sch.finish(block, ctx.stack)
    return ctx


def kernel(**inputs):
    x = inputs["L0_attn_a_x"]
    ctx = build_fused(inputs)
    return run_program(ctx, x)


# version 25
# speedup vs baseline: 1.2044x; 1.0609x over previous
import math
from contextlib import ExitStack

import numpy as np
import ml_dtypes

import concourse.bass as bass
import concourse.mybir as mybir
from concourse.bass_utils import run_bass_kernel_spmd

F32 = mybir.dt.float32
BF16 = mybir.dt.bfloat16
AF = mybir.ActivationFunctionType
ALU = mybir.AluOpType
AX = mybir.AxisListType
bf16 = ml_dtypes.bfloat16

N_CORES = 8
T = 4096
D = 1024
NT = T // 128
EPS_LN = 1e-5

ENGINES = ("tensor", "vector", "scalar", "gpsimd", "sync")


class Op:
    __slots__ = ("eng", "fn", "gid", "exec_idx", "dma_idx", "dma", "deps", "waits")


class Sched:

    def __init__(self, nc):
        self.nc = nc
        self.ops = []
        self.per_eng = {e: [] for e in ENGINES}
        self.n_exec = {e: 0 for e in ENGINES}
        self.n_dma = {e: 0 for e in ENGINES}
        self.last = {}
        self.last_dma_slot = {e: {} for e in ENGINES}
        self.last_exec_op = {e: None for e in ENGINES}
        self.pending_barrier = None
        self.barrier_armed = set()

    def barrier(self):
        deps = set()
        for e in ENGINES:
            if self.last_exec_op[e] is not None:
                deps.add(self.last_exec_op[e])
            deps.update(self.last_dma_slot[e].values())
        self.pending_barrier = deps
        self.barrier_armed = set(ENGINES)

    def op(self, eng, fn, reads=(), writes=(), dma=False):
        o = Op()
        o.eng = eng
        o.fn = fn
        o.dma = dma
        o.gid = len(self.ops)
        deps = set()
        if self.pending_barrier is not None and eng in self.barrier_armed:
            deps |= self.pending_barrier
            self.barrier_armed.discard(eng)
            if not self.barrier_armed:
                self.pending_barrier = None
        for k in reads:
            e = self.last.get(k)
            if e and e["w"] is not None:
                deps.add(e["w"])
        for k in writes:
            e = self.last.get(k)
            if e:
                if e["w"] is not None:
                    deps.add(e["w"])
                deps.update(e["r"])
        o.deps = deps
        self.ops.append(o)
        self.per_eng[eng].append(o)
        if dma:
            o.dma_idx = self.n_dma[eng]
            self.n_dma[eng] += 1
            o.exec_idx = None
            self.last_dma_slot[eng][o.dma_idx % self.R_SLOTS] = o
        else:
            o.exec_idx = self.n_exec[eng]
            self.n_exec[eng] += 1
            o.dma_idx = None
            self.last_exec_op[eng] = o
        for k in reads:
            self.last.setdefault(k, {"w": None, "r": []})["r"].append(o)
        for k in writes:
            self.last[k] = {"w": o, "r": []}
        return o

    R_SLOTS = 8

    def finish(self, block, stack):
        R = self.R_SLOTS
        exec_sem = {}
        dma_sem = {}
        for e in ENGINES:
            if self.n_exec[e]:
                exec_sem[e] = stack.enter_context(self.nc.semaphore("sx_" + e))
            if self.n_dma[e]:
                dma_sem[e] = [stack.enter_context(self.nc.semaphore(f"sd_{e}{i}"))
                              for i in range(R)]
        observed = {e: {} for e in ENGINES}
        for o in self.ops:
            o.waits = []
            obs = observed[o.eng]
            if o.dma and o.dma_idx >= R:
                j = o.dma_idx
                key = ("d", o.eng, j % R)
                need = 16 * (j // R)
                if obs.get(key, 0) < need:
                    obs[key] = need
                    o.waits.append((dma_sem[o.eng][j % R], need))
            for d in sorted(o.deps, key=lambda d: d.gid):
                if d.dma:
                    j = d.dma_idx
                    key = ("d", d.eng, j % R)
                    need = 16 * (j // R + 1)
                    sem = dma_sem[d.eng][j % R]
                else:
                    if d.eng == o.eng == "tensor":
                        continue
                    key = ("x", d.eng)
                    need = d.exec_idx + 1
                    sem = exec_sem[d.eng]
                if obs.get(key, 0) >= need:
                    continue
                obs[key] = need
                o.waits.append((sem, need))

        stack.enter_context(self.nc.allow_low_precision(
            "bf16 operands with fp32 accumulation per problem tolerance"))

        def make_body(name):
            def body(eng):
                for o in self.per_eng[name]:
                    for sem, val in o.waits:
                        eng.wait_ge(sem, val)
                    inst = o.fn(eng)
                    if o.dma:
                        inst.then_inc(dma_sem[name][o.dma_idx % R], 16)
                    else:
                        inst.then_inc(exec_sem[name], 1)
                if name == "gpsimd" and self.n_dma["gpsimd"]:
                    n = self.n_dma["gpsimd"]
                    for i in range(min(R, n)):
                        eng.wait_ge(dma_sem["gpsimd"][i], 16 * ((n - 1 - i) // R + 1))
            return body

        for name in ENGINES:
            if self.per_eng[name]:
                getattr(block, name)(make_body(name))


class Ctx:

    ARENA_BYTES = 194 * 1024

    def __init__(self, name, fused=False):
        self.nc = bass.Bass("TRN2", target_bir_lowering=False, debug=False)
        self.sch = Sched(self.nc)
        self.stack = ExitStack()
        self.inputs = {}
        self.name = name
        self.fused = fused
        self.inv = 0
        self.ext_cache = {}
        self.scratch_cache = {}
        self.const_cache = {}
        self.arena_base = None
        self.arena_ptr = 0
        self.arena_peak = 0
        self.pool_f = []
        self.pool_b = []
        self.pool_f_used = 0
        self.pool_b_used = 0

    def init_fused(self):
        r = self.nc.bump_sbuf(self.ARENA_BYTES, align_bytes=64)
        assert r is not None, "arena reservation failed"
        self.arena_base = r[0]
        self.pool_f = [self.stack.enter_context(
            self.nc.psum_tensor(f"poolf{i}", [128, 512], F32)) for i in range(6)]
        self.pool_b = [self.stack.enter_context(
            self.nc.psum_tensor(f"poolb{i}", [128, 1024], BF16)) for i in range(2)]

    def new_sublayer(self):
        self.sch.barrier()
        self.arena_peak = max(self.arena_peak, self.arena_ptr)
        self.arena_ptr = 0
        self.pool_f_used = 0
        self.pool_b_used = 0
        self.inv += 1

    def ext_in(self, name, arr):
        if name in self.ext_cache:
            return self.ext_cache[name]
        dt = {np.dtype(np.float32): F32, np.dtype(bf16): BF16}[arr.dtype]
        ap = self.nc.dram_tensor(name, list(arr.shape), dt, kind="ExternalInput").ap()
        self.inputs[name] = arr
        self.ext_cache[name] = ap
        return ap

    def ext_in_percore(self, name, shape, dtyp=F32):
        return self.nc.dram_tensor(name, list(shape), dtyp, kind="ExternalInput").ap()

    def ext_out(self, name, shape, dtyp=F32):
        return self.nc.dram_tensor(name, list(shape), dtyp, kind="ExternalOutput").ap()

    def scratch(self, name, shape, dtyp):
        if name in self.scratch_cache:
            return self.scratch_cache[name]
        ap = self.nc.dram_tensor(name, list(shape), dtyp, kind="Internal").ap()
        self.scratch_cache[name] = ap
        return ap

    def sb(self, name, shape, dtyp, n=1, persist=False):
        if n != 1:
            return [self.sb(f"{name}{i}", shape, dtyp, persist=persist)
                    for i in range(n)]
        if not self.fused or persist:
            return self.stack.enter_context(self.nc.sbuf_tensor(name, shape, dtyp))
        nbytes = shape[1] * mybir.dt.size(dtyp)
        nbytes = (nbytes + 63) & ~63
        assert self.arena_ptr + nbytes <= self.ARENA_BYTES, \
            f"arena overflow at {name}: {self.arena_ptr + nbytes}"
        h = self.nc.alloc_sbuf_tensor_at(f"{name}_v{self.inv}", shape, dtyp,
                                         offset=self.arena_base + self.arena_ptr)
        self.arena_ptr += nbytes
        return h

    def ps(self, name, shape, dtyp=F32, n=1):
        if n != 1:
            return [self.ps(f"{name}{i}", shape, dtyp) for i in range(n)]
        if not self.fused:
            return self.stack.enter_context(self.nc.psum_tensor(name, shape, dtyp))
        if dtyp == F32:
            assert list(shape) == [128, 512], f"psum pool shape {shape}"
            t = self.pool_f[self.pool_f_used]
            self.pool_f_used += 1
        else:
            assert list(shape) == [128, 1024], f"psum pool shape {shape}"
            t = self.pool_b[self.pool_b_used]
            self.pool_b_used += 1
        return t

    def dma(self, queue, out, in_, reads=(), writes=()):
        self.sch.op(queue, lambda e, o=out, i=in_: e.dma_start(out=o, in_=i),
                    reads=reads, writes=writes, dma=True)

    def mm(self, out, lhsT, rhs, start, stop, reads=(), writes=()):
        self.sch.op("tensor",
                    lambda e, o=out, l=lhsT, r=rhs, s0=start, s1=stop:
                        e.matmul(o, l, r, start=s0, stop=s1),
                    reads=reads, writes=writes)

    def trans(self, out, in_, ident, reads=(), writes=()):
        self.sch.op("tensor",
                    lambda e, o=out, i=in_, d=ident: e.transpose(o, i, d),
                    reads=reads, writes=writes)

    def vec(self, fn, reads=(), writes=()):
        self.sch.op("vector", fn, reads=reads, writes=writes)

    def sca(self, fn, reads=(), writes=()):
        self.sch.op("scalar", fn, reads=reads, writes=writes)

    def gps(self, fn, reads=(), writes=()):
        self.sch.op("gpsimd", fn, reads=reads, writes=writes)



def base_consts():
    return {
        "ident_bf": np.eye(128, dtype=bf16),
        "ones_row_bf": np.ones((1, 512), dtype=bf16),
        "ones_sq_bf": np.ones((128, 128), dtype=bf16),
    }


def declare_consts(ctx, consts):
    return {k: ctx.ext_in(k, v) for k, v in consts.items()}


def load_const_sb(ctx, name, ap, shape, dtyp):
    if name in ctx.const_cache:
        return ctx.const_cache[name]
    sbt = ctx.sb(name + "_sb", shape, dtyp, persist=True)
    ctx.dma("sync", sbt[:], ap, writes=(name,))
    ctx.const_cache[name] = sbt
    return sbt


def load_weight_bf(ctx, name, host_arr):
    arr = np.ascontiguousarray(host_arr, dtype=np.float32)
    ap = ctx.ext_in(name, arr)
    R, C = arr.shape
    if R % 128 == 0:
        nch = R // 128
        sbt = ctx.sb(name + "_sb", [128, nch * C], BF16)
        in_ = ap.rearrange("(c p) m -> p c m", p=128)
        out = sbt[:].rearrange("p (c m) -> p c m", c=nch)
        ctx.dma("gpsimd", out, in_, writes=(name,))
    else:
        assert R < 128
        sbt = ctx.sb(name + "_sb", [R, C], BF16)
        ctx.dma("gpsimd", sbt[:], ap, writes=(name,))
    return sbt



def emit_ln(ctx, kp, kx, xt, xc, sq, st, u_bf, uT=None, ps_tr=None, ident=None):
    ctx.sca(lambda e: e.activation(out=sq[:], in_=xt[:], func=AF.Copy,
                                   accum_out=st[:, 0:1]),
            reads=(kx,), writes=(kp + "sq", kp + "st0"))
    ctx.sca(lambda e: e.activation(out=sq[:], in_=xt[:], func=AF.Square,
                                   accum_out=st[:, 1:2]),
            reads=(kx, kp + "st0"), writes=(kp + "sq", kp + "st1"))
    ctx.vec(lambda e: e.tensor_scalar(st[:, 2:3], st[:, 0:1], 1.0 / D, None,
                                      ALU.mult),
            reads=(kp + "st0",), writes=(kp + "st2",))
    ctx.vec(lambda e: e.tensor_tensor(st[:, 4:5], st[:, 2:3], st[:, 2:3],
                                      ALU.mult),
            reads=(kp + "st2",), writes=(kp + "st4",))
    ctx.vec(lambda e: e.scalar_tensor_tensor(st[:, 3:4], st[:, 1:2], 1.0 / D,
                                             st[:, 4:5], ALU.mult, ALU.subtract),
            reads=(kp + "st1", kp + "st4"), writes=(kp + "st3",))
    ctx.vec(lambda e: e.tensor_scalar(st[:, 3:4], st[:, 3:4], 1.0, EPS_LN,
                                      ALU.mult, ALU.add),
            reads=(kp + "st3",), writes=(kp + "st3b",))
    ctx.sca(lambda e: e.activation(out=st[:, 5:6], in_=st[:, 3:4], func=AF.Sqrt),
            reads=(kp + "st3b",), writes=(kp + "st5",))
    ctx.vec(lambda e: e.reciprocal(st[:, 6:7], st[:, 5:6]),
            reads=(kp + "st5",), writes=(kp + "st6",))
    ctx.vec(lambda e: e.tensor_tensor(st[:, 7:8], st[:, 2:3], st[:, 6:7],
                                      ALU.mult),
            reads=(kp + "st2", kp + "st6"), writes=(kp + "st7",))
    ctx.vec(lambda e: e.tensor_scalar(u_bf[:], xt[:], st[:, 6:7], st[:, 7:8],
                                      ALU.mult, ALU.subtract),
            reads=(kx, kp + "st6", kp + "st7"), writes=(kp + "u",))
    if uT is not None:
        for g in range(2):
            p = ps_tr[g]
            pk = f"pstr{g}"
            for ch in range(4 * g, 4 * g + 4):
                ctx.trans(p[:, (ch % 4) * 128:(ch % 4 + 1) * 128],
                          u_bf[:, ch * 128:(ch + 1) * 128], ident[:],
                          reads=(kp + "u", "ident_bf"), writes=(pk,))
            ctx.vec(lambda e, p=p, g=g: e.tensor_copy(
                uT[:, g * 512:(g + 1) * 512], p[:, 0:512]),
                reads=(pk,), writes=(kp + "uT",))



DECAY = 0.99
LR = 0.01


def s5_consts():
    lam = DECAY
    j = np.arange(512)
    c = {}
    c["s5_colmap"] = np.tile((lam ** j).astype(bf16)[None, :], (128, 1))
    c["s5_rowmap"] = np.tile((lam ** (-(j % 128).astype(np.float64))
                              ).astype(bf16)[None, :], (128, 1))
    return c


def emit_S5(ctx, aps, x_ap, y_ap, w):
    wqT = load_weight_bf(ctx, "s5_wqT", w["Wq"].T)
    wkT = load_weight_bf(ctx, "s5_wkT", w["Wk"].T)
    wvT = load_weight_bf(ctx, "s5_wvT", w["Wv"].T)
    woT = load_weight_bf(ctx, "s5_woT", w["Wo"].T)
    bq = load_weight_bf(ctx, "s5_bq", w["bq"][None, :])
    bk = load_weight_bf(ctx, "s5_bk", w["bk"][None, :])
    bv = load_weight_bf(ctx, "s5_bv", w["bv"][None, :])
    bo = load_weight_bf(ctx, "s5_bo", w["bo"][None, :])
    ident = load_const_sb(ctx, "ident_bf", aps["ident_bf"], [128, 128], BF16)
    ones = load_const_sb(ctx, "ones_row_bf", aps["ones_row_bf"], [1, 512], BF16)
    Wm = load_const_sb(ctx, "s3_W", aps["s3_W"], [128, 896], F32)
    colmap = load_const_sb(ctx, "s5_colmap", aps["s5_colmap"], [128, 512], BF16)
    rowmap = load_const_sb(ctx, "s5_rowmap", aps["s5_rowmap"], [128, 512], BF16)

    xt = ctx.sb("s5_xt", [128, 1024], F32, n=4)
    xc = ctx.sb("s5_xc", [128, 1024], F32, n=4)
    sq = ctx.sb("s5_sq", [128, 1024], F32, n=2)
    st = ctx.sb("s5_st", [128, 8], F32, n=4)
    u_bf = ctx.sb("s5_u", [128, 1024], BF16, n=4)
    uT = ctx.sb("s5_uT", [128, 1024], BF16, n=4)
    qf = ctx.sb("s5_qf", [128, T], BF16)
    kf = ctx.sb("s5_kf", [128, T], BF16)
    vtm = ctx.sb("s5_vtm", [128, T], BF16)
    PT = ctx.sb("s5_PT", [128, 32 * 512], BF16)
    otm = ctx.sb("s5_otT", [128, T], BF16)
    ysb = xc
    x2 = xt

    ps_tr = ctx.ps("s5_ptr", [128, 1024], BF16, n=2)
    ps_a = ctx.ps("s5_pa", [128, 512], F32)
    ps_b = ctx.ps("s5_pb", [128, 512], F32)
    ps_O = ctx.ps("s5_pO", [128, 512], F32, n=2)

    rot4 = None
    for i in range(NT):
        b = i % 4
        kxt = f"s5xt{b}"
        ctx.dma("sync", xt[b][:], x_ap[i * 128:(i + 1) * 128, :], writes=(kxt,))
        kp = f"s5l{b}"
        emit_ln(ctx, kp, kxt, xt[b], xc[b], sq[b % 2], st[b], u_bf[b],
                uT=uT[b], ps_tr=ps_tr, ident=ident)
        kuT = kp + "uT"
        rot4 = (ps_a, ps_b, ps_O[0], ps_O[1])

        def proj(ps, wT, wkey, bias, bkey, dest, dkey, fm, i=i, kuT=kuT, b=b):
            pk = ps.name
            for ch in range(8):
                if fm:
                    lhsT = wT[:, ch * 128:(ch + 1) * 128]
                    rhs = uT[b][:, ch * 128:(ch + 1) * 128]
                else:
                    lhsT = uT[b][:, ch * 128:(ch + 1) * 128]
                    rhs = wT[:, ch * 128:(ch + 1) * 128]
                ctx.mm(ps[:, 0:128], lhsT, rhs, start=(ch == 0), stop=False,
                       reads=(kuT, wkey), writes=(pk,))
            if fm:
                ctx.mm(ps[:, 0:128], bias[:], ones[:, 0:128], start=False, stop=True,
                       reads=(bkey, "ones_row_bf"), writes=(pk,))
            else:
                ctx.mm(ps[:, 0:128], ones[:, 0:128], bias[:], start=False, stop=True,
                       reads=(bkey, "ones_row_bf"), writes=(pk,))
            ctx.vec(lambda e, ps=ps, i=i, dest=dest: e.tensor_copy(
                dest[:, i * 128:(i + 1) * 128], ps[:, 0:128]),
                reads=(pk,), writes=(f"{dkey}{i}",))

        proj(rot4[(3 * i) % 4], wqT, "s5_wqT", bq, "s5_bq", qf, "s5qf", fm=True)
        proj(rot4[(3 * i + 1) % 4], wkT, "s5_wkT", bk, "s5_bk", kf, "s5kf",
             fm=True)
        proj(rot4[(3 * i + 2) % 4], wvT, "s5_wvT", bv, "s5_bv", vtm, "s5vtm",
             fm=False)

    for s in range(8):
        sl = slice(s * 512, (s + 1) * 512)
        tk = tuple(f"s5qf{i}" for i in range(4 * s, 4 * s + 4))
        ctx.vec(lambda e, sl=sl: e.tensor_tensor(qf[:, sl], qf[:, sl],
                                                 colmap[:], ALU.mult),
                reads=tk + ("s5_colmap",), writes=(f"s5qs{s}",))
        tk = tuple(f"s5kf{i}" for i in range(4 * s, 4 * s + 4))
        ctx.gps(lambda e, sl=sl: e.tensor_tensor(kf[:, sl], kf[:, sl],
                                                 rowmap[:], ALU.mult),
                reads=tk + ("s5_rowmap",), writes=(f"s5ks{s}",))

    allq = tuple(f"s5qs{s}" for s in range(8))
    allk = tuple(f"s5ks{s}" for s in range(8))
    lam = DECAY
    for qb in range(8):
        nkt = 4 * qb + 4
        psO = ps_O[qb % 2]
        for ktb in range(nkt):
            di = ktb - 4 * qb
            c0 = max(0, di) * 128
            bs = float(LR * lam ** (qb * 512 - ktb * 128 - 1))
            sb2 = ps_a if ktb % 2 == 0 else ps_b
            ctx.mm(sb2[:, c0:512], kf[:, ktb * 128:(ktb + 1) * 128],
                   qf[:, qb * 512 + c0:(qb + 1) * 512],
                   start=True, stop=True, reads=(allq[qb], allk[ktb // 4]),
                   writes=(sb2.name,))
            kpt = f"s5pt{ktb}"
            pts = PT[:, ktb * 512:(ktb + 1) * 512]
            if di >= 0:
                if c0 > 0:
                    ctx.gps(lambda e, pts=pts, c0=c0: e.memset(pts[:, 0:c0], 0.0),
                            writes=(kpt,))
                ctx.vec(lambda e, pts=pts, sb2=sb2, c0=c0, di=di, bs=bs:
                        e.scalar_tensor_tensor(
                            pts[:, c0:512], sb2[:, c0:512], bs,
                            Wm[:, 383:895 - di * 128], ALU.mult, ALU.mult),
                        reads=(sb2.name, "s3_W"), writes=(kpt,))
            elif ktb % 2 == 0:
                ctx.sca(lambda e, pts=pts, sb2=sb2, bs=bs: e.activation(
                    out=pts[:], in_=sb2[:], func=AF.Copy, scale=bs),
                    reads=(sb2.name,), writes=(kpt,))
            else:
                ctx.vec(lambda e, pts=pts, sb2=sb2, bs=bs: e.tensor_scalar(
                    pts[:], sb2[:], bs, None, ALU.mult),
                    reads=(sb2.name,), writes=(kpt,))
        for ktb in range(nkt):
            kpt = f"s5pt{ktb}"
            pts = PT[:, ktb * 512:(ktb + 1) * 512]
            ctx.mm(psO[:], vtm[:, ktb * 128:(ktb + 1) * 128], pts,
                   start=(ktb == 0), stop=(ktb == nkt - 1),
                   reads=(kpt, f"s5vtm{ktb}"), writes=(psO.name,))
        ctx.vec(lambda e, qb=qb, psO=psO: e.tensor_copy(
            otm[:, qb * 512:(qb + 1) * 512], psO[:]),
            reads=(psO.name,), writes=(f"s5otm{qb}",))

    rot4 = (ps_a, ps_b, ps_O[0], ps_O[1])
    for i in range(NT):
        b = i % 4
        kx2 = f"s5x2{b}"
        ctx.dma("sync", x2[b][:], x_ap[i * 128:(i + 1) * 128, :],
                writes=(kx2, f"s5xt{b}"))
        for blk in range(2):
            nsl = slice(blk * 512, (blk + 1) * 512)
            ps = rot4[(2 * i + blk) % 4]
            pk = ps.name
            ctx.mm(ps[:], otm[:, i * 128:(i + 1) * 128], woT[:, nsl],
                   start=True, stop=False,
                   reads=(f"s5otm{i // 4}", "s5_woT"), writes=(pk,))
            ctx.mm(ps[:], ones[:, 0:128], bo[:, nsl], start=False, stop=True,
                   reads=("s5_bo", "ones_row_bf"), writes=(pk,))
            ctx.vec(lambda e, ps=ps, b=b, nsl=nsl: e.tensor_tensor(
                ysb[b][:, nsl], ps[:], x2[b][:, nsl], ALU.add),
                reads=(pk, kx2), writes=(f"s5y{b}_{blk}", f"s5l{b}xc"))
        ctx.dma("gpsimd", y_ap[i * 128:(i + 1) * 128, :], ysb[b][:],
                reads=(f"s5y{b}_0", f"s5y{b}_1"))



def s6_consts():
    sel = np.zeros((8, 8 * 128), dtype=bf16)
    for e in range(8):
        sel[e, e * 128:(e + 1) * 128] = 1.0
    return {"s6_sel": sel}


def emit_S6(ctx, aps, x_ap, y_ap, w):
    E, H = 8, 1408
    NH = H // 128
    w1_ap = ctx.ext_in("s6_w1", np.ascontiguousarray(w["w1"], np.float32))
    w2_ap = ctx.ext_in("s6_w2", np.ascontiguousarray(w["w2"], np.float32))
    w3_ap = ctx.ext_in("s6_w3", np.ascontiguousarray(w["w3"], np.float32))
    w1bf = ctx.scratch("s6_w1bf", [E, 1024, H], BF16)
    w2bf = ctx.scratch("s6_w2bf", [E, H, 1024], BF16)
    w3bf = ctx.scratch("s6_w3bf", [E, 1024, H], BF16)
    if not hasattr(ctx, "s6_cast_done"):
        ctx.s6_cast_done = True
        for e_i in range(E):
            ctx.dma("gpsimd", w1bf[e_i], w1_ap[e_i], writes=(f"s6c1{e_i}",))
            ctx.dma("gpsimd", w3bf[e_i], w3_ap[e_i], writes=(f"s6c3{e_i}",))
            ctx.dma("gpsimd", w2bf[e_i], w2_ap[e_i], writes=(f"s6c2{e_i}",))
    wrT = load_weight_bf(ctx, "s6_wrT", w["w_router"].T)
    ident = load_const_sb(ctx, "ident_bf", aps["ident_bf"], [128, 128], BF16)
    sel8 = load_const_sb(ctx, "s6_sel", aps["s6_sel"], [8, 8 * 128], BF16)

    xt = ctx.sb("s6_xt", [128, 1024], F32, n=2)
    xc = ctx.sb("s6_xc", [128, 1024], F32, n=2)
    sq = ctx.sb("s6_sq", [128, 1024], F32)
    st = ctx.sb("s6_st", [128, 8], F32, n=2)
    u_bf = ctx.sb("s6_u", [128, 1024], BF16, n=2)
    uT = ctx.sb("s6_uT", [128, 8 * 2048], BF16)
    yac = ctx.sb("s6_yac", [128, 8 * 2048], BF16)
    w1s = ctx.sb("s6_w1s", [128, 8 * H], BF16)
    w3s = ctx.sb("s6_w3s", [128, 8 * H], BF16)
    w2s = ctx.sb("s6_w2s", [128, NH * 1024], BF16)
    h1s = ctx.sb("s6_h1s", [128, NH * 512], BF16, n=2)
    gbf = ctx.sb("s6_gbf", [128, 512], BF16)
    tmp = ctx.sb("s6_tmp", [128, 512], BF16, n=2)
    cb = ctx.sb("s6_cb", [128, 512], BF16, n=2)
    prob = ctx.sb("s6_prob", [128, 8], F32)
    p2 = ctx.sb("s6_p2", [128, 8], F32)
    msk = ctx.sb("s6_msk", [128, 8], F32)
    comb = ctx.sb("s6_comb", [128, 8], BF16)
    combT = ctx.sb("s6_combT", [8, 2048], BF16)
    rst = ctx.sb("s6_rst", [128, 4], F32)
    ysb = xc
    x2 = xt

    ps_tr = ctx.ps("s6_ptr", [128, 1024], BF16, n=2)
    ps_g = ctx.ps("s6_pg", [128, 512], F32, n=2)
    ps_u = ctx.ps("s6_pu", [128, 512], F32, n=2)
    ps_y = ctx.ps("s6_py", [128, 512], F32, n=2)

    for half in range(2):
        t0 = half * 2048
        def emit_tile6(it, half=half):
            i = half * 16 + it
            b = it % 2
            kxt = f"s6xt{b}"
            ctx.dma("sync", xt[b][:], x_ap[i * 128:(i + 1) * 128, :], writes=(kxt,))
            kp = f"s6l{b}"
            emit_ln(ctx, kp, kxt, xt[b], xc[b], sq, st[b], u_bf[b],
                    uT=None, ps_tr=None, ident=None)
            for ch in range(8):
                p = ps_tr[ch % 2]
                pk = p.name
                ctx.trans(p[:, 0:128], u_bf[b][:, ch * 128:(ch + 1) * 128], ident[:],
                          reads=(kp + "u", "ident_bf"), writes=(pk,))
                ctx.vec(lambda e, p=p, ch=ch, it=it: e.tensor_copy(
                    uT[:, ch * 2048 + it * 128: ch * 2048 + (it + 1) * 128],
                    p[:, 0:128]),
                    reads=(pk,), writes=(f"s6uT{ch}_{it // 4}",))
            pg = ps_g[0]
            for ch in range(8):
                ctx.mm(pg[:, 0:8], uT[:, ch * 2048 + it * 128: ch * 2048 + (it + 1) * 128],
                       wrT[:, ch * 8:(ch + 1) * 8], start=(ch == 0), stop=(ch == 7),
                       reads=(f"s6uT{ch}_{it // 4}", "s6_wrT"), writes=(pg.name,))
            ctx.vec(lambda e, pg=pg: e.reduce_max(rst[:, 0:1], pg[:, 0:8], axis=AX.X),
                    reads=(pg.name,), writes=("s6rst",))
            ctx.vec(lambda e: e.tensor_scalar(rst[:, 1:2], rst[:, 0:1], -1.0, None,
                                              ALU.mult),
                    reads=("s6rst",), writes=("s6rst",))
            ctx.sca(lambda e, pg=pg: e.activation(out=prob[:], in_=pg[:, 0:8], func=AF.Exp,
                                           bias=rst[:, 1:2], scale=1.0,
                                           accum_out=rst[:, 2:3]),
                    reads=(pg.name, "s6rst"), writes=("s6prob", "s6rst"))
            ctx.vec(lambda e: e.reciprocal(rst[:, 3:4], rst[:, 2:3]),
                    reads=("s6rst",), writes=("s6rst",))
            ctx.vec(lambda e: e.tensor_scalar(prob[:], prob[:], rst[:, 3:4], None,
                                              ALU.mult),
                    reads=("s6prob", "s6rst"), writes=("s6prob",))
            ctx.vec(lambda e: e.reduce_max(rst[:, 0:1], prob[:], axis=AX.X),
                    reads=("s6prob",), writes=("s6rst",))
            ctx.vec(lambda e: e.tensor_scalar(msk[:], prob[:], rst[:, 0:1], None,
                                              ALU.is_equal),
                    reads=("s6prob", "s6rst"), writes=("s6msk",))
            ctx.vec(lambda e: e.scalar_tensor_tensor(p2[:], msk[:], -1e30, prob[:],
                                                     ALU.mult, ALU.add),
                    reads=("s6msk", "s6prob"), writes=("s6p2",))
            ctx.vec(lambda e: e.reduce_max(rst[:, 1:2], p2[:], axis=AX.X),
                    reads=("s6p2",), writes=("s6rst",))
            ctx.vec(lambda e: e.tensor_scalar(msk[:], prob[:], rst[:, 1:2], None,
                                              ALU.is_ge),
                    reads=("s6prob", "s6rst"), writes=("s6msk",))
            ctx.vec(lambda e: e.tensor_tensor(comb[:], prob[:], msk[:], ALU.mult),
                    reads=("s6prob", "s6msk"), writes=("s6comb",))
            p = ps_tr[0]
            ctx.trans(p[0:8, 0:128], comb[:], ident[:],
                      reads=("s6comb", "ident_bf"), writes=(p.name,))
            ctx.vec(lambda e, p=p, it=it: e.tensor_copy(
                combT[0:8, it * 128:(it + 1) * 128], p[0:8, 0:128]),
                reads=(p.name,), writes=(f"s6combT{it // 4}",))

        def s6_gu(g):
            e_i, tb = divmod(g, 4)
            gp = g % 2
            tsl = slice(tb * 512, (tb + 1) * 512)
            wload = ((("s6w1", w1bf, w1s, 8, "scalar", f"s6c1{e_i}"),
                      ("s6w3", w3bf, w3s, 8, "sync", f"s6c3{e_i}"))
                     if tb == 0 else
                     ((("s6w2", w2bf, w2s, NH, "sync", f"s6c2{e_i}"),)
                      if tb == 1 else ()))
            for nm, ap3, sbt, nch, q, ck in wload:
                src = ap3[e_i].rearrange("(c p) m -> p c m", p=128)
                dst = sbt[:].rearrange("p (c m) -> p c m", c=nch)
                ctx.dma(q, dst, src, reads=(ck,), writes=(nm,))
            pu = ps_u[0]
            ctx.mm(pu[:], sel8[:, e_i * 128:(e_i + 1) * 128], combT[:, tsl],
                   start=True, stop=True,
                   reads=("s6_sel", f"s6combT{tb}"), writes=(pu.name,))
            ctx.vec(lambda e, pu=pu, gp=gp: e.tensor_copy(cb[gp][:], pu[:]),
                    reads=(pu.name,), writes=(f"s6cb{gp}",))
            for hch in range(NH):
                pg = ps_g[hch % 2]
                pu2 = ps_u[hch % 2]
                for dch in range(8):
                    usl = slice(dch * 2048 + tb * 512, dch * 2048 + (tb + 1) * 512)
                    ctx.mm(pg[:], w1s[:, dch * H + hch * 128: dch * H + (hch + 1) * 128],
                           uT[:, usl], start=(dch == 0), stop=(dch == 7),
                           reads=(f"s6uT{dch}_{tb}", "s6w1"), writes=(pg.name,))
                for dch in range(8):
                    usl = slice(dch * 2048 + tb * 512, dch * 2048 + (tb + 1) * 512)
                    ctx.mm(pu2[:], w3s[:, dch * H + hch * 128: dch * H + (hch + 1) * 128],
                           uT[:, usl], start=(dch == 0), stop=(dch == 7),
                           reads=(f"s6uT{dch}_{tb}", "s6w3"), writes=(pu2.name,))
                ctx.sca(lambda e, pg=pg: e.activation(out=gbf[:], in_=pg[:],
                                                      func=AF.Silu),
                        reads=(pg.name,), writes=("s6gbf",))
                ctx.vec(lambda e, pu2=pu2, hch=hch, gp=gp: e.tensor_tensor(
                    h1s[gp][:, hch * 512:(hch + 1) * 512], gbf[:], pu2[:],
                    ALU.mult),
                    reads=("s6gbf", pu2.name), writes=(f"s6h1{gp}_{hch}",))

        def s6_down(g):
            e_i, tb = divmod(g, 4)
            gp = g % 2
            for ddch in range(8):
                py = ps_y[ddch % 2]
                for hch in range(NH):
                    ctx.mm(py[:], w2s[:, hch * 1024 + ddch * 128: hch * 1024 + (ddch + 1) * 128],
                           h1s[gp][:, hch * 512:(hch + 1) * 512],
                           start=(hch == 0), stop=(hch == NH - 1),
                           reads=(f"s6h1{gp}_{hch}", "s6w2"), writes=(py.name,))
                ysl = slice(ddch * 2048 + tb * 512, ddch * 2048 + (tb + 1) * 512)
                kya = f"s6ya{ddch}_{tb}"
                if e_i == 0:
                    ctx.vec(lambda e, py=py, ysl=ysl, gp=gp: e.tensor_tensor(
                        yac[:, ysl], py[:], cb[gp][:], ALU.mult),
                        reads=(py.name, f"s6cb{gp}"), writes=(kya,))
                else:
                    tm = tmp[ddch % 2]
                    ktm2 = f"s6tmp{ddch % 2}"
                    ctx.vec(lambda e, py=py, tm=tm, gp=gp: e.tensor_tensor(
                        tm[:], py[:], cb[gp][:], ALU.mult),
                        reads=(py.name, f"s6cb{gp}"), writes=(ktm2,))
                    ctx.vec(lambda e, tm=tm, ysl=ysl: e.tensor_tensor(
                        yac[:, ysl], yac[:, ysl], tm[:], ALU.add),
                        reads=(kya, ktm2), writes=(kya,))

        for it in range(4):
            emit_tile6(it)
        for g in range(4 * E):
            if 1 <= g <= 3:
                for it in range(4 * g, 4 * g + 4):
                    emit_tile6(it)
            s6_gu(g)
            if g >= 1:
                s6_down(g - 1)
        s6_down(4 * E - 1)

        for it in range(16):
            i = half * 16 + it
            b = it % 2
            kx2 = f"s6x2{b}"
            ctx.dma("sync", x2[b][:], x_ap[i * 128:(i + 1) * 128, :],
                    writes=(kx2, f"s6xt{b}"))
            for ch in range(8):
                p = ps_tr[ch % 2]
                ctx.trans(p[:, 0:128], yac[:, ch * 2048 + it * 128: ch * 2048 + (it + 1) * 128],
                          ident[:],
                          reads=(f"s6ya{ch}_{it // 4}", "ident_bf"), writes=(p.name,))
                ctx.vec(lambda e, p=p, ch=ch, b=b: e.tensor_tensor(
                    ysb[b][:, ch * 128:(ch + 1) * 128], p[:, 0:128],
                    x2[b][:, ch * 128:(ch + 1) * 128], ALU.add),
                    reads=(p.name, kx2, f"s6xt{b}"),
                    writes=(f"s6y{b}_{ch}", f"s6l{b}xc"))
            ctx.dma("gpsimd", y_ap[i * 128:(i + 1) * 128, :], ysb[b][:],
                    reads=tuple(f"s6y{b}_{ch}" for ch in range(8)) +
                          (f"s6l{b}xc",))


def w_s6(inputs):
    return {"w_router": inputs["L1_ffn_w_router"], "w1": inputs["L1_ffn_w1"],
            "w2": inputs["L1_ffn_w2"], "w3": inputs["L1_ffn_w3"]}



def s2_consts():
    sel = np.zeros((16, 16 * 128), dtype=bf16)
    for e in range(16):
        sel[e, e * 128:(e + 1) * 128] = 1.0
    return {"s2_sel": sel}


def emit_S2(ctx, aps, x_ap, y_ap, w):
    E, I = 16, 512
    NI = I // 128
    gw_ap = ctx.ext_in("s2_gw", np.ascontiguousarray(w["expert_gate_w"], np.float32))
    uw_ap = ctx.ext_in("s2_uw", np.ascontiguousarray(w["expert_up_w"], np.float32))
    dw_ap = ctx.ext_in("s2_dw", np.ascontiguousarray(w["expert_down_w"], np.float32))
    gwbf = ctx.scratch("s2_gwbf", [15, 1024, I], BF16)
    uwbf = ctx.scratch("s2_uwbf", [15, 1024, I], BF16)
    dwbf = ctx.scratch("s2_dwbf", [15, I, 1024], BF16)
    if not hasattr(ctx, "s2_cast_done"):
        ctx.s2_cast_done = True
        for e_i in range(15):
            ctx.dma("gpsimd", gwbf[e_i], gw_ap[e_i], writes=(f"s2cg{e_i}",))
            ctx.dma("gpsimd", uwbf[e_i], uw_ap[e_i], writes=(f"s2cu{e_i}",))
            ctx.dma("gpsimd", dwbf[e_i], dw_ap[e_i], writes=(f"s2cd{e_i}",))
    rw = load_weight_bf(ctx, "s2_rw", w["router_w"])
    rb = load_weight_bf(ctx, "s2_rb", w["router_bias"][None, :])
    convT = np.ascontiguousarray(np.transpose(w["conv_w"], (2, 1, 0)), np.float32)
    cv = [load_weight_bf(ctx, f"s2_cv{k}", convT[k]) for k in range(3)]
    sup = load_weight_bf(ctx, "s2_sup", w["shared_up_w"])
    sdn = load_weight_bf(ctx, "s2_sdn", w["shared_down_w"])
    ident = load_const_sb(ctx, "ident_bf", aps["ident_bf"], [128, 128], BF16)
    ones = load_const_sb(ctx, "ones_row_bf", aps["ones_row_bf"], [1, 512], BF16)
    sel16 = load_const_sb(ctx, "s2_sel", aps["s2_sel"], [16, 16 * 128], BF16)

    xt = ctx.sb("s2_xt", [128, 1024], F32, n=2)
    xc = ctx.sb("s2_xc", [128, 1024], F32, n=2)
    sq = ctx.sb("s2_sq", [128, 1024], F32)
    st = ctx.sb("s2_st", [128, 8], F32, n=2)
    u_bf = ctx.sb("s2_u", [128, 1024], BF16, n=2)
    TH = 2048 + 2
    uT = ctx.sb("s2_uT", [128, 8 * TH], BF16)
    yac = ctx.sb("s2_yac", [128, 8 * 2048], BF16)
    gws = ctx.sb("s2_gws", [128, 8 * I], BF16)
    uws = ctx.sb("s2_uws", [128, 8 * I], BF16)
    dws = ctx.sb("s2_dws", [128, NI * 1024], BF16)
    h1s = ctx.sb("s2_h1s", [128, NI * 512], BF16, n=2)
    gbf = ctx.sb("s2_gbf", [128, 512], BF16)
    tmp = ctx.sb("s2_tmp", [128, 512], BF16, n=2)
    cb = ctx.sb("s2_cb", [128, 512], BF16, n=2)
    cb15 = ctx.sb("s2_cb15", [128, 512], BF16, n=2)
    prob = ctx.sb("s2_prob", [128, 16], F32)
    p2 = ctx.sb("s2_p2", [128, 16], F32)
    msk = ctx.sb("s2_msk", [128, 16], F32)
    comb = ctx.sb("s2_comb", [128, 16], BF16)
    combT = ctx.sb("s2_combT", [16, 2048], BF16)
    rst = ctx.sb("s2_rst", [128, 8], F32)
    ysb = xc
    x2 = xt

    ps_tr = ctx.ps("s2_ptr", [128, 1024], BF16, n=2)
    ps_g = ctx.ps("s2_pg", [128, 512], F32, n=2)
    ps_u = ctx.ps("s2_pu", [128, 512], F32, n=2)
    ps_y = ctx.ps("s2_py", [128, 512], F32, n=2)

    def uslice(ch, t0, n):
        return uT[:, ch * TH + 2 + t0: ch * TH + 2 + t0 + n]

    for ch in range(8):
        ctx.vec(lambda e, ch=ch: e.memset(uslice(ch, -2, 2), 0.0),
                writes=(f"s2pad{ch}",))

    for half in range(2):
        if half == 1:
            for ch in range(8):
                ctx.vec(lambda e, ch=ch: e.tensor_copy(
                    uslice(ch, -2, 2), uslice(ch, 2046, 2)),
                    reads=(f"s2uT{ch}_3",), writes=(f"s2pad{ch}",))
        def emit_tile2(it, half=half):
            i = half * 16 + it
            b = it % 2
            kxt = f"s2xt{b}"
            ctx.dma("sync", xt[b][:], x_ap[i * 128:(i + 1) * 128, :], writes=(kxt,))
            kp = f"s2l{b}"
            emit_ln(ctx, kp, kxt, xt[b], xc[b], sq, st[b], u_bf[b])
            for ch in range(8):
                p = ps_tr[ch % 2]
                rd = (f"s2pad{ch}",) if it == 0 else ()
                ctx.trans(p[:, 0:128], u_bf[b][:, ch * 128:(ch + 1) * 128], ident[:],
                          reads=(kp + "u", "ident_bf"), writes=(p.name,))
                ctx.vec(lambda e, p=p, ch=ch, it=it: e.tensor_copy(
                    uslice(ch, it * 128, 128), p[:, 0:128]),
                    reads=(p.name,) + rd, writes=(f"s2uT{ch}_{it // 4}",))
            pg = ps_g[0]
            for ch in range(8):
                ctx.mm(pg[:, 0:16], uslice(ch, it * 128, 128),
                       rw[:, ch * 16:(ch + 1) * 16], start=(ch == 0), stop=False,
                       reads=(f"s2uT{ch}_{it // 4}", "s2_rw"), writes=(pg.name,))
            ctx.mm(pg[:, 0:16], ones[:, 0:128], rb[:], start=False, stop=True,
                   reads=("s2_rb", "ones_row_bf"), writes=(pg.name,))
            ctx.vec(lambda e, pg=pg: e.reduce_max(rst[:, 0:1], pg[:, 0:16], axis=AX.X),
                    reads=(pg.name,), writes=("s2rst",))
            ctx.vec(lambda e: e.tensor_scalar(rst[:, 1:2], rst[:, 0:1], -1.0, None,
                                              ALU.mult),
                    reads=("s2rst",), writes=("s2rst",))
            ctx.sca(lambda e, pg=pg: e.activation(out=prob[:], in_=pg[:, 0:16], func=AF.Exp,
                                           bias=rst[:, 1:2], scale=1.0,
                                           accum_out=rst[:, 2:3]),
                    reads=(pg.name, "s2rst"), writes=("s2prob", "s2rst"))
            ctx.vec(lambda e: e.reciprocal(rst[:, 3:4], rst[:, 2:3]),
                    reads=("s2rst",), writes=("s2rst",))
            ctx.vec(lambda e: e.tensor_scalar(prob[:], prob[:], rst[:, 3:4], None,
                                              ALU.mult),
                    reads=("s2prob", "s2rst"), writes=("s2prob",))
            ctx.vec(lambda e: e.reduce_max(rst[:, 0:1], prob[:], axis=AX.X),
                    reads=("s2prob",), writes=("s2rst",))
            ctx.vec(lambda e: e.tensor_scalar(msk[:], prob[:], rst[:, 0:1], None,
                                              ALU.is_equal),
                    reads=("s2prob", "s2rst"), writes=("s2msk",))
            ctx.vec(lambda e: e.scalar_tensor_tensor(p2[:], msk[:], -1e30, prob[:],
                                                     ALU.mult, ALU.add),
                    reads=("s2msk", "s2prob"), writes=("s2p2",))
            ctx.vec(lambda e: e.reduce_max(rst[:, 1:2], p2[:], axis=AX.X),
                    reads=("s2p2",), writes=("s2rst",))
            ctx.vec(lambda e: e.tensor_scalar(msk[:], prob[:], rst[:, 1:2], None,
                                              ALU.is_ge),
                    reads=("s2prob", "s2rst"), writes=("s2msk",))
            ctx.vec(lambda e: e.tensor_tensor(rst[:, 4:5], rst[:, 0:1], rst[:, 1:2],
                                              ALU.add),
                    reads=("s2rst",), writes=("s2rst",))
            ctx.vec(lambda e: e.tensor_scalar(rst[:, 5:6], rst[:, 4:5], 1e-6, None,
                                              ALU.add),
                    reads=("s2rst",), writes=("s2rst",))
            ctx.vec(lambda e: e.reciprocal(rst[:, 6:7], rst[:, 5:6]),
                    reads=("s2rst",), writes=("s2rst",))
            ctx.vec(lambda e: e.tensor_tensor(p2[:], prob[:], msk[:], ALU.mult),
                    reads=("s2prob", "s2msk"), writes=("s2p2",))
            ctx.vec(lambda e: e.tensor_scalar(comb[:], p2[:], rst[:, 6:7], None,
                                              ALU.mult),
                    reads=("s2p2", "s2rst"), writes=("s2comb",))
            p = ps_tr[0]
            ctx.trans(p[0:16, 0:128], comb[:], ident[:],
                      reads=("s2comb", "ident_bf"), writes=(p.name,))
            ctx.vec(lambda e, p=p, it=it: e.tensor_copy(
                combT[0:16, it * 128:(it + 1) * 128], p[0:16, 0:128]),
                reads=(p.name,), writes=(f"s2combT{it // 4}",))

        def s2_gu_shared(tb):
            t0 = tb * 512
            gp = tb % 2
            tsl = slice(tb * 512, (tb + 1) * 512)
            uTk = tuple(f"s2uT{ch}_{tb}" for ch in range(8))
            if tb > 0:
                uTkm = uTk + tuple(f"s2uT{ch}_{tb - 1}" for ch in range(8))
            else:
                uTkm = uTk + tuple(f"s2pad{ch}" for ch in range(8))
            pu = ps_u[0]
            ctx.mm(pu[:], sel16[:, 15 * 128:16 * 128], combT[:, tsl],
                   start=True, stop=True,
                   reads=("s2_sel", f"s2combT{tb}"), writes=(pu.name,))
            ctx.vec(lambda e, pu=pu, gp=gp: e.tensor_copy(cb15[gp][:], pu[:]),
                    reads=(pu.name,), writes=(f"s2cb15{gp}",))
            for ich in range(NI):
                pg = ps_g[ich % 2]
                pu2 = ps_u[ich % 2]
                first = True
                for k in range(3):
                    for dch in range(8):
                        ctx.mm(pg[:], cv[k][:, dch * I + ich * 128: dch * I + (ich + 1) * 128],
                               uslice(dch, t0 + k - 2, 512),
                               start=first, stop=(k == 2 and dch == 7),
                               reads=uTkm + (f"s2_cv{k}",), writes=(pg.name,))
                        first = False
                for dch in range(8):
                    ctx.mm(pu2[:], sup[:, dch * I + ich * 128: dch * I + (ich + 1) * 128],
                           uslice(dch, t0, 512), start=(dch == 0), stop=(dch == 7),
                           reads=uTk + ("s2_sup",), writes=(pu2.name,))
                ctx.sca(lambda e, pg=pg: e.activation(out=gbf[:], in_=pg[:],
                                                      func=AF.Silu),
                        reads=(pg.name,), writes=("s2gbf",))
                ctx.vec(lambda e, pu2=pu2, ich=ich, gp=gp: e.tensor_tensor(
                    h1s[gp][:, ich * 512:(ich + 1) * 512], gbf[:], pu2[:],
                    ALU.mult),
                    reads=("s2gbf", pu2.name), writes=(f"s2h1{gp}_{ich}",))

        def s2_down_shared(tb):
            t0 = tb * 512
            gp = tb % 2
            for ddch in range(8):
                py = ps_y[ddch % 2]
                for ich in range(NI):
                    ctx.mm(py[:], sdn[:, ich * 1024 + ddch * 128: ich * 1024 + (ddch + 1) * 128],
                           h1s[gp][:, ich * 512:(ich + 1) * 512],
                           start=(ich == 0), stop=(ich == NI - 1),
                           reads=(f"s2h1{gp}_{ich}", "s2_sdn"), writes=(py.name,))
                ysl = slice(ddch * 2048 + tb * 512, ddch * 2048 + (tb + 1) * 512)
                ctx.vec(lambda e, py=py, ysl=ysl: e.tensor_copy(yac[:, ysl], py[:]),
                        reads=(py.name,), writes=(f"s2ya{ddch}_{tb}",))
            for ddch in range(8):
                tm = tmp[ddch % 2]
                ktm2 = f"s2tmp{ddch % 2}"
                ysl = slice(ddch * 2048 + tb * 512, ddch * 2048 + (tb + 1) * 512)
                ctx.vec(lambda e, tm=tm, ddch=ddch, t0=t0, gp=gp: e.tensor_tensor(
                    tm[:], uslice(ddch, t0, 512), cb15[gp][:], ALU.mult),
                    reads=(f"s2uT{ddch}_{tb}", f"s2cb15{gp}"), writes=(ktm2,))
                ctx.vec(lambda e, tm=tm, ysl=ysl: e.tensor_tensor(
                    yac[:, ysl], yac[:, ysl], tm[:], ALU.add),
                    reads=(f"s2ya{ddch}_{tb}", ktm2), writes=(f"s2ya{ddch}_{tb}",))

        def s2_gu_exp(g):
            e_i, tb = divmod(g, 4)
            gp = g % 2
            t0 = tb * 512
            tsl = slice(tb * 512, (tb + 1) * 512)
            uTk = tuple(f"s2uT{ch}_{tb}" for ch in range(8))
            wload = ((("s2gw", gwbf, gws, 8, "scalar", f"s2cg{e_i}"),
                      ("s2uw", uwbf, uws, 8, "sync", f"s2cu{e_i}"))
                     if tb == 0 else
                     ((("s2dw", dwbf, dws, NI, "sync", f"s2cd{e_i}"),)
                      if tb == 1 else ()))
            for nm, ap3, sbt, nch, q, ck in wload:
                src = ap3[e_i].rearrange("(c p) m -> p c m", p=128)
                dst = sbt[:].rearrange("p (c m) -> p c m", c=nch)
                ctx.dma(q, dst, src, reads=(ck,), writes=(nm,))
            pu = ps_u[0]
            ctx.mm(pu[:], sel16[:, e_i * 128:(e_i + 1) * 128], combT[:, tsl],
                   start=True, stop=True,
                   reads=("s2_sel", f"s2combT{tb}"), writes=(pu.name,))
            ctx.vec(lambda e, pu=pu, gp=gp: e.tensor_copy(cb[gp][:], pu[:]),
                    reads=(pu.name,), writes=(f"s2cb{gp}",))
            for ich in range(NI):
                pg = ps_g[ich % 2]
                pu2 = ps_u[ich % 2]
                for dch in range(8):
                    ctx.mm(pg[:], gws[:, dch * I + ich * 128: dch * I + (ich + 1) * 128],
                           uslice(dch, t0, 512), start=(dch == 0), stop=(dch == 7),
                           reads=uTk + ("s2gw",), writes=(pg.name,))
                for dch in range(8):
                    ctx.mm(pu2[:], uws[:, dch * I + ich * 128: dch * I + (ich + 1) * 128],
                           uslice(dch, t0, 512), start=(dch == 0), stop=(dch == 7),
                           reads=uTk + ("s2uw",), writes=(pu2.name,))
                ctx.sca(lambda e, pg=pg: e.activation(out=gbf[:], in_=pg[:],
                                                      func=AF.Silu),
                        reads=(pg.name,), writes=("s2gbf",))
                ctx.vec(lambda e, pu2=pu2, ich=ich, gp=gp: e.tensor_tensor(
                    h1s[gp][:, ich * 512:(ich + 1) * 512], gbf[:], pu2[:],
                    ALU.mult),
                    reads=("s2gbf", pu2.name), writes=(f"s2h1{gp}_{ich}",))

        def s2_down_exp(g):
            e_i, tb = divmod(g, 4)
            gp = g % 2
            for ddch in range(8):
                py = ps_y[ddch % 2]
                for ich in range(NI):
                    ctx.mm(py[:], dws[:, ich * 1024 + ddch * 128: ich * 1024 + (ddch + 1) * 128],
                           h1s[gp][:, ich * 512:(ich + 1) * 512],
                           start=(ich == 0), stop=(ich == NI - 1),
                           reads=(f"s2h1{gp}_{ich}", "s2dw"), writes=(py.name,))
                tm = tmp[ddch % 2]
                ktm2 = f"s2tmp{ddch % 2}"
                ysl = slice(ddch * 2048 + tb * 512, ddch * 2048 + (tb + 1) * 512)
                ctx.vec(lambda e, py=py, tm=tm, gp=gp: e.tensor_tensor(
                    tm[:], py[:], cb[gp][:], ALU.mult),
                    reads=(py.name, f"s2cb{gp}"), writes=(ktm2,))
                ctx.vec(lambda e, tm=tm, ysl=ysl: e.tensor_tensor(
                    yac[:, ysl], yac[:, ysl], tm[:], ALU.add),
                    reads=(f"s2ya{ddch}_{tb}", ktm2), writes=(f"s2ya{ddch}_{tb}",))

        def s2_gu(g):
            if g < 4:
                s2_gu_shared(g)
            else:
                s2_gu_exp(g - 4)

        def s2_down(g):
            if g < 4:
                s2_down_shared(g)
            else:
                s2_down_exp(g - 4)

        for it in range(4):
            emit_tile2(it)
        for g in range(4 + 15 * 4):
            if 1 <= g <= 3:
                for it in range(4 * g, 4 * g + 4):
                    emit_tile2(it)
            s2_gu(g)
            if g >= 1:
                s2_down(g - 1)
        s2_down(4 + 15 * 4 - 1)

        for it in range(16):
            i = half * 16 + it
            b = it % 2
            kx2 = f"s2x2{b}"
            ctx.dma("sync", x2[b][:], x_ap[i * 128:(i + 1) * 128, :],
                    writes=(kx2, f"s2xt{b}"))
            for ch in range(8):
                p = ps_tr[ch % 2]
                ctx.trans(p[:, 0:128], yac[:, ch * 2048 + it * 128: ch * 2048 + (it + 1) * 128],
                          ident[:],
                          reads=(f"s2ya{ch}_{it // 4}", "ident_bf"), writes=(p.name,))
                ctx.vec(lambda e, p=p, ch=ch, b=b: e.tensor_tensor(
                    ysb[b][:, ch * 128:(ch + 1) * 128], p[:, 0:128],
                    x2[b][:, ch * 128:(ch + 1) * 128], ALU.add),
                    reads=(p.name, kx2, f"s2xt{b}"),
                    writes=(f"s2y{b}_{ch}", f"s2l{b}xc"))
            ctx.dma("gpsimd", y_ap[i * 128:(i + 1) * 128, :], ysb[b][:],
                    reads=tuple(f"s2y{b}_{ch}" for ch in range(8)) +
                          (f"s2l{b}xc",))


def w_s2(inputs):
    return {k: inputs["L0_ffn_" + k] for k in
            ("router_w", "router_bias", "expert_gate_w", "expert_up_w",
             "expert_down_w", "conv_w", "shared_up_w", "shared_down_w")}



def s3_consts():
    kt = np.arange(128)[:, None]
    jj = np.arange(896)[None, :]
    c = {"s3_W": ((jj - 384) >= kt).astype(np.float32)}
    c["ones_col_bf"] = np.ones((128, 1), dtype=bf16)
    return c


def emit_S3(ctx, aps, x_ap, y_ap, w):
    wqT = load_weight_bf(ctx, "s3_wqT", w["Wq"].T)
    wkT = load_weight_bf(ctx, "s3_wkT", w["Wk"].T)
    wvT = load_weight_bf(ctx, "s3_wvT", w["Wv"].T)
    woT = load_weight_bf(ctx, "s3_woT", w["Wo"].T)
    nw_ap = ctx.ext_in("s3_nwf", np.ascontiguousarray(w["norm_w"][:, None],
                                                      np.float32))
    nw = ctx.sb("s3_nw_sb", [128, 1], F32)
    ctx.dma("gpsimd", nw[:], nw_ap, writes=("s3_nw",))
    ident = load_const_sb(ctx, "ident_bf", aps["ident_bf"], [128, 128], BF16)
    ones = load_const_sb(ctx, "ones_row_bf", aps["ones_row_bf"], [1, 512], BF16)
    onesc = load_const_sb(ctx, "ones_col_bf", aps["ones_col_bf"], [128, 1], BF16)
    Wm = load_const_sb(ctx, "s3_W", aps["s3_W"], [128, 896], F32)

    qT_sc = ctx.scratch("s3_qTsc", [8, 128, T], BF16)
    kT_sc = ctx.scratch("s3_kTsc", [8, 128, T], BF16)
    vtm_sc = ctx.scratch("s3_vtmsc", [T, D], BF16)
    onT_sc = ctx.scratch("s3_onTsc", [8, 128, T], BF16)

    xt = ctx.sb("s3_xt", [128, 1024], F32, n=2)
    xc = ctx.sb("s3_xc", [128, 1024], F32, n=2)
    sq = ctx.sb("s3_sq", [128, 1024], F32)
    st = ctx.sb("s3_st", [128, 8], F32, n=2)
    u_bf = ctx.sb("s3_u", [128, 1024], BF16, n=2)
    uT = ctx.sb("s3_uT", [128, 1024], BF16, n=2)
    ktm = ctx.sb("s3_ktm", [128, 1024], BF16, n=2)
    vtm = ctx.sb("s3_vtm", [128, 1024], BF16, n=2)
    qtm = ctx.sb("s3_qtm", [128, 1024], BF16, n=2)
    qTt = ctx.sb("s3_qTt", [128, 1024], BF16, n=2)
    kTt = ctx.sb("s3_kTt", [128, 1024], BF16, n=2)
    qTh = ctx.sb("s3_qTh", [128, T], BF16, n=2)
    kTh = ctx.sb("s3_kTh", [128, T], BF16, n=2)
    vth = ctx.sb("s3_vth", [128, 32 * 128], BF16, n=2)
    PT = ctx.sb("s3_PT", [128, 10 * 512], BF16)
    obf = ctx.sb("s3_obf", [128, 512], BF16, n=2)
    o1 = ctx.sb("s3_o1", [128, 512], BF16, n=2)
    o2 = ctx.sb("s3_o2", [128, 512], BF16, n=2)
    t1 = ctx.sb("s3_t1", [128, 512], BF16, n=2)
    onb = ctx.sb("s3_onb", [128, 512], BF16, n=2)
    rowf = ctx.sb("s3_rowf", [1, 512], F32, n=2)
    rdrow = ctx.sb("s3_rdrow", [1, 512], BF16, n=2)
    rrrow = ctx.sb("s3_rrrow", [1, 512], BF16, n=1)
    onTt = qtm
    ysb = xc
    x2 = xt

    ps_tr = ctx.ps("s3_ptr", [128, 1024], BF16, n=2)
    ps_S = ctx.ps("s3_S", [128, 512], F32, n=2)
    ps_O = ctx.ps("s3_O", [128, 512], F32, n=2)
    ps_D = ctx.ps("s3_D", [128, 512], F32, n=1)
    ps_B = ctx.ps("s3_B", [128, 512], F32, n=1)

    for i in range(NT):
        b = i % 2
        kxt = f"s3xt{b}"
        ctx.dma("sync", xt[b][:], x_ap[i * 128:(i + 1) * 128, :], writes=(kxt,))
        kp = f"s3l{b}"
        emit_ln(ctx, kp, kxt, xt[b], xc[b], sq, st[b], u_bf[b],
                uT=uT[b], ps_tr=ps_tr, ident=ident)
        kuT = kp + "uT"

        def proj_tm(wT, wkey, dest, dkey, b=b, kuT=kuT):
            for blk in range(2):
                ps = ps_S[blk]
                for ch in range(8):
                    ctx.mm(ps[:], uT[b][:, ch * 128:(ch + 1) * 128],
                           wT[:, ch * 1024 + blk * 512: ch * 1024 + blk * 512 + 512],
                           start=(ch == 0), stop=(ch == 7),
                           reads=(kuT, wkey), writes=(ps.name,))
                ctx.vec(lambda e, ps=ps, blk=blk, dest=dest: e.tensor_copy(
                    dest[:, blk * 512:(blk + 1) * 512], ps[:]),
                    reads=(ps.name,), writes=(dkey,))

        proj_tm(wqT, "s3_wqT", qtm[b], f"s3qtm{b}")
        proj_tm(wkT, "s3_wkT", ktm[b], f"s3ktm{b}")
        proj_tm(wvT, "s3_wvT", vtm[b], f"s3vtm{b}")
        for src, dest, dkey, skey in ((qtm[b], qTt[b], f"s3qTt{b}", f"s3qtm{b}"),
                                      (ktm[b], kTt[b], f"s3kTt{b}", f"s3ktm{b}")):
            for g in range(2):
                p = ps_tr[g]
                for ch in range(4 * g, 4 * g + 4):
                    ctx.trans(p[:, (ch % 4) * 128:(ch % 4 + 1) * 128],
                              src[:, ch * 128:(ch + 1) * 128], ident[:],
                              reads=(skey, "ident_bf"), writes=(p.name,))
                ctx.vec(lambda e, p=p, g=g, dest=dest: e.tensor_copy(
                    dest[:, g * 512:(g + 1) * 512], p[:, 0:512]),
                    reads=(p.name,), writes=(dkey,))
        tsl = slice(i * 128, (i + 1) * 128)
        ctx.dma("gpsimd", qT_sc[:, :, tsl].rearrange("c p m -> p c m"),
                qTt[b][:].rearrange("p (c m) -> p c m", c=8), reads=(f"s3qTt{b}",),
                writes=(f"sc3qT{i}",))
        ctx.dma("gpsimd", kT_sc[:, :, tsl].rearrange("c p m -> p c m"),
                kTt[b][:].rearrange("p (c m) -> p c m", c=8), reads=(f"s3kTt{b}",),
                writes=(f"sc3kT{i}",))
        ctx.dma("gpsimd", vtm_sc[tsl, :], vtm[b][:], reads=(f"s3vtm{b}",),
                writes=(f"sc3v{i}",))

    allq = tuple(f"sc3qT{i}" for i in range(NT))
    allk = tuple(f"sc3kT{i}" for i in range(NT))
    allv = tuple(f"sc3v{i}" for i in range(NT))

    def load_head(h):
        p = h % 2
        ctx.dma("sync", qTh[p][:], qT_sc[h], reads=allq, writes=(f"s3qTh{p}",))
        ctx.dma("sync", kTh[p][:], kT_sc[h], reads=allk, writes=(f"s3kTh{p}",))
        ctx.dma("sync", vth[p][:].rearrange("p (n m) -> p n m", n=32),
                vtm_sc[:, h * 128:(h + 1) * 128].rearrange("(n p) m -> p n m", p=128),
                reads=allv, writes=(f"s3vth{p}",))

    LAG = 8
    load_head(0)
    load_head(1)
    for h in range(8):
        p = h % 2
        kq, kk, kv = f"s3qTh{p}", f"s3kTh{p}", f"s3vth{p}"

        def post_bcast1(qb):
            q2 = qb % 2
            kd = f"s3dr{q2}"
            ctx.mm(ps_B[:], ones[:, 0:128], rdrow[q2][:], start=True, stop=True,
                   reads=(kd + "c", "ones_row_bf"), writes=(ps_B.name,))

        def post_ss(qb):
            q2 = qb % 2
            psO = ps_O[q2]
            kd, ko = f"s3dr{q2}", f"s3o{q2}"
            ctx.sca(lambda e, q2=q2, psO=psO: e.copy(obf[q2][:], psO[:]),
                    reads=(psO.name,), writes=(ko,))
            ctx.vec(lambda e, q2=q2: e.tensor_tensor(
                o1[q2][:], obf[q2][:], ps_B[:], ALU.mult),
                reads=(ko, ps_B.name), writes=(ko + "1",))
            ctx.sca(lambda e, q2=q2: e.activation(
                out=o2[q2][:], in_=o1[q2][:], func=AF.Square),
                reads=(ko + "1",), writes=(ko + "2",))
            ctx.mm(psO[0:1, 0:512], onesc[:, 0:1], o2[q2][:], start=True, stop=True,
                   reads=(ko + "2", "ones_col_bf"), writes=(psO.name,))
            ctx.vec(lambda e, q2=q2, psO=psO: e.tensor_scalar(
                rowf[q2][:], psO[0:1, 0:512], 1.0 / 128, EPS_LN, ALU.mult, ALU.add),
                reads=(psO.name, kd + "c"), writes=(kd + "d",))
            ctx.sca(lambda e, q2=q2: e.activation(
                out=rowf[q2][:], in_=rowf[q2][:], func=AF.Ln),
                reads=(kd + "d",), writes=(kd + "d2",))
            ctx.sca(lambda e, q2=q2: e.activation(
                out=rrrow[:], in_=rowf[q2][:], func=AF.Exp, scale=-0.5),
                reads=(kd + "d2", "s3rrtok"), writes=(kd + "e",))

        def post_fin(qb, h=h):
            q2 = qb % 2
            kd, ko = f"s3dr{q2}", f"s3o{q2}"
            ctx.mm(ps_B[:], ones[:, 0:128], rrrow[:], start=True, stop=True,
                   reads=(kd + "e", "ones_row_bf"), writes=(ps_B.name, "s3rrtok"))
            ctx.vec(lambda e, q2=q2: e.tensor_tensor(
                t1[q2][:], o1[q2][:], ps_B[:], ALU.mult),
                reads=(ko + "1", ps_B.name), writes=(ko + "3",))
            ctx.gps(lambda e, q2=q2: e.tensor_scalar(
                onb[q2][:], t1[q2][:], nw[:, 0:1], None, ALU.mult),
                reads=(ko + "3", "s3_nw"), writes=(ko + "4",))
            ctx.dma("sync", onT_sc[h][:, qb * 512:(qb + 1) * 512], onb[q2][:],
                    reads=(ko + "4",), writes=(f"s3onT{h}_{qb}",))

        def emit_block(qb, p=p, kq=kq, kk=kk, kv=kv):
            nkt = 4 * qb + 4
            psO = ps_O[qb % 2]
            psD = ps_D

            def qk(ktb):
                di = ktb - 4 * qb
                c0 = max(0, di) * 128
                sb2 = ps_S[ktb % 2]
                ctx.mm(sb2[:, c0:512], kTh[p][:, ktb * 128:(ktb + 1) * 128],
                       qTh[p][:, qb * 512 + c0:(qb + 1) * 512],
                       start=True, stop=True, reads=(kq, kk), writes=(sb2.name,))
                kpt = f"s3pt{ktb % 10}"
                pts = PT[:, (ktb % 10) * 512:(ktb % 10 + 1) * 512]
                if di >= 0:
                    if c0 > 0:
                        ctx.gps(lambda e, pts=pts, c0=c0: e.memset(
                            pts[:, 0:c0], 0.0), writes=(kpt,))
                    ctx.vec(lambda e, pts=pts, sb2=sb2, c0=c0, di=di:
                            e.tensor_tensor(pts[:, c0:512], sb2[:, c0:512],
                                            Wm[:, 384:896 - di * 128], ALU.mult),
                            reads=(sb2.name, "s3_W"), writes=(kpt,))
                elif ktb % 2 == 0:
                    ctx.sca(lambda e, pts=pts, sb2=sb2: e.copy(pts[:], sb2[:]),
                            reads=(sb2.name,), writes=(kpt,))
                else:
                    ctx.vec(lambda e, pts=pts, sb2=sb2: e.tensor_copy(
                        pts[:], sb2[:]), reads=(sb2.name,), writes=(kpt,))

            def den(ktb):
                kpt = f"s3pt{ktb % 10}"
                pts = PT[:, (ktb % 10) * 512:(ktb % 10 + 1) * 512]
                ctx.mm(psD[0:1, 0:512], onesc[:, 0:1], pts,
                       start=(ktb == 0), stop=(ktb == nkt - 1),
                       reads=(kpt, "ones_col_bf"), writes=(psD.name,))

            def av(ktb):
                kpt = f"s3pt{ktb % 10}"
                pts = PT[:, (ktb % 10) * 512:(ktb % 10 + 1) * 512]
                ctx.mm(psO[:], vth[p][:, ktb * 128:(ktb + 1) * 128], pts,
                       start=(ktb == 0), stop=(ktb == nkt - 1),
                       reads=(kpt, kv), writes=(psO.name,))

            kfin = max(10, nkt - 4)
            for k in range(nkt + LAG):
                if k < nkt:
                    qk(k)
                if k == 5 and qb >= 1:
                    post_ss(qb - 1)
                if 2 <= k < nkt + 2:
                    den(k - 2)
                if k == kfin and qb >= 1:
                    post_fin(qb - 1)
                if k >= LAG:
                    av(k - LAG)
            q2 = qb % 2
            kd = f"s3dr{q2}"
            ctx.vec(lambda e, q2=q2: e.tensor_scalar(
                rowf[q2][:], psD[0:1, 0:512], -1.0, 1.0, ALU.mult, ALU.max),
                reads=(psD.name, kd + "e"), writes=(kd,))
            ctx.vec(lambda e, q2=q2: e.tensor_tensor(
                rowf[q2][:], psD[0:1, 0:512], rowf[q2][:], ALU.max),
                reads=(psD.name, kd), writes=(kd + "b",))
            ctx.sca(lambda e, q2=q2: e.activation(
                out=rowf[q2][:], in_=rowf[q2][:], func=AF.Ln),
                reads=(kd + "b",), writes=(kd + "b2",))
            ctx.sca(lambda e, q2=q2: e.activation(
                out=rdrow[q2][:], in_=rowf[q2][:], func=AF.Exp, scale=-1.0),
                reads=(kd + "b2",), writes=(kd + "c",))

        for qb in range(8):
            if qb >= 1:
                post_bcast1(qb - 1)
            emit_block(qb)
        post_bcast1(7)
        post_ss(7)
        post_fin(7)
        if h + 2 < 8:
            load_head(h + 2)

    for i in range(NT):
        b = i % 2
        kx2 = f"s3x2{b}"
        tsl = slice(i * 128, (i + 1) * 128)
        qb = i // 4
        ctx.dma("sync", x2[b][:], x_ap[tsl, :], writes=(kx2, f"s3xt{b}"))
        ctx.dma("sync", onTt[b][:].rearrange("p (c m) -> p c m", c=8),
                onT_sc[:, :, tsl].rearrange("c p m -> p c m"),
                reads=tuple(f"s3onT{hh}_{qb}" for hh in range(8)),
                writes=(f"s3onTt{b}", f"s3qtm{b}"))
        for blk in range(2):
            ps = ps_S[blk]
            for ch in range(8):
                ctx.mm(ps[:], onTt[b][:, ch * 128:(ch + 1) * 128],
                       woT[:, ch * 1024 + blk * 512: ch * 1024 + blk * 512 + 512],
                       start=(ch == 0), stop=(ch == 7),
                       reads=(f"s3onTt{b}", "s3_woT"), writes=(ps.name,))
            ctx.vec(lambda e, ps=ps, b=b, blk=blk: e.tensor_tensor(
                ysb[b][:, blk * 512:(blk + 1) * 512], ps[:],
                x2[b][:, blk * 512:(blk + 1) * 512], ALU.add),
                reads=(ps.name, kx2), writes=(f"s3y{b}_{blk}", f"s3l{b}xc"))
        ctx.dma("gpsimd", y_ap[tsl, :], ysb[b][:],
                reads=(f"s3y{b}_0", f"s3y{b}_1"))



def w_s3(inputs):
    return {k: inputs["L0_attn_b_" + k] for k in ("Wq", "Wk", "Wv", "Wo", "norm_w")}



EXP_SHIFT = 20.0


def s7_consts():
    kt = np.arange(128)[:, None]
    q = np.arange(128)[None, :]
    return {"s7_triadd": np.where(q >= kt, 0.0, -1e30).astype(np.float32)}


def emit_attn_core(ctx, pref, aps, scale, ym_sc, load_pair_fn):
    onesq = load_const_sb(ctx, "ones_sq_bf", aps["ones_sq_bf"], [128, 128], BF16)
    tri = ctx.s7_tri
    LAG, NPT = 4, 8
    QTp = ctx.sb(pref + "_QTp", [128, T], BF16, n=2)
    KTp = ctx.sb(pref + "_KTp", [128, T], BF16, n=2)
    vA = ctx.sb(pref + "_vA", [128, 32 * 65], BF16, n=2)
    vB = ctx.sb(pref + "_vB", [128, 32 * 65], BF16, n=2)
    PT = ctx.sb(pref + "_PT", [128, NPT * 512], BF16)
    rbrow = ctx.sb(pref + "_rbrow", [128, 512], BF16, n=2)
    rb = ctx.sb(pref + "_rb", [64, 512], BF16, n=2)
    ym = ctx.sb(pref + "_ym", [64, 512], BF16, n=2)
    shift = ctx.sb(pref + "_shift", [128, 2], F32)
    ctx.vec(lambda e: e.memset(shift[:, 0:1], -EXP_SHIFT), writes=(pref + "shift",))
    ctx.vec(lambda e: e.memset(shift[:, 1:2], scale), writes=(pref + "shift",))
    ps_S = ctx.attn_ps_s
    ps_O = ctx.ps(pref + "_pO", [128, 512], F32, n=2)
    ps_B = ctx.ps(pref + "_pB", [128, 512], F32)
    for par in range(2):
        for vv, kv in ((vA[par], f"{pref}vA{par}"), (vB[par], f"{pref}vB{par}")):
            ctx.vec(lambda e, vv=vv: e.memset(
                vv[:].rearrange("p (n m) -> p n m", m=65)[:, :, 64:65], 1.0),
                writes=(kv + "_ones",))

    load_pair_fn(0, QTp[0], KTp[0], vA[0], vB[0], 0, 0)
    load_pair_fn(0, QTp[0], KTp[0], vA[0], vB[0], 0, 1)
    for hp in range(8):
        par = hp % 2
        kQ, kK = f"{pref}QTp{par}", f"{pref}KTp{par}"
        blocks = []
        jobs = []
        for qb in range(8):
            for nm in ("A", "B"):
                j = len(jobs)
                nkt = 4 * qb + 4
                for ktb in range(nkt):
                    blocks.append((j, qb, nm, ktb, ktb == 0, ktb == nkt - 1))
                jobs.append((qb, nm, len(blocks) - 1))
        nblocks = len(blocks)
        end2job = {e: j for j, (_, _, e) in enumerate(jobs)}

        def emit_qk_exp(t):
            j, qb, nm, ktb, _, _ = blocks[t]
            di = ktb - 4 * qb
            c0 = max(0, di) * 128
            base = 0 if nm == "A" else 64
            psl = slice(base, base + 64)
            ps = ps_S[t % 2]
            ctx.mm(ps[:, c0:512], KTp[par][psl, ktb * 128:(ktb + 1) * 128],
                   QTp[par][psl, qb * 512 + c0:(qb + 1) * 512],
                   start=True, stop=True, reads=(kQ, kK), writes=(ps.name,))
            if di >= 0:
                ctx.vec(lambda e, ps=ps, c0=c0: e.tensor_tensor(
                    ps[:, c0:c0 + 128], ps[:, c0:c0 + 128], tri[:], ALU.add),
                    reads=(ps.name, "s7_triadd"), writes=(ps.name,))
            kpt = f"{pref}pt{t % NPT}"
            pt = PT[:, (t % NPT) * 512:(t % NPT + 1) * 512]
            if c0 > 0:
                ctx.vec(lambda e, pt=pt, c0=c0: e.memset(pt[:, 0:c0], 0.0),
                        writes=(kpt,))
            ctx.sca(lambda e, ps=ps, pt=pt, c0=c0: e.activation(
                out=pt[:, c0:512], in_=ps[:, c0:512], func=AF.Exp,
                bias=shift[:, 0:1], scale=shift[:, 1:2]),
                reads=(ps.name, pref + "shift"), writes=(kpt,))

        def emit_av(u):
            j, qb, nm, ktb, st0, st1 = blocks[u]
            vv = vA[par] if nm == "A" else vB[par]
            kv = f"{pref}v{nm}{par}"
            kpt = f"{pref}pt{u % NPT}"
            pt = PT[:, (u % NPT) * 512:(u % NPT + 1) * 512]
            po = ps_O[j % 2]
            ctx.mm(po[0:65, :], vv[:, ktb * 65:(ktb + 1) * 65], pt[:],
                   start=st0, stop=st1,
                   reads=(kpt, kv, kv + "_ones"), writes=(po.name,))

        def post_r(j):
            j2 = j % 2
            po = ps_O[j2]
            krb = f"{pref}rb{j2}"
            ctx.vec(lambda e, j2=j2, po=po: e.reciprocal(
                rbrow[j2][64:65, 0:512], po[64:65, 0:512]),
                reads=(po.name,), writes=(krb + "r",))

        def post_m(j, hp=hp):
            qb, nm, _ = jobs[j]
            j2 = j % 2
            po = ps_O[j2]
            krb, kym = f"{pref}rb{j2}", f"{pref}ym{j2}"
            ctx.mm(ps_B[:], onesq[64:65, 0:128], rbrow[j2][64:65, 0:512],
                   start=True, stop=True,
                   reads=(krb + "r", "ones_sq_bf"), writes=(ps_B.name,))
            ctx.vec(lambda e, j2=j2: e.tensor_copy(rb[j2][:], ps_B[0:64, 0:512]),
                    reads=(ps_B.name,), writes=(krb,))
            ctx.vec(lambda e, j2=j2, po=po: e.tensor_tensor(
                ym[j2][:], po[0:64, 0:512], rb[j2][:], ALU.mult),
                reads=(po.name, krb), writes=(kym,))
            ctx.dma("gpsimd", ym_sc[2 * hp + (0 if nm == "A" else 1), :,
                                    qb * 512:(qb + 1) * 512],
                    ym[j2][:], reads=(kym,), writes=(f"sc_ym{hp}_{qb}{nm}",))

        for t in range(nblocks + LAG + 5):
            if t < nblocks:
                emit_qk_exp(t)
            if hp + 1 < 8 and t in (16, 112):
                load_pair_fn(hp + 1, QTp[1 - par], KTp[1 - par], vA[1 - par],
                             vB[1 - par], 1 - par, 0 if t == 16 else 1)
            u = t - LAG
            if 0 <= u < nblocks:
                emit_av(u)
            e1 = t - LAG - 1
            if 0 <= e1 < nblocks and e1 in end2job:
                post_r(end2job[e1])
            e2 = t - LAG - 4
            if 0 <= e2 < nblocks and e2 in end2job:
                post_m(end2job[e2])


def emit_S7(ctx, aps, x_ap, y_ap, w):
    wqT = load_weight_bf(ctx, "s7_wqT", w["wq"].T)
    wkT = load_weight_bf(ctx, "s7_wkT", w["wk"].T)
    wvT = load_weight_bf(ctx, "s7_wvT", w["wv"].T)
    wpT = load_weight_bf(ctx, "s7_wpT", w["wp"].T)
    bq = load_weight_bf(ctx, "s7_bq", w["bq"][None, :])
    bk = load_weight_bf(ctx, "s7_bk", w["bk"][None, :])
    bv = load_weight_bf(ctx, "s7_bv", w["bv"][None, :])
    bp = load_weight_bf(ctx, "s7_bp", w["bp"][None, :])
    ident = load_const_sb(ctx, "ident_bf", aps["ident_bf"], [128, 128], BF16)
    ones = load_const_sb(ctx, "ones_row_bf", aps["ones_row_bf"], [1, 512], BF16)
    onesc = load_const_sb(ctx, "ones_col_bf", aps["ones_col_bf"], [128, 1], BF16)
    tri = load_const_sb(ctx, "s7_triadd", aps["s7_triadd"], [128, 128], F32)
    ctx.s7_ident, ctx.s7_ones, ctx.s7_onesc, ctx.s7_tri = ident, ones, onesc, tri

    qT_sc = ctx.scratch("s7_qTsc", [8, 128, T], BF16)
    kT_sc = ctx.scratch("s7_kTsc", [8, 128, T], BF16)
    v_sc = ctx.scratch("s7_vsc", [T, D], BF16)
    ym_sc = ctx.scratch("s7_ymsc", [16, 64, T], BF16)

    xt = ctx.sb("s7_xt", [128, 1024], F32, n=2)
    xc = ctx.sb("s7_xc", [128, 1024], F32, n=2)
    sq = ctx.sb("s7_sq", [128, 1024], F32)
    st = ctx.sb("s7_st", [128, 8], F32, n=2)
    u_bf = ctx.sb("s7_u", [128, 1024], BF16, n=2)
    uT = ctx.sb("s7_uT", [128, 1024], BF16, n=2)
    qtm = ctx.sb("s7_qtm", [128, 1024], BF16, n=2)
    ktm = ctx.sb("s7_ktm", [128, 1024], BF16, n=2)
    vtm = ctx.sb("s7_vtm", [128, 1024], BF16, n=2)
    qTt = ctx.sb("s7_qTt", [128, 1024], BF16, n=2)
    kTt = ctx.sb("s7_kTt", [128, 1024], BF16, n=2)
    ymTt = qtm
    ysb = xc
    x2 = xt

    ps_tr = ctx.ps("s7_ptr", [128, 1024], BF16, n=2)
    ps_a = ctx.ps("s7_pa", [128, 512], F32)
    ps_b = ctx.ps("s7_pb", [128, 512], F32)

    for i in range(NT):
        b = i % 2
        kxt = f"s7xt{b}"
        ctx.dma("sync", xt[b][:], x_ap[i * 128:(i + 1) * 128, :], writes=(kxt,))
        kp = f"s7l{b}"
        emit_ln(ctx, kp, kxt, xt[b], xc[b], sq, st[b], u_bf[b],
                uT=uT[b], ps_tr=ps_tr, ident=ident)
        kuT = kp + "uT"

        def proj_tm(wT, wkey, bias, bkey, dest, dkey, b=b, kuT=kuT):
            for blk in range(2):
                ps = ps_a if blk == 0 else ps_b
                for ch in range(8):
                    ctx.mm(ps[:], uT[b][:, ch * 128:(ch + 1) * 128],
                           wT[:, ch * 1024 + blk * 512: ch * 1024 + blk * 512 + 512],
                           start=(ch == 0), stop=False,
                           reads=(kuT, wkey), writes=(ps.name,))
                ctx.mm(ps[:], ones[:, 0:128], bias[:, blk * 512:(blk + 1) * 512],
                       start=False, stop=True,
                       reads=(bkey, "ones_row_bf"), writes=(ps.name,))
                ctx.vec(lambda e, ps=ps, blk=blk, dest=dest: e.tensor_copy(
                    dest[:, blk * 512:(blk + 1) * 512], ps[:]),
                    reads=(ps.name,), writes=(dkey,))

        proj_tm(wqT, "s7_wqT", bq, "s7_bq", qtm[b], f"s7qtm{b}")
        proj_tm(wkT, "s7_wkT", bk, "s7_bk", ktm[b], f"s7ktm{b}")
        proj_tm(wvT, "s7_wvT", bv, "s7_bv", vtm[b], f"s7vtm{b}")
        for src, dest, dkey, skey in ((qtm[b], qTt[b], f"s7qTt{b}", f"s7qtm{b}"),
                                      (ktm[b], kTt[b], f"s7kTt{b}", f"s7ktm{b}")):
            for g in range(2):
                p = ps_tr[g]
                for ch in range(4 * g, 4 * g + 4):
                    ctx.trans(p[:, (ch % 4) * 128:(ch % 4 + 1) * 128],
                              src[:, ch * 128:(ch + 1) * 128], ident[:],
                              reads=(skey, "ident_bf"), writes=(p.name,))
                ctx.vec(lambda e, p=p, g=g, dest=dest: e.tensor_copy(
                    dest[:, g * 512:(g + 1) * 512], p[:, 0:512]),
                    reads=(p.name,), writes=(dkey,))
        tsl = slice(i * 128, (i + 1) * 128)
        ctx.dma("gpsimd", qT_sc[:, :, tsl].rearrange("c p m -> p c m"),
                qTt[b][:].rearrange("p (c m) -> p c m", c=8), reads=(f"s7qTt{b}",),
                writes=(f"sc7qT{i}",))
        ctx.dma("gpsimd", kT_sc[:, :, tsl].rearrange("c p m -> p c m"),
                kTt[b][:].rearrange("p (c m) -> p c m", c=8), reads=(f"s7kTt{b}",),
                writes=(f"sc7kT{i}",))
        ctx.dma("gpsimd", v_sc[tsl, :], vtm[b][:], reads=(f"s7vtm{b}",),
                writes=(f"sc7v{i}",))

    allq7 = tuple(f"sc7qT{i}" for i in range(NT))
    allk7 = tuple(f"sc7kT{i}" for i in range(NT))
    allv7 = tuple(f"sc7v{i}" for i in range(NT))

    def load_pair(hp, QTp, KTp, vA, vB, par, stage):
        if stage != 0:
            return
        ctx.dma("sync", QTp[:], qT_sc[hp], reads=allq7, writes=(f"s7QTp{par}",))
        ctx.dma("sync", KTp[:], kT_sc[hp], reads=allk7, writes=(f"s7KTp{par}",))
        for vv, h, kv in ((vA, 2 * hp, f"s7vA{par}"), (vB, 2 * hp + 1,
                                                       f"s7vB{par}")):
            ctx.dma("sync", vv[:].rearrange("p (n m) -> p n m", m=65)[:, :, 0:64],
                    v_sc.rearrange("(n p) m -> p n m", p=128)
                    [:, :, h * 64:(h + 1) * 64],
                    reads=allv7, writes=(kv,))

    ctx.attn_ps_s = (ps_a, ps_b)
    emit_attn_core(ctx, "s7", aps, 1.0 / 64.0, ym_sc, load_pair)

    for i in range(NT):
        b = i % 2
        kx2 = f"s7x2{b}"
        tsl = slice(i * 128, (i + 1) * 128)
        ctx.dma("sync", x2[b][:], x_ap[tsl, :], writes=(kx2, f"s7xt{b}"))
        ctx.dma("sync", ymTt[b][:].rearrange("p (c m) -> p c m", c=8),
                ym_sc.rearrange("(c h) p t -> (h p) c t", h=2)[:, :, tsl],
                reads=tuple(f"sc_ym{hp}_{i // 4}{nm}" for hp in range(8)
                            for nm in "AB"),
                writes=(f"s7ymTt{b}", f"s7qtm{b}"))
        for blk in range(2):
            ps = ps_a if blk == 0 else ps_b
            for ch in range(8):
                ctx.mm(ps[:], ymTt[b][:, ch * 128:(ch + 1) * 128],
                       wpT[:, ch * 1024 + blk * 512: ch * 1024 + blk * 512 + 512],
                       start=(ch == 0), stop=False,
                       reads=(f"s7ymTt{b}", "s7_wpT"), writes=(ps.name,))
            ctx.mm(ps[:], ones[:, 0:128], bp[:, blk * 512:(blk + 1) * 512],
                   start=False, stop=True,
                   reads=("s7_bp", "ones_row_bf"), writes=(ps.name,))
            ctx.vec(lambda e, ps=ps, b=b, blk=blk: e.tensor_tensor(
                ysb[b][:, blk * 512:(blk + 1) * 512], ps[:],
                x2[b][:, blk * 512:(blk + 1) * 512], ALU.add),
                reads=(ps.name, kx2), writes=(f"s7y{b}_{blk}", f"s7l{b}xc"))
        ctx.dma("gpsimd", y_ap[tsl, :], ysb[b][:],
                reads=(f"s7y{b}_0", f"s7y{b}_1"))


def w_s7(inputs):
    return {k: inputs["L1_attn_b_" + k] for k in
            ("wq", "bq", "wk", "bk", "wv", "bv", "wp", "bp")}



def emit_S1(ctx, aps, x_ap, y_ap, w):
    wqkv = load_weight_bf(ctx, "s1_wqkv", w["Wqkv"])
    bqkv = load_weight_bf(ctx, "s1_bqkv", w["bqkv"][None, :])
    wproj = load_weight_bf(ctx, "s1_wproj", w["Wproj"])
    bproj = load_weight_bf(ctx, "s1_bproj", w["bproj"][None, :])
    ident = load_const_sb(ctx, "ident_bf", aps["ident_bf"], [128, 128], BF16)
    ones = load_const_sb(ctx, "ones_row_bf", aps["ones_row_bf"], [1, 512], BF16)
    onesc = load_const_sb(ctx, "ones_col_bf", aps["ones_col_bf"], [128, 1], BF16)
    tri = load_const_sb(ctx, "s7_triadd", aps["s7_triadd"], [128, 128], F32)
    ctx.s7_ident, ctx.s7_ones, ctx.s7_onesc, ctx.s7_tri = ident, ones, onesc, tri

    xp_sc = ctx.scratch("s1_xpsc", [T, 3 * D], BF16)
    ym_sc = ctx.scratch("s1_ymsc", [16, 64, T], BF16)

    xt = ctx.sb("s1_xt", [128, 1024], F32, n=2)
    xc = ctx.sb("s1_xc", [128, 1024], F32, n=2)
    sq = ctx.sb("s1_sq", [128, 1024], F32)
    st = ctx.sb("s1_st", [128, 8], F32, n=2)
    u_bf = ctx.sb("s1_u", [128, 1024], BF16, n=2)
    uT = ctx.sb("s1_uT", [128, 1024], BF16, n=2)
    xp = ctx.sb("s1_xp", [128, 3072], BF16, n=2)
    qAB = ctx.sb("s1_qAB", [128, 32 * 128], BF16)
    kAB = ctx.sb("s1_kAB", [128, 32 * 128], BF16)
    ymTt = uT
    ysb = xc
    x2 = xt

    ps_tr = ctx.ps("s1_ptr", [128, 1024], BF16, n=2)
    ps_a = ctx.ps("s1_pa", [128, 512], F32)
    ps_b = ctx.ps("s1_pb", [128, 512], F32)

    for i in range(NT):
        b = i % 2
        kxt = f"s1xt{b}"
        ctx.dma("sync", xt[b][:], x_ap[i * 128:(i + 1) * 128, :], writes=(kxt,))
        kp = f"s1l{b}"
        emit_ln(ctx, kp, kxt, xt[b], xc[b], sq, st[b], u_bf[b],
                uT=uT[b], ps_tr=ps_tr, ident=ident)
        kuT = kp + "uT"
        for blk in range(6):
            ps = ps_a if blk % 2 == 0 else ps_b
            for ch in range(8):
                ctx.mm(ps[:], uT[b][:, ch * 128:(ch + 1) * 128],
                       wqkv[:, ch * 3072 + blk * 512: ch * 3072 + blk * 512 + 512],
                       start=(ch == 0), stop=False,
                       reads=(kuT, "s1_wqkv"), writes=(ps.name,))
            ctx.mm(ps[:], ones[:, 0:128], bqkv[:, blk * 512:(blk + 1) * 512],
                   start=False, stop=True,
                   reads=("s1_bqkv", "ones_row_bf"), writes=(ps.name,))
            ctx.vec(lambda e, ps=ps, blk=blk, b=b: e.tensor_copy(
                xp[b][:, blk * 512:(blk + 1) * 512], ps[:]),
                reads=(ps.name,), writes=(f"s1xp{b}",))
        ctx.dma("gpsimd", xp_sc[i * 128:(i + 1) * 128, :], xp[b][:],
                reads=(f"s1xp{b}",), writes=(f"sc1xp{i}",))

    xp_r = xp_sc.rearrange("(h kb phi) (s plo d) -> h s phi plo kb d",
                           h=16, kb=32, phi=8, s=3, plo=16, d=64)

    allxp = tuple(f"sc1xp{i}" for i in range(NT))

    def load_pair(hp, QTp, KTp, vA, vB, par, stage):
        if stage == 0:
            for sect, buf, bk in ((0, qAB, "s1qAB"), (1, kAB, "s1kAB")):
                for half, h in ((0, 2 * hp), (1, 2 * hp + 1)):
                    bview = buf[:].rearrange("p (n m) -> p n m", m=128)
                    for phi in range(8):
                        ctx.dma("sync",
                                bview[phi * 16:(phi + 1) * 16, :,
                                      half * 64:(half + 1) * 64],
                                xp_r[h, sect, phi], reads=allxp, writes=(bk,))
            for vv, h, kv in ((vA, 2 * hp, f"s1vA{par}"), (vB, 2 * hp + 1,
                                                           f"s1vB{par}")):
                vview = vv[:].rearrange("p (n m) -> p n m", m=65)
                for phi in range(8):
                    ctx.dma("sync", vview[phi * 16:(phi + 1) * 16, :, 0:64],
                            xp_r[h, 2, phi], reads=allxp, writes=(kv,))
            return
        for sect, buf, bk in ((0, qAB, "s1qAB"), (1, kAB, "s1kAB")):
            dst = QTp if sect == 0 else KTp
            dk = f"s1QTp{par}" if sect == 0 else f"s1KTp{par}"
            for kb in range(32):
                p = ps_tr[kb % 2]
                ctx.trans(p[:, 0:128], buf[:, kb * 128:(kb + 1) * 128], ident[:],
                          reads=(bk, "ident_bf"), writes=(p.name,))
                ctx.vec(lambda e, p=p, kb=kb, dst=dst: e.tensor_copy(
                    dst[:, kb * 128:(kb + 1) * 128], p[:, 0:128]),
                    reads=(p.name,), writes=(dk,))

    ctx.attn_ps_s = (ps_a, ps_b)
    emit_attn_core(ctx, "s1", aps, 1.0 / 8.0, ym_sc, load_pair)

    for i in range(NT):
        b = i % 2
        kx2 = f"s1x2{b}"
        tsl = slice(i * 128, (i + 1) * 128)
        ctx.dma("sync", x2[b][:], x_ap[tsl, :], writes=(kx2, f"s1xt{b}"))
        ctx.dma("sync", ymTt[b][:].rearrange("p (c m) -> p c m", c=8),
                ym_sc.rearrange("(c h) p t -> (h p) c t", h=2)[:, :, tsl],
                reads=tuple(f"sc_ym{hp}_{i // 4}{nm}" for hp in range(8)
                            for nm in "AB"),
                writes=(f"s1ymTt{b}", f"s1l{b}uT"))
        for blk in range(2):
            ps = ps_a if blk == 0 else ps_b
            for ch in range(8):
                ctx.mm(ps[:], ymTt[b][:, ch * 128:(ch + 1) * 128],
                       wproj[:, ch * 1024 + blk * 512: ch * 1024 + blk * 512 + 512],
                       start=(ch == 0), stop=False,
                       reads=(f"s1ymTt{b}", "s1_wproj"), writes=(ps.name,))
            ctx.mm(ps[:], ones[:, 0:128], bproj[:, blk * 512:(blk + 1) * 512],
                   start=False, stop=True,
                   reads=("s1_bproj", "ones_row_bf"), writes=(ps.name,))
            ctx.vec(lambda e, ps=ps, b=b, blk=blk: e.tensor_tensor(
                ysb[b][:, blk * 512:(blk + 1) * 512], ps[:],
                x2[b][:, blk * 512:(blk + 1) * 512], ALU.add),
                reads=(ps.name, kx2), writes=(f"s1y{b}_{blk}", f"s1l{b}xc"))
        ctx.dma("gpsimd", y_ap[tsl, :], ysb[b][:],
                reads=(f"s1y{b}_0", f"s1y{b}_1"))


def w_s1(inputs):
    return {k: inputs["L0_attn_a_" + k] for k in ("Wqkv", "bqkv", "Wproj", "bproj")}



def build_program(name, emitters, const_fns, weights):
    ctx = Ctx(name)
    consts = base_consts()
    for f in const_fns:
        consts.update(f())
    x_ap = ctx.ext_in_percore("x", [T, D])
    y_ap = ctx.ext_out("y", [T, D])
    with ctx.stack:
        aps = declare_consts(ctx, consts)
        block = ctx.stack.enter_context(ctx.nc.Block())
        for emit_fn, w in emitters:
            emit_fn(ctx, aps, x_ap, y_ap, w)
        ctx.sch.finish(block, ctx.stack)
    return ctx


def run_program(ctx, x_all):
    in_maps = []
    for i in range(N_CORES):
        m = dict(ctx.inputs)
        m["x"] = np.ascontiguousarray(x_all[i])
        in_maps.append(m)
    res = run_bass_kernel_spmd(ctx.nc, in_maps, core_ids=list(range(N_CORES)))
    return np.stack([r["y"] for r in res.results])


def w_s5(inputs):
    return {k: inputs["L1_attn_a_" + k] for k in
            ("Wk", "bk", "Wv", "bv", "Wq", "bq", "Wo", "bo")}


SEQ = (("s1", emit_S1, w_s1), ("s2", emit_S2, w_s2), ("s3", emit_S3, w_s3),
       ("s2", emit_S2, w_s2), ("s5", emit_S5, w_s5), ("s6", emit_S6, w_s6),
       ("s7", emit_S7, w_s7), ("s6", emit_S6, w_s6))


def build_fused(inputs):
    ctx = Ctx("mk", fused=True)
    consts = base_consts()
    for f in (s5_consts, s6_consts, s2_consts, s3_consts, s7_consts):
        consts.update(f())
    x_ap = ctx.ext_in_percore("x", [T, D])
    y_ap = ctx.ext_out("y", [T, D])
    t0_ap = ctx.scratch("chain0", [T, D], F32)
    t1_ap = ctx.scratch("chain1", [T, D], F32)
    with ctx.stack:
        ctx.init_fused()
        aps = declare_consts(ctx, consts)
        block = ctx.stack.enter_context(ctx.nc.Block())
        n = len(SEQ)
        for i, (nm, emit_fn, wf) in enumerate(SEQ):
            src_ap = x_ap if i == 0 else (t0_ap if i % 2 == 1 else t1_ap)
            dst_ap = y_ap if i == n - 1 else (t0_ap if i % 2 == 0 else t1_ap)
            emit_fn(ctx, aps, src_ap, dst_ap, wf(inputs))
            if i < n - 1:
                ctx.new_sublayer()
        ctx.sch.finish(block, ctx.stack)
    return ctx


def kernel(**inputs):
    x = inputs["L0_attn_a_x"]
    ctx = build_fused(inputs)
    return run_program(ctx, x)
